# version 05
# speedup vs baseline: 1.0239x; 1.0239x over previous
import math
import os
from contextlib import ExitStack

import numpy as np
import ml_dtypes

import concourse.bass as bass
from concourse.bass import IndirectOffsetOnAxis
import concourse.mybir as mybir
from concourse.bass_utils import run_bass_kernel_spmd

F32 = mybir.dt.float32
BF16 = mybir.dt.bfloat16
I32 = mybir.dt.int32
U32 = mybir.dt.uint32
U8 = mybir.dt.uint8
AF = mybir.ActivationFunctionType
ALU = mybir.AluOpType
AX = mybir.AxisListType

NCORES = 8
D = 1024
NDC = D // 128

bfc = lambda a: np.asarray(a, dtype=ml_dtypes.bfloat16)
f32c = lambda a: np.ascontiguousarray(a, dtype=np.float32)


def np_to_mybir_dt(dt):
    dt = np.dtype(dt)
    return {np.dtype(np.float32): F32, np.dtype(np.int32): I32,
            np.dtype(np.uint32): U32, np.dtype(np.uint8): U8,
            np.dtype(ml_dtypes.bfloat16): BF16}[dt]


ENGS = ("tensor", "vector", "scalar", "sync", "gpsimd")


class Sem:

    def __init__(self, ctx, name):
        self.h = ctx.es.enter_context(ctx.nc.semaphore(name))
        self.v = 0

    def inc(self, instr, n=1):
        instr.then_inc(self.h, n)
        self.v += n
        return instr


class Ctx:

    def __init__(self, nc, T):
        self.nc = nc
        self.T = T
        self.es = ExitStack()
        self.sems = {}
        self.consts = {}
        self.inputs = {}
        self._const_tiles = []
        self._nsem = 0
        self.dbg = {}

    def sem(self, name=None):
        self._nsem += 1
        name = name or f"s{self._nsem}"
        if name in self.sems:
            return self.sems[name]
        s = Sem(self, name)
        self.sems[name] = s
        return s

    def sbuf(self, name, shape, dtype, scope=None):
        cm = self.nc.sbuf_tensor(name, shape, dtype)
        return (scope or self.es).enter_context(cm)

    def psum(self, name, shape, dtype=F32, scope=None):
        cm = self.nc.psum_tensor(name, shape, dtype)
        return (scope or self.es).enter_context(cm)

    def dram_in(self, name, arr):
        a0 = arr[0] if isinstance(arr, list) else arr
        t = self.nc.dram_tensor(name, list(a0.shape), np_to_mybir_dt(a0.dtype),
                                kind="ExternalInput").ap()
        self.inputs[name] = arr
        return t

    def dram_scratch(self, name, shape, dtype):
        return self.nc.dram_tensor(name, list(shape), dtype, kind="Internal").ap()

    def dram_out(self, name, shape, dtype):
        return self.nc.dram_tensor(name, list(shape), dtype, kind="ExternalOutput").ap()

    def const_sb(self, name, arr, dtype=None):
        if name in self.consts:
            return self.consts[name]
        dt = np_to_mybir_dt(arr.dtype) if dtype is None else dtype
        ap = self.dram_in("c_" + name, arr)
        sb = self.sbuf(name, list(arr.shape), dt)
        self._const_tiles.append((ap, sb))
        self.consts[name] = sb
        return sb

    def fconst(self, val):
        key = f"fc_{float(val)}".replace("-", "m").replace(".", "p").replace("+", "")
        return self.const_sb(key, f32c(np.full((128, 1), np.float32(val))))

    def auto_sems(self):
        if not hasattr(self, "_auto_sems"):
            self._auto_sems = {e: self.sem(f"auto_{e}") for e in ENGS}
        return self._auto_sems

    def ones_col(self):
        return self.const_sb("ones_col", f32c(np.ones((128, 1))))

    def ones_col_bf(self):
        return self.const_sb("ones_col_bf", bfc(np.ones((128, 1))))

    def ones_row(self):
        return self.const_sb("ones_row", f32c(np.ones((1, 128))))

    def ident_bf(self):
        return self.const_sb("ident_bf", bfc(np.eye(128)))

    def emit_const_preload(self):
        nc = self.nc
        if not hasattr(self, "_const_cursor"):
            self._const_cursor = 0
            self._preload_n = 0
        todo = self._const_tiles[self._const_cursor:]
        if not todo:
            return
        self._const_cursor = len(self._const_tiles)
        self._preload_n += 1
        ld = self.sem(f"constld{self._preload_n}")
        with nc.Block(f"const_preload{self._preload_n}") as blk:
            @blk.sync
            def _(sync):
                for ap, sb in todo:
                    sync.wait_ge(ld.h, ld.v)
                    ld.inc(sync.dma_start(out=sb[:], in_=ap[:]), 16)
                sync.wait_ge(ld.h, ld.v)



def emit_layernorm(ctx, x_dram, xn_dram, name, eps=1e-5):
    nc, T = ctx.nc, ctx.T
    NS = T // 512
    ones_col = ctx.ones_col()
    ones_row = ctx.ones_row()
    with ExitStack() as sc:
        x_sl = ctx.sbuf(f"{name}_x", [128, NDC * 512 * 2], F32, sc)
        xn_sl = ctx.sbuf(f"{name}_xn", [128, NDC * 512 * 2], BF16, sc)
        sq = ctx.sbuf(f"{name}_sq", [128, 512 * 2], F32, sc)
        srow = ctx.sbuf(f"{name}_srow", [1, 512 * 4], F32, sc)
        mub = ctx.sbuf(f"{name}_mub", [128, 512], F32, sc)
        rsb = ctx.sbuf(f"{name}_rsb", [128, 512], F32, sc)
        tmp = ctx.sbuf(f"{name}_tmp", [128, 512], F32, sc)
        p_s0 = ctx.psum(f"{name}_ps0", [1, 512], F32, sc)
        p_s1 = ctx.psum(f"{name}_ps1", [1, 512], F32, sc)
        p_b0 = ctx.psum(f"{name}_pb0", [128, 512], F32, sc)
        p_b1 = ctx.psum(f"{name}_pb1", [128, 512], F32, sc)

        ld = ctx.sem("lnS_ld")
        sqs = ctx.sem("lnS_sqs")
        mmq = ctx.sem("lnS_mmq")
        mms = ctx.sem("lnS_mms")
        vst = ctx.sem("lnS_vst")
        sct = ctx.sem("lnS_sct")
        bcs = ctx.sem("lnS_bcs")
        dn = ctx.sem("lnS_dn")
        sts = ctx.sem("lnS_sts")
        ldb, sqb, mqb, msb = ld.v, sqs.v, mmq.v, mms.v
        vsb, scb, bcb, dnb = vst.v, sct.v, bcs.v, dn.v

        with nc.Block(f"{name}") as blk:
            @blk.sync
            def _(sync):
                for s in range(NS):
                    if s >= 2:
                        sync.wait_ge(dn.h, dnb + s - 1)
                    elif s == 1:
                        sync.wait_ge(ld.h, ldb + 16 * NDC)
                    b = (s % 2) * NDC * 512
                    for dc in range(NDC):
                        ld.inc(sync.dma_start(
                            out=x_sl[:, b + dc * 512: b + dc * 512 + 512],
                            in_=x_dram[dc * 128:(dc + 1) * 128, s * 512:(s + 1) * 512]), 16)
                    if s >= 1:
                        sync.wait_ge(dn.h, dnb + s)
                        b2 = ((s - 1) % 2) * NDC * 512
                        for dc in range(NDC):
                            sts.inc(sync.dma_start(
                                out=xn_dram[dc * 128:(dc + 1) * 128,
                                            (s - 1) * 512: s * 512],
                                in_=xn_sl[:, b2 + dc * 512: b2 + dc * 512 + 512]), 16)
                sync.wait_ge(dn.h, dnb + NS)
                b2 = ((NS - 1) % 2) * NDC * 512
                for dc in range(NDC):
                    sts.inc(sync.dma_start(
                        out=xn_dram[dc * 128:(dc + 1) * 128, (NS - 1) * 512: NS * 512],
                        in_=xn_sl[:, b2 + dc * 512: b2 + dc * 512 + 512]), 16)
                sync.wait_ge(sts.h, sts.v)

            @blk.vector
            def _(vector):
                for s in range(NS):
                    b = (s % 2) * NDC * 512
                    vector.wait_ge(ld.h, ldb + 16 * NDC * (s + 1))
                    for dc in range(NDC):
                        g = s * NDC + dc
                        if g >= 2:
                            vector.wait_ge(mmq.h, mqb + g - 1)
                        xc = x_sl[:, b + dc * 512: b + dc * 512 + 512]
                        sqs.inc(vector.tensor_tensor(
                            out=sq[:, (g % 2) * 512:(g % 2) * 512 + 512],
                            in0=xc, in1=xc, op=ALU.mult))
                    vector.wait_ge(mms.h, msb + s + 1)
                    vector.wait_ge(mmq.h, mqb + NDC * (s + 1))
                    vector.tensor_scalar(out=srow[0:1, 0:512], in0=p_s0[0:1, :],
                                         scalar1=1.0 / D, scalar2=None, op0=ALU.mult)
                    vector.drain()
                    vector.tensor_tensor(out=srow[0:1, 512:1024], in0=srow[0:1, 0:512],
                                         in1=srow[0:1, 0:512], op=ALU.mult)
                    vector.drain()
                    vector.scalar_tensor_tensor(out=srow[0:1, 1024:1536], in0=p_s1[0:1, :],
                                                scalar=1.0 / D, in1=srow[0:1, 512:1024],
                                                op0=ALU.mult, op1=ALU.subtract)
                    vst.inc(vector.drain())
                    vector.wait_ge(bcs.h, bcb + s + 1)
                    vector.tensor_copy(out=mub[:], in_=p_b0[:])
                    vector.tensor_copy(out=rsb[:], in_=p_b1[:])
                    vector.drain()
                    for dc in range(NDC):
                        xc = x_sl[:, b + dc * 512: b + dc * 512 + 512]
                        vector.tensor_tensor(out=tmp[:], in0=xc, in1=mub[:], op=ALU.subtract)
                        vector.drain()
                        vector.tensor_tensor(
                            out=xn_sl[:, b + dc * 512: b + dc * 512 + 512],
                            in0=tmp[:], in1=rsb[:], op=ALU.mult)
                        vector.drain()
                    dn.inc(vector.drain())

            @blk.scalar
            def _(scalar):
                for s in range(NS):
                    scalar.wait_ge(vst.h, vsb + s + 1)
                    scalar.activation(out=srow[0:1, 1536:2048], in_=srow[0:1, 1024:1536],
                                      func=AF.Ln, bias=ctx.fconst(eps)[0:1, 0:1])
                    scalar.drain()
                    sct.inc(scalar.activation(out=srow[0:1, 1536:2048],
                                              in_=srow[0:1, 1536:2048],
                                              func=AF.Exp, scale=-0.5))

            @blk.tensor
            def _(tensor):
                for s in range(NS):
                    b = (s % 2) * NDC * 512
                    tensor.wait_ge(ld.h, ldb + 16 * NDC * (s + 1))
                    if s >= 1:
                        tensor.wait_ge(dn.h, dnb + s)
                    for dc in range(NDC):
                        g = s * NDC + dc
                        tensor.wait_ge(sqs.h, sqb + g + 1)
                        mmq.inc(tensor.matmul(p_s1[0:1, :], ones_col[:],
                                              sq[:, (g % 2) * 512:(g % 2) * 512 + 512],
                                              start=(dc == 0), stop=(dc == NDC - 1)))
                    for dc in range(NDC):
                        ins = tensor.matmul(p_s0[0:1, :], ones_col[:],
                                            x_sl[:, b + dc * 512: b + dc * 512 + 512],
                                            start=(dc == 0), stop=(dc == NDC - 1))
                        if dc == NDC - 1:
                            mms.inc(ins)
                    tensor.wait_ge(vst.h, vsb + s + 1)
                    tensor.wait_ge(sct.h, scb + s + 1)
                    tensor.matmul(p_b0[:], ones_row[:], srow[0:1, 0:512],
                                  start=True, stop=True)
                    bcs.inc(tensor.matmul(p_b1[:], ones_row[:], srow[0:1, 1536:2048],
                                          start=True, stop=True))



def emit_l0ffn(ctx, xn_dram, x_base_dram, x_out_dram, W1T_d, W2T_d, b1c_d, b2c_d,
               name="l0f"):
    nc, T = ctx.nc, ctx.T
    NS = T // 512
    F = 4096
    NFC = F // 128
    EPS = 1e-5
    ones_col = ctx.ones_col()
    ones_row = ctx.ones_row()
    with ExitStack() as sc:
        w1q = ctx.sbuf(f"{name}_w1q", [128, NDC * F], BF16, sc)
        w2q = ctx.sbuf(f"{name}_w2q", [128, NFC * D], BF16, sc)

        gld = [ctx.sem(f"{name}_gld0"), ctx.sem(f"{name}_gld1")]
        vq = ctx.sem(f"{name}_vq")
        mms = ctx.sem(f"{name}_mm")
        ves = ctx.sem(f"{name}_ve")
        scs = ctx.sem(f"{name}_sc")
        bls = ctx.sem(f"{name}_bls")
        xns = ctx.sem(f"{name}_xns")
        xbs = ctx.sem(f"{name}_xbs")
        sts = ctx.sem(f"{name}_sts")
        vy = ctx.sem(f"{name}_vy")

        chunks = ([("w1", dc, W1T_d[dc * 128:(dc + 1) * 128, :], F) for dc in range(NDC)]
                  + [("w2", fc, W2T_d[fc * 128:(fc + 1) * 128, :], D) for fc in range(NFC)])
        NCH = len(chunks)

        with ExitStack() as qsc:
            wld = ctx.sbuf(f"{name}_wld", [128, 4096 * 2], F32, qsc)
            qt = ctx.sbuf(f"{name}_qt", [128, 4096 * 2], F32, qsc)
            acc = ctx.sbuf(f"{name}_acc", [128, 4], F32, qsc)
            gsc = ctx.sbuf(f"{name}_gsc", [128, 2], F32, qsc)
            srow = ctx.sbuf(f"{name}_srow", [1, 8], F32, qsc)
            p_g = ctx.psum(f"{name}_pg", [1, 4], F32, qsc)
            p_bc = ctx.psum(f"{name}_pbc", [128, 2], F32, qsc)

            with nc.Block(f"{name}_quant") as blk:
                @blk.sync
                def _(sync):
                    for p in range(2):
                        for i, (which, ci, dsl, ncol) in enumerate(chunks):
                            j = p * NCH + i
                            if j >= 2:
                                sync.wait_ge(vq.h, j - 1)
                            gld[j % 2].inc(sync.dma_start(
                                out=wld[:, (j % 2) * 4096:(j % 2) * 4096 + ncol],
                                in_=dsl), 16)

                @blk.scalar
                def _(scalar):
                    for i, (which, ci, dsl, ncol) in enumerate(chunks):
                        scalar.wait_ge(gld[i % 2].h, 16 * (i // 2 + 1))
                        if i >= 1:
                            scalar.wait_ge(vq.h, i)
                        w = 0 if which == "w1" else 1
                        scalar.activation(
                            out=qt[:, 0:ncol],
                            in_=wld[:, (i % 2) * 4096:(i % 2) * 4096 + ncol],
                            func=AF.Abs, accum_out=acc[:, 2 + w:3 + w])
                        scs.inc(scalar.drain())

                @blk.vector
                def _(vector):
                    vector.memset(acc[:, 0:2], 0.0)
                    vector.drain()
                    for i, (which, ci, dsl, ncol) in enumerate(chunks):
                        vector.wait_ge(scs.h, i + 1)
                        w = 0 if which == "w1" else 1
                        vector.tensor_tensor(out=acc[:, w:w + 1], in0=acc[:, w:w + 1],
                                             in1=acc[:, 2 + w:3 + w], op=ALU.add)
                        vq.inc(vector.drain())
                    vector.wait_ge(mms.h, 1)
                    vector.tensor_scalar(out=srow[0:1, 0:1], in0=p_g[0:1, 0:1],
                                         scalar1=1.0 / (D * F), scalar2=EPS,
                                         op0=ALU.mult, op1=ALU.add)
                    vector.drain()
                    vector.tensor_scalar(out=srow[0:1, 1:2], in0=p_g[0:1, 1:2],
                                         scalar1=1.0 / (D * F), scalar2=EPS,
                                         op0=ALU.mult, op1=ALU.add)
                    ves.inc(vector.drain())
                    vector.wait_ge(mms.h, 2)
                    vector.reciprocal(out=gsc[:, 0:2], in_=p_bc[:, 0:2])
                    vector.drain()
                    for i, (which, ci, dsl, ncol) in enumerate(chunks):
                        j = NCH + i
                        vector.wait_ge(gld[j % 2].h, 16 * (j // 2 + 1))
                        w = 0 if which == "w1" else 1
                        src = wld[:, (j % 2) * 4096:(j % 2) * 4096 + ncol]
                        dst = (w1q[:, ci * F:ci * F + ncol] if which == "w1"
                               else w2q[:, ci * D:ci * D + ncol])
                        t = qt[:, 0:ncol]
                        m1 = qt[:, 4096:4096 + ncol]
                        vector.tensor_scalar(out=t, in0=src, scalar1=gsc[:, w:w + 1],
                                             scalar2=None, op0=ALU.mult)
                        vector.drain()
                        vector.tensor_scalar(out=m1, in0=t, scalar1=0.5, scalar2=None,
                                             op0=ALU.is_gt)
                        vector.drain()
                        vector.tensor_scalar(out=t, in0=t, scalar1=-0.5, scalar2=None,
                                             op0=ALU.is_lt)
                        vector.drain()
                        vector.tensor_tensor(out=dst, in0=m1, in1=t, op=ALU.subtract)
                        vq.inc(vector.drain())

                @blk.tensor
                def _(tensor):
                    tensor.wait_ge(vq.h, NCH)
                    mms.inc(tensor.matmul(p_g[0:1, 0:2], ones_col[:], acc[:, 0:2],
                                          start=True, stop=True))
                    tensor.wait_ge(ves.h, 1)
                    mms.inc(tensor.matmul(p_bc[:, 0:2], ones_row[:], srow[0:1, 0:2],
                                          start=True, stop=True))

        with ExitStack() as msc:
            b1_sb = ctx.sbuf(f"{name}_b1", [128, NFC], F32, msc)
            b2_sb = ctx.sbuf(f"{name}_b2", [128, NDC], F32, msc)
            h_sb = ctx.sbuf(f"{name}_h", [128, NFC * 512], BF16, msc)
            xn_sl = ctx.sbuf(f"{name}_xnsl", [128, NDC * 512 * 2], BF16, msc)
            xb_sl = ctx.sbuf(f"{name}_xb", [128, NDC * 512], F32, msc)
            p_h = [ctx.psum(f"{name}_ph{i}", [128, 512], F32, msc) for i in range(2)]
            p_y = [ctx.psum(f"{name}_py{i}", [128, 512], F32, msc) for i in range(2)]

            with nc.Block(f"{name}_main") as blk:
                @blk.sync
                def _(sync):
                    bls.inc(sync.dma_start(out=b1_sb[:], in_=b1c_d[:]), 16)
                    bls.inc(sync.dma_start(out=b2_sb[:], in_=b2c_d[:]), 16)
                    for s in range(NS):
                        if s >= 2:
                            sync.wait_ge(scs.h, NCH + (s - 1) * NFC)
                        bxn = (s % 2) * NDC * 512
                        for dc in range(NDC):
                            xns.inc(sync.dma_start(
                                out=xn_sl[:, bxn + dc * 512: bxn + dc * 512 + 512],
                                in_=xn_dram[dc * 128:(dc + 1) * 128,
                                            s * 512:(s + 1) * 512]), 16)
                        if s >= 1:
                            sync.wait_ge(sts.h, 16 * NDC * s)
                        for dc in range(NDC):
                            xbs.inc(sync.dma_start(
                                out=xb_sl[:, dc * 512:(dc + 1) * 512],
                                in_=x_base_dram[dc * 128:(dc + 1) * 128,
                                                s * 512:(s + 1) * 512]), 16)
                        sync.wait_ge(ves.h, 2 + s)
                        for dc in range(NDC):
                            sts.inc(sync.dma_start(
                                out=x_out_dram[dc * 128:(dc + 1) * 128,
                                               s * 512:(s + 1) * 512],
                                in_=xb_sl[:, dc * 512: dc * 512 + 512]), 16)
                    sync.wait_ge(sts.h, 16 * NDC * NS)

                @blk.tensor
                def _(tensor):
                    for s in range(NS):
                        bxn = (s % 2) * NDC * 512
                        tensor.wait_ge(xns.h, 16 * NDC * (s + 1))
                        for fc in range(NFC):
                            g = s * NFC + fc
                            pb = p_h[fc % 2]
                            if g >= 2:
                                tensor.wait_ge(scs.h, NCH + g - 1)
                            for dc in range(NDC):
                                ins = tensor.matmul(
                                    pb[:],
                                    w1q[:, dc * F + fc * 128: dc * F + fc * 128 + 128],
                                    xn_sl[:, bxn + dc * 512: bxn + dc * 512 + 512],
                                    start=(dc == 0), stop=(dc == NDC - 1))
                                if dc == NDC - 1:
                                    mms.inc(ins)
                        tensor.wait_ge(scs.h, NCH + (s + 1) * NFC)
                        for dc in range(NDC):
                            g2 = s * NDC + dc
                            pb = p_y[dc % 2]
                            if g2 >= 2:
                                tensor.wait_ge(vy.h, g2 - 1)
                            for fc in range(NFC):
                                ins = tensor.matmul(
                                    pb[:],
                                    w2q[:, fc * D + dc * 128: fc * D + dc * 128 + 128],
                                    h_sb[:, fc * 512:(fc + 1) * 512],
                                    start=(fc == 0), stop=(fc == NFC - 1))
                                if fc == NFC - 1:
                                    mms.inc(ins)

                @blk.scalar
                def _(scalar):
                    scalar.wait_ge(bls.h, 32)
                    base = 2
                    for s in range(NS):
                        for fc in range(NFC):
                            pb = p_h[fc % 2]
                            scalar.wait_ge(mms.h, base + s * (NFC + NDC) + fc + 1)
                            scs.inc(scalar.activation(
                                out=h_sb[:, fc * 512:(fc + 1) * 512], in_=pb[:],
                                func=AF.Relu, bias=b1_sb[:, fc:fc + 1], scale=1.0))

                @blk.vector
                def _(vector):
                    vector.wait_ge(bls.h, 32)
                    base = 2
                    for s in range(NS):
                        vector.wait_ge(xbs.h, 16 * NDC * (s + 1))

                        for dc in range(NDC):
                            pb = p_y[dc % 2]
                            vector.wait_ge(mms.h, base + s * (NFC + NDC) + NFC + dc + 1)
                            vector.scalar_tensor_tensor(
                                out=xb_sl[:, dc * 512: dc * 512 + 512],
                                in0=pb[:], scalar=b2_sb[:, dc:dc + 1],
                                in1=xb_sl[:, dc * 512:(dc + 1) * 512],
                                op0=ALU.add, op1=ALU.add)
                            vy.inc(vector.drain())
                        ves.inc(vector.drain())



def emit_l2ffn(ctx, xn_dram, x_base_dram, x_out_dram, W1_d, W2_d, b1c_d, thc_d,
               wrc_d, wic_d, b2c_d, name="l2f"):
    nc, T = ctx.nc, ctx.T
    NS = T // 512
    F = 4096
    NFC = F // 128
    PI = float(np.pi)
    with ExitStack() as sc:
        w1b = ctx.sbuf(f"{name}_w1b", [128, NDC * F], BF16, sc)
        w2b = ctx.sbuf(f"{name}_w2b", [128, NFC * D], BF16, sc)
        cst = ctx.sbuf(f"{name}_cst", [128, NFC * 5], F32, sc)
        b2_sb = ctx.sbuf(f"{name}_b2", [128, NDC], F32, sc)
        h_sb = ctx.sbuf(f"{name}_h", [128, NFC * 512], BF16, sc)
        xn_sl = ctx.sbuf(f"{name}_xnsl", [128, NDC * 512 * 2], BF16, sc)
        xb_sl = ctx.sbuf(f"{name}_xb", [128, NDC * 512], F32, sc)
        sv = ctx.sbuf(f"{name}_sv", [128, 512 * 3], F32, sc)

        gw = [ctx.sem(f"{name}_gw0"), ctx.sem(f"{name}_gw1")]
        bls = ctx.sem(f"{name}_bls")
        xns = ctx.sem(f"{name}_xns")
        xbs = ctx.sem(f"{name}_xbs")
        mms = ctx.sem(f"{name}_mm")
        scs = ctx.sem(f"{name}_sc")
        sts = ctx.sem(f"{name}_sts")
        vy = ctx.sem(f"{name}_vy")
        vqd = ctx.sem(f"{name}_vqd")
        cps = ctx.sem(f"{name}_cps")

        wchunks = ([("w1", dc, W1_d[dc * 128:(dc + 1) * 128, :], F) for dc in range(NDC)]
                   + [("w2", fc, W2_d[fc * 128:(fc + 1) * 128, :], D) for fc in range(NFC)])

        with ExitStack() as asc:
            craw = ctx.sbuf(f"{name}_craw", [128, NFC * 4], F32, asc)

            with nc.Block(f"{name}_prep") as blk:
                @blk.gpsimd
                def _(gpsimd):
                    for j, (which, ci, dsl, ncol) in enumerate(wchunks):
                        if j >= 2:
                            gpsimd.wait_ge(gw[j % 2].h, 16 * (j // 2))
                        dst = (w1b[:, ci * F:ci * F + ncol] if which == "w1"
                               else w2b[:, ci * D:ci * D + ncol])
                        gw[j % 2].inc(gpsimd.dma_start(out=dst, in_=dsl), 16)

                @blk.sync
                def _(sync):
                    bls.inc(sync.dma_start(out=craw[:, 0:NFC], in_=b1c_d[:]), 16)
                    bls.inc(sync.dma_start(out=craw[:, NFC:2 * NFC], in_=thc_d[:]), 16)
                    bls.inc(sync.dma_start(out=craw[:, 2 * NFC:3 * NFC], in_=wrc_d[:]), 16)
                    bls.inc(sync.dma_start(out=craw[:, 3 * NFC:4 * NFC], in_=wic_d[:]), 16)
                    bls.inc(sync.dma_start(out=b2_sb[:], in_=b2c_d[:]), 16)

                @blk.vector
                def _(vector):
                    vector.wait_ge(bls.h, 80)
                    thw = cst[:, NFC:2 * NFC]
                    vector.tensor_copy(out=thw, in_=craw[:, NFC:2 * NFC])
                    vector.drain()
                    for _i in range(6):
                        vector.tensor_scalar(out=sv[:, 0:NFC], in0=thw, scalar1=PI,
                                             scalar2=None, op0=ALU.is_gt)
                        vector.drain()
                        vector.tensor_scalar(out=sv[:, 512:512 + NFC], in0=thw,
                                             scalar1=-PI, scalar2=None, op0=ALU.is_lt)
                        vector.drain()
                        vector.tensor_tensor(out=sv[:, 0:NFC], in0=sv[:, 0:NFC],
                                             in1=sv[:, 512:512 + NFC], op=ALU.subtract)
                        vector.drain()
                        vector.scalar_tensor_tensor(out=thw, in0=sv[:, 0:NFC],
                                                    scalar=-2 * PI, in1=thw,
                                                    op0=ALU.mult, op1=ALU.add)
                        vector.drain()
                    cos_in = sv[:, 1024:1024 + NFC]
                    vector.tensor_scalar(out=cos_in, in0=thw, scalar1=PI / 2,
                                         scalar2=None, op0=ALU.add)
                    vector.drain()
                    vector.tensor_scalar(out=sv[:, 0:NFC], in0=cos_in, scalar1=PI,
                                         scalar2=None, op0=ALU.is_gt)
                    vector.drain()
                    vector.scalar_tensor_tensor(out=cos_in, in0=sv[:, 0:NFC],
                                                scalar=-2 * PI, in1=cos_in,
                                                op0=ALU.mult, op1=ALU.add)
                    cps.inc(vector.drain())
                    vector.wait_ge(scs.h, 2)
                    c1 = sv[:, 1024:1024 + NFC]
                    vector.tensor_tensor(out=c1, in0=craw[:, 2 * NFC:3 * NFC],
                                         in1=sv[:, 512:512 + NFC], op=ALU.mult)
                    vector.drain()
                    vector.tensor_tensor(out=craw[:, 2 * NFC:3 * NFC],
                                         in0=sv[:, 0:NFC],
                                         in1=craw[:, 3 * NFC:4 * NFC], op=ALU.mult)
                    vector.drain()
                    vector.tensor_tensor(out=c1, in0=c1,
                                         in1=craw[:, 2 * NFC:3 * NFC], op=ALU.add)
                    vector.drain()
                    vector.scalar_tensor_tensor(out=c1, in0=c1, scalar=0.1,
                                                in1=craw[:, 3 * NFC:4 * NFC],
                                                op0=ALU.mult, op1=ALU.mult)
                    vector.drain()
                    vector.tensor_tensor(out=cst[:, 2 * NFC:3 * NFC], in0=c1,
                                         in1=sv[:, 512:512 + NFC], op=ALU.mult)
                    vector.drain()
                    vector.tensor_tensor(out=cst[:, NFC:2 * NFC], in0=c1,
                                         in1=sv[:, 0:NFC], op=ALU.mult)
                    vector.drain()
                    vector.tensor_copy(out=cst[:, 0:NFC], in_=craw[:, 0:NFC])
                    vector.drain()
                    vector.tensor_scalar(out=cst[:, 3 * NFC:4 * NFC], in0=craw[:, 0:NFC],
                                         scalar1=0.1, scalar2=None, op0=ALU.mult)
                    vector.drain()
                    vector.tensor_scalar(out=cst[:, 4 * NFC:5 * NFC], in0=craw[:, 0:NFC],
                                         scalar1=0.1, scalar2=PI / 2, op0=ALU.mult,
                                         op1=ALU.add)
                    cps.inc(vector.drain())

                @blk.scalar
                def _(scalar):
                    scalar.wait_ge(cps.h, 1)
                    scs.inc(scalar.activation(out=sv[:, 0:NFC], in_=cst[:, NFC:2 * NFC],
                                              func=AF.Sin))
                    scs.inc(scalar.activation(out=sv[:, 512:512 + NFC],
                                              in_=sv[:, 1024:1024 + NFC], func=AF.Sin))

        with ExitStack() as msc:
            p_h = [ctx.psum(f"{name}_ph{i}", [128, 512], F32, msc) for i in range(2)]
            p_y = [ctx.psum(f"{name}_py{i}", [128, 512], F32, msc) for i in range(2)]
            NW = len(wchunks)
            sh = ctx.sem(f"{name}_sh")

            with nc.Block(f"{name}_main") as blk:
                @blk.sync
                def _(sync):
                    for s in range(NS):
                        if s >= 2:
                            sync.wait_ge(sh.h, (s - 1) * NFC)
                        bxn = (s % 2) * NDC * 512
                        for dc in range(NDC):
                            xns.inc(sync.dma_start(
                                out=xn_sl[:, bxn + dc * 512: bxn + dc * 512 + 512],
                                in_=xn_dram[dc * 128:(dc + 1) * 128,
                                            s * 512:(s + 1) * 512]), 16)
                        if s >= 1:
                            sync.wait_ge(sts.h, 16 * NDC * s)
                        for dc in range(NDC):
                            xbs.inc(sync.dma_start(
                                out=xb_sl[:, dc * 512:(dc + 1) * 512],
                                in_=x_base_dram[dc * 128:(dc + 1) * 128,
                                                s * 512:(s + 1) * 512]), 16)
                        sync.wait_ge(vy.h, NDC * (s + 1))
                        for dc in range(NDC):
                            sts.inc(sync.dma_start(
                                out=x_out_dram[dc * 128:(dc + 1) * 128,
                                               s * 512:(s + 1) * 512],
                                in_=xb_sl[:, dc * 512: dc * 512 + 512]), 16)
                    sync.wait_ge(sts.h, 16 * NDC * NS)

                @blk.tensor
                def _(tensor):
                    for j in range(NW):
                        tensor.wait_ge(gw[j % 2].h, 16 * (j // 2 + 1))
                    for s in range(NS):
                        bxn = (s % 2) * NDC * 512
                        tensor.wait_ge(xns.h, 16 * NDC * (s + 1))
                        for fc in range(NFC):
                            g = s * NFC + fc
                            pb = p_h[fc % 2]
                            if g >= 2:
                                tensor.wait_ge(scs.h, 2 + 3 * (g - 1))
                            for dc in range(NDC):
                                ins = tensor.matmul(
                                    pb[:],
                                    w1b[:, dc * F + fc * 128: dc * F + fc * 128 + 128],
                                    xn_sl[:, bxn + dc * 512: bxn + dc * 512 + 512],
                                    start=(dc == 0), stop=(dc == NDC - 1))
                                if dc == NDC - 1:
                                    mms.inc(ins)
                        tensor.wait_ge(vqd.h, (s + 1) * NFC)
                        for dc in range(NDC):
                            g2 = s * NDC + dc
                            pb = p_y[dc % 2]
                            if g2 >= 2:
                                tensor.wait_ge(vy.h, g2 - 1)
                            for fc in range(NFC):
                                ins = tensor.matmul(
                                    pb[:],
                                    w2b[:, fc * D + dc * 128: fc * D + dc * 128 + 128],
                                    h_sb[:, fc * 512:(fc + 1) * 512],
                                    start=(fc == 0), stop=(fc == NFC - 1))
                                if fc == NFC - 1:
                                    mms.inc(ins)

                @blk.scalar
                def _(scalar):
                    for s in range(NS):
                        for fc in range(NFC):
                            g = s * NFC + fc
                            pb = p_h[fc % 2]
                            scalar.wait_ge(mms.h, s * (NFC + NDC) + fc + 1)
                            if s >= 1:
                                scalar.wait_ge(mms.h, s * (NFC + NDC))
                            if g >= 1:
                                scalar.wait_ge(vqd.h, g)
                            sh.inc(scalar.activation(
                                out=h_sb[:, fc * 512:(fc + 1) * 512],
                                in_=pb[:], func=AF.Identity,
                                bias=cst[:, fc:fc + 1], scale=1.0))
                            scalar.activation(out=sv[:, 0:512], in_=pb[:], func=AF.Sin,
                                              bias=cst[:, 3 * NFC + fc:3 * NFC + fc + 1],
                                              scale=0.1)
                            scs.inc(scalar.activation(out=sv[:, 512:1024], in_=pb[:],
                                                      func=AF.Sin,
                                                      bias=cst[:, 4 * NFC + fc:4 * NFC + fc + 1],
                                                      scale=0.1), 3)

                @blk.vector
                def _(vector):
                    for s in range(NS):
                        for fc in range(NFC):
                            g = s * NFC + fc
                            vector.wait_ge(scs.h, 2 + 3 * (g + 1))
                            vector.wait_ge(sh.h, g + 1)
                            vector.scalar_tensor_tensor(
                                out=sv[:, 1024:1536], in0=sv[:, 512:1024],
                                scalar=cst[:, NFC + fc:NFC + fc + 1],
                                in1=h_sb[:, fc * 512:(fc + 1) * 512],
                                op0=ALU.mult, op1=ALU.add)
                            vector.drain()
                            vector.scalar_tensor_tensor(
                                out=sv[:, 1024:1536], in0=sv[:, 0:512],
                                scalar=cst[:, 2 * NFC + fc:2 * NFC + fc + 1],
                                in1=sv[:, 1024:1536],
                                op0=ALU.mult, op1=ALU.add)
                            vector.drain()
                            vector.tensor_scalar(out=h_sb[:, fc * 512:(fc + 1) * 512],
                                                 in0=sv[:, 1024:1536], scalar1=0.0,
                                                 scalar2=None, op0=ALU.max)
                            vqd.inc(vector.drain())
                        vector.wait_ge(xbs.h, 16 * NDC * (s + 1))
                        for dc in range(NDC):
                            pb = p_y[dc % 2]
                            vector.wait_ge(mms.h, s * (NFC + NDC) + NFC + dc + 1)
                            vector.scalar_tensor_tensor(
                                out=xb_sl[:, dc * 512: dc * 512 + 512],
                                in0=pb[:], scalar=b2_sb[:, dc:dc + 1],
                                in1=xb_sl[:, dc * 512:(dc + 1) * 512],
                                op0=ALU.add, op1=ALU.add)
                            vy.inc(vector.drain())



def emit_l3ffn(ctx, xn_dram, x_base_dram, x_out_dram, rwc_ds, rb_d, ewTc_d, ebc_d,
               name="l3f"):
    nc, T = ctx.nc, ctx.T
    NS = T // 512
    NTT = T // 128
    E = 8
    negbig = ctx.const_sb(f"negbig8", f32c(np.full((128, 8), -3e38)))
    ident = ctx.ident_bf()
    with ExitStack() as sc:
        maskf = ctx.sbuf(f"{name}_maskf", [128, E * NTT], F32, sc)
        maskT = ctx.sbuf(f"{name}_maskT", [8, T], BF16, sc)
        x2_dram = ctx.dram_scratch(f"{name}_x2", [D, T], BF16)
        ctx.dbg["l3f_x2"] = x2_dram
        ctx.dbg["l3f_maskf"] = maskf
        ctx.dbg["l3f_maskT"] = maskT

        for l in range(2):
            xin_dram = xn_dram if l == 0 else x2_dram
            _l3_route(ctx, sc, l, xin_dram, rwc_ds[l], rb_d[l:l + 1, :],
                      maskf, maskT, negbig, ident, name)
            _l3_experts(ctx, l, xin_dram, ewTc_d, ebc_d, maskf, maskT,
                        x2_dram, x_base_dram, x_out_dram, ident, name)


def _l3_route(ctx, sc, l, xin_dram, rwc_d, rbl_d, maskf, maskT, negbig, ident,
              name):
    nc, T = ctx.nc, ctx.T
    NS = T // 512
    NTT = T // 128
    E = 8
    with ExitStack() as rsc:
        rwb = ctx.sbuf(f"{name}_rwb{l}", [128, NDC * E], BF16, rsc)
        rbb = ctx.sbuf(f"{name}_rbb{l}", [1, E], F32, rsc)
        xn_sl = ctx.sbuf(f"{name}_rxn{l}", [128, NDC * 512 * 2], BF16, rsc)
        lg = ctx.sbuf(f"{name}_lg{l}", [128, E * 4], F32, rsc)
        rb_bc = ctx.sbuf(f"{name}_rbbc{l}", [128, E], F32, rsc)
        mk8 = ctx.sbuf(f"{name}_mk8{l}", [128, E], mybir.dt.uint8, rsc)
        mbf = ctx.sbuf(f"{name}_mbf{l}", [128, E], BF16, rsc)
        p_lg = ctx.psum(f"{name}_plg{l}", [128, E], F32, rsc)
        p_rb = ctx.psum(f"{name}_prb{l}", [128, E], F32, rsc)
        p_mt = ctx.psum(f"{name}_pmt{l}", [8, 128], BF16, rsc)

        gwl = ctx.sem(f"{name}_rgw{l}")
        rbl = ctx.sem(f"{name}_rbl{l}")
        xnl = ctx.sem(f"{name}_rxns{l}")
        mm = ctx.sem(f"{name}_rmm{l}")
        ve = ctx.sem(f"{name}_rve{l}")

        with nc.Block(f"{name}_route{l}") as blk:
            @blk.gpsimd
            def _(gpsimd):
                gwl.inc(gpsimd.dma_start(out=rwb[:], in_=rwc_d[:]), 16)

            @blk.sync
            def _(sync):
                rbl.inc(sync.dma_start(out=rbb[0:1, :], in_=rbl_d), 16)
                for s in range(NS):
                    if s >= 2:
                        sync.wait_ge(ve.h, 5 * 4 * (s - 1))
                    bxn = (s % 2) * NDC * 512
                    for dc in range(NDC):
                        xnl.inc(sync.dma_start(
                            out=xn_sl[:, bxn + dc * 512: bxn + dc * 512 + 512],
                            in_=xin_dram[dc * 128:(dc + 1) * 128,
                                         s * 512:(s + 1) * 512]), 16)

            @blk.tensor
            def _(tensor):
                tensor.wait_ge(gwl.h, 16)
                tensor.wait_ge(rbl.h, 16)
                mm.inc(tensor.matmul(p_rb[:], ctx.ones_row()[:], rbb[0:1, :],
                                     start=True, stop=True))
                for tt in range(NTT):
                    s = tt // 4
                    bxn = (s % 2) * NDC * 512
                    to = (tt % 4) * 128
                    tensor.wait_ge(xnl.h, 16 * NDC * (s + 1))
                    if tt >= 1:
                        tensor.wait_ge(ve.h, 5 * (tt - 1) + 1)
                    for dc in range(NDC):
                        ins = tensor.matmul(
                            p_lg[:],
                            xn_sl[:, bxn + dc * 512 + to: bxn + dc * 512 + to + 128],
                            rwb[:, dc * E:(dc + 1) * E],
                            start=(dc == 0), stop=(dc == NDC - 1))
                        if dc == NDC - 1:
                            mm.inc(ins)
                    tensor.wait_ge(ve.h, 5 * tt + 4)
                    if tt >= 1:
                        tensor.wait_ge(ve.h, 5 * tt)
                    mm.inc(tensor.transpose(p_mt[:], mbf[:], ident[:]))

            @blk.vector
            def _(vector):
                vector.wait_ge(mm.h, 1)
                vector.tensor_copy(out=rb_bc[:], in_=p_rb[:])
                vector.drain()
                for tt in range(NTT):
                    vector.wait_ge(mm.h, 2 + 2 * tt)
                    vector.tensor_tensor(out=lg[:, 0:E], in0=p_lg[:], in1=rb_bc[:],
                                         op=ALU.add)
                    ve.inc(vector.drain())
                    vector.tensor_reduce(out=lg[:, E:E + 1], in_=lg[:, 0:E],
                                         axis=AX.X, op=ALU.max)
                    vector.drain()
                    vector.tensor_scalar(out=mk8[:], in0=lg[:, 0:E],
                                         scalar1=lg[:, E:E + 1], scalar2=None,
                                         op0=ALU.is_ge)
                    vector.drain()
                    vector.select(out=lg[:, 2 * E:3 * E], mask=mk8[:],
                                  on_true=negbig[:], on_false=lg[:, 0:E],
                                  add_drain=True)
                    ve.inc(vector.drain())
                    vector.tensor_reduce(out=lg[:, E + 1:E + 2],
                                         in_=lg[:, 2 * E:3 * E], axis=AX.X, op=ALU.max)
                    vector.drain()
                    vector.tensor_scalar(out=maskf[:, tt * E:(tt + 1) * E],
                                         in0=lg[:, 0:E], scalar1=lg[:, E + 1:E + 2],
                                         scalar2=None, op0=ALU.is_ge)
                    ve.inc(vector.drain())
                    vector.tensor_copy(out=mbf[:], in_=maskf[:, tt * E:(tt + 1) * E])
                    ve.inc(vector.drain())
                    vector.wait_ge(mm.h, 3 + 2 * tt)
                    vector.tensor_copy(out=maskT[:, tt * 128:(tt + 1) * 128],
                                       in_=p_mt[:])
                    ve.inc(vector.drain())


def _l3_experts(ctx, l, xin_dram, ewTc_d, ebc_d, maskf, maskT, x2_dram,
                x_base_dram, x_out_dram, ident, name):
    nc, T = ctx.nc, ctx.T
    NS = T // 512
    NTT = T // 128
    E = 8
    final = (l == 1)
    with ExitStack() as esc:
        ewb = ctx.sbuf(f"{name}_ewb{l}", [128, E * NDC * D], BF16, esc)
        ebb = ctx.sbuf(f"{name}_ebb{l}", [8, D], BF16, esc)
        xn_sl = ctx.sbuf(f"{name}_exn{l}", [128, NDC * 512 * 2], BF16, esc)
        acc = ctx.sbuf(f"{name}_acc{l}", [128, D * 2], F32, esc)
        accb = ctx.sbuf(f"{name}_accb{l}", [128, D * 2], BF16, esc)
        out_sl = (None if final else
                  ctx.sbuf(f"{name}_osl{l}", [128, NDC * 512], BF16, esc))
        xb_sl = (ctx.sbuf(f"{name}_xb{l}", [128, NDC * 512], F32, esc)
                 if final else None)
        p_b = ctx.psum(f"{name}_pb{l}", [128, D], F32, esc)
        p_u = [ctx.psum(f"{name}_pu{l}{i}", [128, D], F32, esc) for i in range(2)]
        p_t = ctx.psum(f"{name}_pt{l}", [128, 128], BF16, esc)

        gw = [ctx.sem(f"{name}_egw{l}0"), ctx.sem(f"{name}_egw{l}1")]
        xnl = ctx.sem(f"{name}_exns{l}")
        xbl = ctx.sem(f"{name}_exbl{l}")
        mm = ctx.sem(f"{name}_emm{l}")
        mt = ctx.sem(f"{name}_emt{l}")
        va = ctx.sem(f"{name}_eva{l}")
        vb = ctx.sem(f"{name}_evb{l}")
        vo = ctx.sem(f"{name}_evo{l}")
        sts = ctx.sem(f"{name}_ests{l}")

        with nc.Block(f"{name}_exp{l}") as blk:
            @blk.gpsimd
            def _(gpsimd):
                j = 0
                for e in range(E):
                    for dc in range(NDC):
                        if j >= 2:
                            gpsimd.wait_ge(gw[j % 2].h, 16 * (j // 2))
                        r0 = (l * 8 + e) * D + dc * 128
                        gw[j % 2].inc(gpsimd.dma_start(
                            out=ewb[:, (e * NDC + dc) * D:(e * NDC + dc + 1) * D],
                            in_=ewTc_d[r0:r0 + 128, :]), 16)
                        j += 1
                gpsimd.wait_ge(gw[0].h, 16 * 32)
                gw[0].inc(gpsimd.dma_start(out=ebb[:], in_=ebc_d[l * 8:(l + 1) * 8, :]), 16)

            @blk.sync
            def _(sync):
                for s in range(NS):
                    if s >= 2:
                        sync.wait_ge(va.h, E * 4 * (s - 1))
                    bxn = (s % 2) * NDC * 512
                    for dc in range(NDC):
                        xnl.inc(sync.dma_start(
                            out=xn_sl[:, bxn + dc * 512: bxn + dc * 512 + 512],
                            in_=xin_dram[dc * 128:(dc + 1) * 128,
                                         s * 512:(s + 1) * 512]), 16)
                    if final:
                        if s >= 1:
                            sync.wait_ge(sts.h, 16 * NDC * s)
                        for dc in range(NDC):
                            xbl.inc(sync.dma_start(
                                out=xb_sl[:, dc * 512:(dc + 1) * 512],
                                in_=x_base_dram[dc * 128:(dc + 1) * 128,
                                                s * 512:(s + 1) * 512]), 16)
                    sync.wait_ge(vo.h, NDC * 4 * (s + 1))
                    dst = x_out_dram if final else x2_dram
                    src_sl = xb_sl if final else out_sl
                    for dc in range(NDC):
                        sts.inc(sync.dma_start(
                            out=dst[dc * 128:(dc + 1) * 128, s * 512:(s + 1) * 512],
                            in_=src_sl[:, dc * 512:(dc + 1) * 512]), 16)
                sync.wait_ge(sts.h, 16 * NDC * NS)

            @blk.tensor
            def _(tensor):
                tensor.wait_ge(gw[0].h, 16 * 33)
                tensor.wait_ge(gw[1].h, 16 * 32)
                for tt in range(NTT):
                    s = tt // 4
                    bxn = (s % 2) * NDC * 512
                    to = (tt % 4) * 128
                    tensor.wait_ge(xnl.h, 16 * NDC * (s + 1))
                    if tt >= 1:
                        tensor.wait_ge(va.h, E * (tt - 1) + 1)
                    tensor.matmul(p_b[:, 0:512], maskT[:, tt * 128:(tt + 1) * 128],
                                  ebb[:, 0:512], start=True, stop=True)
                    mm.inc(tensor.matmul(p_b[:, 512:1024],
                                         maskT[:, tt * 128:(tt + 1) * 128],
                                         ebb[:, 512:1024], start=True, stop=True))
                    for e in range(E):
                        ge = tt * E + e
                        pb = p_u[e % 2]
                        if ge >= 2:
                            tensor.wait_ge(va.h, ge - 1)
                        for half in range(2):
                            for dc in range(NDC):
                                ins = tensor.matmul(
                                    pb[:, half * 512:(half + 1) * 512],
                                    xn_sl[:, bxn + dc * 512 + to: bxn + dc * 512 + to + 128],
                                    ewb[:, (e * NDC + dc) * D + half * 512:
                                        (e * NDC + dc) * D + half * 512 + 512],
                                    start=(dc == 0), stop=(dc == NDC - 1))
                                if dc == NDC - 1 and half == 1:
                                    mm.inc(ins)
                    tensor.wait_ge(vb.h, tt + 1)
                    for k in range(NDC):
                        if tt >= 1 or k >= 1:
                            tensor.wait_ge(vo.h, NDC * tt + k)
                        mt.inc(tensor.transpose(
                            p_t[:],
                            accb[:, (tt % 2) * D + k * 128:(tt % 2) * D + k * 128 + 128],
                            ident[:]))

            @blk.vector
            def _(vector):
                for tt in range(NTT):
                    s = tt // 4
                    to = (tt % 4) * 128
                    ab = (tt % 2) * D
                    if (not final) and tt % 4 == 0 and tt >= 4:
                        vector.wait_ge(sts.h, 16 * NDC * (tt // 4))
                    vector.wait_ge(mm.h, (E + 1) * tt + 1)
                    vector.tensor_copy(out=acc[:, ab:ab + D], in_=p_b[:])
                    vector.drain()
                    for e in range(E):
                        ge = tt * E + e
                        pb = p_u[e % 2]
                        vector.wait_ge(mm.h, (E + 1) * tt + 1 + e + 1)
                        vector.scalar_tensor_tensor(
                            out=acc[:, ab:ab + D], in0=pb[:],
                            scalar=maskf[:, tt * E + e: tt * E + e + 1],
                            in1=acc[:, ab:ab + D], op0=ALU.mult, op1=ALU.add)
                        va.inc(vector.drain())
                    vector.tensor_copy(out=accb[:, ab:ab + D], in_=acc[:, ab:ab + D])
                    vb.inc(vector.drain())
                    if final:
                        vector.wait_ge(xbl.h, 16 * NDC * (s + 1))
                    for k in range(NDC):
                        vector.wait_ge(mt.h, NDC * tt + k + 1)
                        if final:
                            vector.tensor_tensor(
                                out=xb_sl[:, k * 512 + to: k * 512 + to + 128],
                                in0=xb_sl[:, k * 512 + to: k * 512 + to + 128],
                                in1=p_t[:], op=ALU.add)
                        else:
                            vector.tensor_copy(
                                out=out_sl[:, k * 512 + to: k * 512 + to + 128],
                                in_=p_t[:])
                        vo.inc(vector.drain())



def emit_l2attn(ctx, xn_dram, x_base_dram, x_mid_dram, w_inT_d, binc_d,
                w_outT_d, boutc_d, name="l2a"):
    nc, T = ctx.nc, ctx.T
    NT = T // 128
    trid = ctx.const_sb("trid", bfc(np.tril(np.ones((128, 128))).T))
    tris = ctx.const_sb("tris", bfc(np.tril(np.ones((128, 128)))))
    ident = ctx.ident_bf()
    attn_dram = ctx.dram_scratch(f"{name}_attn", [D, T], BF16)

    with ExitStack() as sc:
        xnr = ctx.sbuf(f"{name}_xnr", [128, NDC * T], BF16, sc)
        bcol = ctx.sbuf(f"{name}_bc", [128, 24], F32, sc)
        q2 = ctx.sbuf(f"{name}_q2", [128, T], BF16, sc)
        k2 = ctx.sbuf(f"{name}_k2", [128, T], BF16, sc)
        v2f = ctx.sbuf(f"{name}_v2f", [128, T], BF16, sc)
        vt = ctx.sbuf(f"{name}_vt", [128, NT * 130], BF16, sc)
        wpair = ctx.sbuf(f"{name}_wp", [128, 3 * NDC * 128], BF16, sc)
        pt = ctx.sbuf(f"{name}_pt", [128, 128 * 4], BF16, sc)
        otr = ctx.sbuf(f"{name}_otr", [1, 256], F32, sc)
        ot = ctx.sbuf(f"{name}_ot", [128, 128 * 2], F32, sc)
        at = ctx.sbuf(f"{name}_at", [128, 128 * 2], BF16, sc)

        xns = ctx.sem(f"{name}_xns")
        bl = ctx.sem(f"{name}_bl")

        with nc.Block(f"{name}_load") as blk:
            @blk.sync
            def _(sync):
                bl.inc(sync.dma_start(out=bcol[:], in_=binc_d[:]), 16)
                for dc in range(NDC):
                    xns.inc(sync.dma_start(out=xnr[:, dc * T:(dc + 1) * T],
                                           in_=xn_dram[dc * 128:(dc + 1) * 128, :]), 16)
                sync.wait_ge(xns.h, xns.v)
                sync.wait_ge(bl.h, 16)

        sems = {k: ctx.sem(f"{name}_{k}") for k in
                ("gwp", "mmp", "sep", "mmt", "vev", "mms", "sse", "vem",
                 "mmo", "vrc", "mmb", "veo", "sta", "stb")}
        for hp in range(int(os.environ.get("L2A_PAIRS", "8"))):
            _l2a_pair(ctx, hp, sems, xnr, bcol, q2, k2, v2f, vt, wpair, pt,
                      otr, ot, at, w_inT_d, attn_dram, trid, tris, ident, name)

    if os.environ.get("L2A_WOUT", "1") == "1":
        _matmul_add_phase(ctx, attn_dram, x_base_dram, x_mid_dram, w_outT_d,
                          boutc_d, f"{name}_wo")


def _l2a_pair(ctx, hp, sems, xnr, bcol, q2, k2, v2f, vt, wpair, pt, otr, ot,
              at, w_inT_d, attn_dram, trid, tris, ident, name):
    nc, T = ctx.nc, ctx.T
    NT = T // 128
    NSL = T // 512
    n = f"{name}{hp}"
    gwp, mmp, sep, mmt, vev, mms, sse, vem, mmo, vrc, mmb, veo, sta, stb = (
        sems[k] for k in ("gwp", "mmp", "sep", "mmt", "vev", "mms", "sse",
                          "vem", "mmo", "vrc", "mmb", "veo", "sta", "stb"))
    sth = [sta, stb]
    Bgw, Bmmp, Bsep, Bmmt, Bvev = gwp.v, mmp.v, sep.v, mmt.v, vev.v
    Bmms, Bsse, Bvem, Bmmo, Bvrc, Bmmb, Bveo, Bsta, Bstb = (
        mms.v, sse.v, vem.v, mmo.v, vrc.v, mmb.v, veo.v, sta.v, stb.v)
    Bsth = [Bsta, Bstb]

    with ExitStack() as psc:
        p_pr = [ctx.psum(f"{n}_ppr{i}", [128, 512], F32, psc) for i in range(2)]
        p_tp = ctx.psum(f"{n}_ptp", [128, 128], BF16, psc)

        with nc.Block(f"{n}_proj") as blk:
            @blk.gpsimd
            def _(gpsimd):
                for P in range(3):
                    if P >= 1:
                        gpsimd.wait_ge(gwp.h, Bgw + 16 * P)
                    src = w_inT_d[:, P * 1024 + hp * 128: P * 1024 + hp * 128 + 128]
                    gwp.inc(gpsimd.dma_start(
                        out=wpair[:, P * NDC * 128:(P + 1) * NDC * 128].rearrange(
                            "p (c f) -> p c f", f=128),
                        in_=src.rearrange("(c p) f -> p c f", p=128)), 16)

            @blk.tensor
            def _(tensor):
                tensor.wait_ge(gwp.h, Bgw + 48)
                for P in range(3):
                    for ts in range(NSL):
                        g = P * NSL + ts
                        pb = p_pr[g % 2]
                        if g >= 2:
                            tensor.wait_ge(sep.h, Bsep + g - 1)
                        for dc in range(NDC):
                            ins = tensor.matmul(
                                pb[:],
                                wpair[:, (P * NDC + dc) * 128:(P * NDC + dc + 1) * 128],
                                xnr[:, dc * T + ts * 512: dc * T + ts * 512 + 512],
                                start=(dc == 0), stop=(dc == NDC - 1))
                            if dc == NDC - 1:
                                mmp.inc(ins)
                for k in range(NT):
                    tensor.wait_ge(sep.h, Bsep + 2 * NSL + k // 4 + 1)
                    if k >= 1:
                        tensor.wait_ge(vev.h, Bvev + 2 * k)
                    mmt.inc(tensor.transpose(
                        p_tp[:], v2f[:, k * 128:(k + 1) * 128], ident[:]))

            @blk.scalar
            def _(scalar):
                for P in range(3):
                    dst = (q2, k2, v2f)[P]
                    for ts in range(NSL):
                        g = P * NSL + ts
                        pb = p_pr[g % 2]
                        scalar.wait_ge(mmp.h, Bmmp + g + 1)
                        sep.inc(scalar.activation(
                            out=dst[:, ts * 512:(ts + 1) * 512], in_=pb[:],
                            func=AF.Identity, bias=bcol[:, P * 8 + hp:P * 8 + hp + 1],
                            scale=1.0))

            @blk.vector
            def _(vector):
                vector.memset(vt[:], 1.0)
                vector.drain()
                for k in range(NT):
                    vector.wait_ge(mmt.h, Bmmt + k + 1)
                    vector.tensor_copy(out=vt[:, k * 130: k * 130 + 64],
                                       in_=p_tp[:, 0:64])
                    vev.inc(vector.drain())
                    vector.tensor_copy(out=vt[:, k * 130 + 65: k * 130 + 129],
                                       in_=p_tp[:, 64:128])
                    vev.inc(vector.drain())

    with ExitStack() as psc:
        p_s = [ctx.psum(f"{n}_ps{i}", [128, 128], F32, psc) for i in range(4)]
        p_o = [ctx.psum(f"{n}_po0", [128, 128], F32, psc),
               ctx.psum(f"{n}_po1", [128, 128], F32, psc)]
        p_bc = [ctx.psum(f"{n}_pbc{i}", [128, 128], F32, psc) for i in range(2)]

        with nc.Block(f"{n}_flash") as blk:
            def nkt(qt):
                return 1 if qt == 0 else 2

            CS = [0]
            for qt in range(NT):
                CS.append(CS[-1] + nkt(qt))

            @blk.tensor
            def _(tensor):
                lastslot = {}
                for qt in range(NT):
                    kts = [qt] if qt == 0 else [qt - 1, qt]
                    for i, kt in enumerate(kts):
                        bidx = 2 * i
                        gi = CS[qt] + i
                        if bidx in lastslot:
                            tensor.wait_ge(sse.h, Bsse + 2 * (lastslot[bidx] + 1))
                        lastslot[bidx] = gi
                        for h in range(2):
                            mms.inc(tensor.matmul(
                                p_s[bidx + h][:],
                                k2[h * 64:(h + 1) * 64, kt * 128:(kt + 1) * 128],
                                q2[h * 64:(h + 1) * 64, qt * 128:(qt + 1) * 128],
                                start=True, stop=True,
                                tile_position=(h * 64, 0)))
                    if qt >= 1:
                        tensor.wait_ge(veo.h, Bveo + 2 * qt)
                    for h in range(2):
                        for i, kt in enumerate(kts):
                            gi = CS[qt] + i
                            tensor.wait_ge(vem.h, Bvem + 2 * gi + h + 1)
                            ins = tensor.matmul(
                                p_o[h][0:65, :],
                                vt[:, kt * 130 + h * 65: kt * 130 + h * 65 + 65],
                                pt[:, (2 * i + h) * 128:(2 * i + h + 1) * 128],
                                start=(i == 0), stop=(i == len(kts) - 1))
                            if i == len(kts) - 1:
                                mmo.inc(ins)
                    for h in range(2):
                        tensor.wait_ge(vrc.h, Bvrc + 2 * qt + h + 1)
                        if qt >= 1:
                            tensor.wait_ge(veo.h, Bveo + 2 * qt + h - 1)
                        mmb.inc(tensor.matmul(
                            p_bc[h][:], ctx.ones_row()[:],
                            otr[0:1, h * 128:(h + 1) * 128],
                            start=True, stop=True))

            @blk.scalar
            def _(scalar):
                for qt in range(NT):
                    if qt >= 1:
                        scalar.wait_ge(mmo.h, Bmmo + 2 * qt)
                    kts = [qt] if qt == 0 else [qt - 1, qt]
                    for i, kt in enumerate(kts):
                        gi = CS[qt] + i
                        for h in range(2):
                            scalar.wait_ge(mms.h, Bmms + 2 * gi + h + 1)
                            sse.inc(scalar.activation(
                                out=pt[:, (2 * i + h) * 128:(2 * i + h + 1) * 128],
                                in_=p_s[2 * i + h][:], func=AF.Exp, scale=0.125))

            @blk.vector
            def _(vector):
                for qt in range(NT):
                    kts = [qt] if qt == 0 else [qt - 1, qt]
                    for i, kt in enumerate(kts):
                        gi = CS[qt] + i
                        mask = trid if kt == qt else tris
                        for h in range(2):
                            vector.wait_ge(sse.h, Bsse + 2 * gi + h + 1)
                            vector.tensor_tensor(
                                out=pt[:, (2 * i + h) * 128:(2 * i + h + 1) * 128],
                                in0=pt[:, (2 * i + h) * 128:(2 * i + h + 1) * 128],
                                in1=mask[:], op=ALU.mult)
                            vem.inc(vector.drain())
                    for h in range(2):
                        vector.wait_ge(mmo.h, Bmmo + 2 * qt + h + 1)
                        vector.reciprocal(out=otr[0:1, h * 128:(h + 1) * 128],
                                          in_=p_o[h][64:65, :])
                        vrc.inc(vector.drain())
                        vector.tensor_copy(out=ot[:, h * 128:(h + 1) * 128][0:64, :],
                                           in_=p_o[h][0:64, :])
                        vector.drain()
                        vector.wait_ge(mmb.h, Bmmb + 2 * qt + h + 1)
                        if qt >= 1:
                            vector.wait_ge(sth[h].h, Bsth[h] + 16 * qt)
                        vector.tensor_tensor(
                            out=at[:, h * 128:(h + 1) * 128][0:64, :],
                            in0=ot[:, h * 128:(h + 1) * 128][0:64, :],
                            in1=p_bc[h][0:64, :], op=ALU.mult)
                        veo.inc(vector.drain())

            @blk.sync
            def _(sync):
                for qt in range(NT):
                    for h in range(2):
                        sync.wait_ge(veo.h, Bveo + 2 * qt + h + 1)
                        sth[h].inc(sync.dma_start(
                            out=attn_dram[hp * 128 + h * 64: hp * 128 + h * 64 + 64,
                                          qt * 128:(qt + 1) * 128],
                            in_=at[:, h * 128:(h + 1) * 128][0:64, :]), 16)
                sync.wait_ge(sta.h, Bsta + 16 * NT)
                sync.wait_ge(stb.h, Bstb + 16 * NT)



def _matmul_add_phase(ctx, act_dram, x_base_dram, x_out_dram, WT_d, bc_d, name):
    nc, T = ctx.nc, ctx.T
    NS = T // 512
    with ExitStack() as sc:
        wb = ctx.sbuf(f"{name}_wb", [128, NDC * D], BF16, sc)
        bcs = ctx.sbuf(f"{name}_bcs", [128, NDC], F32, sc)
        a_sl = ctx.sbuf(f"{name}_asl", [128, NDC * 512 * 2], BF16, sc)
        xb_sl = ctx.sbuf(f"{name}_xb", [128, NDC * 512], F32, sc)
        gw = ctx.sem("mapS_gw")
        bls = ctx.sem("mapS_bls")
        ans = ctx.sem("mapS_ans")
        xbs = ctx.sem("mapS_xbs")
        mms = ctx.sem("mapS_mm")
        vy = ctx.sem("mapS_vy")
        sts = ctx.sem("mapS_sts")
        gwb, blb, anb, xbb = gw.v, bls.v, ans.v, xbs.v
        mmb, vyb, stb = mms.v, vy.v, sts.v

        p_y = [ctx.psum(f"{name}_py{i}", [128, 512], F32, sc) for i in range(2)]

        with nc.Block(f"{name}") as blk:
            @blk.gpsimd
            def _(gpsimd):
                gw.inc(gpsimd.dma_start(out=wb[:].rearrange("p (c f) -> p c f", f=D),
                                        in_=WT_d.rearrange("(c p) f -> p c f", p=128)), 16)

            @blk.sync
            def _(sync):
                bls.inc(sync.dma_start(out=bcs[:], in_=bc_d[:]), 16)
                for s in range(NS):
                    if s >= 2:
                        sync.wait_ge(vy.h, vyb + NDC * (s - 1))
                    ba = (s % 2) * NDC * 512
                    for dc in range(NDC):
                        ans.inc(sync.dma_start(
                            out=a_sl[:, ba + dc * 512: ba + dc * 512 + 512],
                            in_=act_dram[dc * 128:(dc + 1) * 128,
                                         s * 512:(s + 1) * 512]), 16)
                    if s >= 1:
                        sync.wait_ge(sts.h, stb + 16 * NDC * s)
                    for dc in range(NDC):
                        xbs.inc(sync.dma_start(
                            out=xb_sl[:, dc * 512:(dc + 1) * 512],
                            in_=x_base_dram[dc * 128:(dc + 1) * 128,
                                            s * 512:(s + 1) * 512]), 16)
                    sync.wait_ge(vy.h, vyb + NDC * (s + 1))
                    for dc in range(NDC):
                        sts.inc(sync.dma_start(
                            out=x_out_dram[dc * 128:(dc + 1) * 128,
                                           s * 512:(s + 1) * 512],
                            in_=xb_sl[:, dc * 512: dc * 512 + 512]), 16)
                sync.wait_ge(sts.h, stb + 16 * NDC * NS)

            @blk.tensor
            def _(tensor):
                tensor.wait_ge(gw.h, gwb + 16)
                for s in range(NS):
                    ba = (s % 2) * NDC * 512
                    tensor.wait_ge(ans.h, anb + 16 * NDC * (s + 1))
                    for do in range(NDC):
                        g = s * NDC + do
                        pb = p_y[do % 2]
                        if g >= 2:
                            tensor.wait_ge(vy.h, vyb + g - 1)
                        for dc in range(NDC):
                            ins = tensor.matmul(
                                pb[:], wb[:, dc * D + do * 128: dc * D + do * 128 + 128],
                                a_sl[:, ba + dc * 512: ba + dc * 512 + 512],
                                start=(dc == 0), stop=(dc == NDC - 1))
                            if dc == NDC - 1:
                                mms.inc(ins)

            @blk.vector
            def _(vector):
                vector.wait_ge(bls.h, blb + 16)
                for s in range(NS):
                    vector.wait_ge(xbs.h, xbb + 16 * NDC * (s + 1))
                    for do in range(NDC):
                        pb = p_y[do % 2]
                        vector.wait_ge(mms.h, mmb + s * NDC + do + 1)
                        vector.scalar_tensor_tensor(
                            out=xb_sl[:, do * 512: do * 512 + 512],
                            in0=pb[:], scalar=bcs[:, do:do + 1],
                            in1=xb_sl[:, do * 512:(do + 1) * 512],
                            op0=ALU.add, op1=ALU.add)
                        vy.inc(vector.drain())



BAND = 6
CHK = 32

def emit_l0attn(ctx, xn_dram, x_base_dram, x_mid_dram, W, name="l0a"):
    nc, T = ctx.nc, ctx.T
    NG = T // 128
    NCH = T // CHK
    attn_dram = ctx.dram_scratch(f"{name}_attn", [D, T], BF16)
    beta_dram = ctx.dram_scratch(f"{name}_beta", [8, T], F32)

    identf = ctx.const_sb("identf", f32c(np.eye(128)))
    identb = ctx.ident_bf()
    blk32 = (np.arange(128)[:, None] // 32) == (np.arange(128)[None, :] // 32)
    dist = np.arange(128)[None, :] - np.arange(128)[:, None]
    m_su = ctx.const_sb("m_su", f32c(blk32 & (dist >= 1)))
    m_ui = ctx.const_sb("m_ui", f32c(blk32 & (dist >= 0)))
    m_farW = ctx.const_sb("m_farW", f32c(blk32 & (dist > BAND)))
    m_farWq = m_farW
    shifts = []
    for dd in range(0, BAND + 1):
        sh = (dist == dd) & blk32
        shifts.append(ctx.const_sb(f"m_sh{dd}", bfc(sh)))

    with ExitStack() as sc:
        t1 = ctx.sbuf(f"{name}_t1", [128, T], BF16, sc)
        t2 = ctx.sbuf(f"{name}_t2", [128, T], BF16, sc)
        btok = ctx.sbuf(f"{name}_btok", [128, NG * 8], F32, sc)
        acol = ctx.sbuf(f"{name}_acol", [128, 8], F32, sc)
        dtbc = ctx.sbuf(f"{name}_dtb", [128, 8], F32, sc)
        onw = ctx.sbuf(f"{name}_onw", [128, 1], F32, sc)
        xnr = ctx.sbuf(f"{name}_xnr", [128, NDC * T], BF16, sc)

        _l0_pre(ctx, sc, W, xn_dram, beta_dram, t1, t2, btok, acol, dtbc, onw,
                xnr, name)
        for h in range(8):
            _l0_head(ctx, h, W, xnr, t1, t2, btok, acol, dtbc, onw,
                     attn_dram, identf, identb, m_su, m_ui, m_farW, shifts,
                     name)

    _matmul_add_phase(ctx, attn_dram, x_base_dram, x_mid_dram, W["WoT"],
                      W["zb8"], f"{name}_wo")


def _l0_pre(ctx, sc, W, xn_dram, beta_dram, t1, t2, btok, acol, dtbc, onw,
            xnr, name):
    nc, T = ctx.nc, ctx.T
    NS = T // 512
    NG = T // 128
    with ExitStack() as psc:
        p_a = ctx.psum(f"{name}_ppa", [128, 512], F32, psc)
        p_b = ctx.psum(f"{name}_ppb", [128, 512], F32, psc)
        p_c = ctx.psum(f"{name}_ppc", [8, 512], F32, psc)
        p_t = ctx.psum(f"{name}_ppt", [128, 8], F32, psc)
        p_tb = ctx.psum(f"{name}_pptb", [128, 8], BF16, psc)
        wfa = ctx.sbuf(f"{name}_wfa", [128, NDC * 128], BF16, psc)
        wga = ctx.sbuf(f"{name}_wga", [128, NDC * 128], BF16, psc)
        wbt = ctx.sbuf(f"{name}_wbt", [128, NDC * 8], BF16, psc)
        asb = ctx.sbuf(f"{name}_asb", [1, 8], F32, psc)
        bsl = ctx.sbuf(f"{name}_bsl", [8, T], BF16, psc)

        gld = ctx.sem(f"{name}_pgld")
        sld = ctx.sem(f"{name}_psld")
        xld = ctx.sem(f"{name}_pxld")
        mm = ctx.sem(f"{name}_pmm")
        mmA = ctx.sem(f"{name}_pmmA")
        mmT = ctx.sem(f"{name}_pmmT")
        se1 = ctx.sem(f"{name}_pse1")
        se2 = ctx.sem(f"{name}_pse2")
        se3 = ctx.sem(f"{name}_pse3")
        vA = ctx.sem(f"{name}_pvA")
        vacol = ctx.sem(f"{name}_pvac")
        bld = ctx.sem(f"{name}_pbld")

        with nc.Block(f"{name}_pre") as blk:
            @blk.gpsimd
            def _(gpsimd):
                gld.inc(gpsimd.dma_start(
                    out=wfa[:].rearrange("p (c f) -> p c f", f=128),
                    in_=W["WfaT"].rearrange("(c p) f -> p c f", p=128)), 16)
                gpsimd.wait_ge(gld.h, 16)
                gld.inc(gpsimd.dma_start(
                    out=wga[:].rearrange("p (c f) -> p c f", f=128),
                    in_=W["WgaT"].rearrange("(c p) f -> p c f", p=128)), 16)
                gpsimd.wait_ge(gld.h, 32)
                gld.inc(gpsimd.dma_start(
                    out=wbt[:].rearrange("p (c f) -> p c f", f=8),
                    in_=W["WbT"].rearrange("(c p) f -> p c f", p=128)), 16)

            @blk.sync
            def _(sync):
                sld.inc(sync.dma_start(out=asb[0:1, :], in_=W["A_log"]), 16)
                sld.inc(sync.dma_start(out=dtbc[:], in_=W["dtb"]), 16)
                sld.inc(sync.dma_start(out=onw[:], in_=W["onwc"]), 16)
                for dc in range(NDC):
                    xld.inc(sync.dma_start(out=xnr[:, dc * T:(dc + 1) * T],
                                           in_=xn_dram[dc * 128:(dc + 1) * 128, :]), 16)

            @blk.tensor
            def _(tensor):
                tensor.wait_ge(gld.h, 48)
                tensor.wait_ge(xld.h, 16 * NDC)
                for s in range(NS):
                    if s >= 1:
                        tensor.wait_ge(se1.h, 2 * s)
                    for dc in range(NDC):
                        ins = tensor.matmul(
                            p_a[:], wfa[:, dc * 128:(dc + 1) * 128],
                            xnr[:, dc * T + s * 512: dc * T + s * 512 + 512],
                            start=(dc == 0), stop=(dc == NDC - 1))
                        if dc == NDC - 1:
                            mm.inc(ins)
                    for dc in range(NDC):
                        ins = tensor.matmul(
                            p_b[:], wga[:, dc * 128:(dc + 1) * 128],
                            xnr[:, dc * T + s * 512: dc * T + s * 512 + 512],
                            start=(dc == 0), stop=(dc == NDC - 1))
                        if dc == NDC - 1:
                            mm.inc(ins)
                for s in range(NS):
                    if s >= 1:
                        tensor.wait_ge(se2.h, s)
                    for dc in range(NDC):
                        ins = tensor.matmul(
                            p_c[:], wbt[:, dc * 8:(dc + 1) * 8],
                            xnr[:, dc * T + s * 512: dc * T + s * 512 + 512],
                            start=(dc == 0), stop=(dc == NDC - 1))
                        if dc == NDC - 1:
                            mm.inc(ins)
                tensor.wait_ge(vA.h, 1)
                mmA.inc(tensor.matmul(p_t[:], ctx.ones_row()[:], asb[0:1, :],
                                      start=True, stop=True))
                tensor.wait_ge(vacol.h, 1)
                tensor.wait_ge(se2.h, NS)
                for g in range(NG):
                    if g >= 1:
                        tensor.wait_ge(bld.h, g)
                    mmT.inc(tensor.transpose(p_t[:, 0:8] if False else p_t[:],
                                             bsl[:, g * 128:(g + 1) * 128],
                                             identf_g(ctx)))

            @blk.scalar
            def _(scalar):
                scalar.wait_ge(sld.h, 48)
                se3.inc(scalar.activation(out=asb[0:1, :], in_=asb[0:1, :],
                                          func=AF.Exp))
                for s in range(NS):
                    scalar.wait_ge(mm.h, 2 * s + 1)
                    se1.inc(scalar.activation(out=t1[:, s * 512:(s + 1) * 512],
                                              in_=p_a[:], func=AF.Identity))
                    scalar.wait_ge(mm.h, 2 * s + 2)
                    se1.inc(scalar.activation(out=t2[:, s * 512:(s + 1) * 512],
                                              in_=p_b[:], func=AF.Identity))
                for s in range(NS):
                    scalar.wait_ge(mm.h, 2 * NS + s + 1)
                    se2.inc(scalar.activation(out=bsl[0:8, s * 512:(s + 1) * 512],
                                              in_=p_c[:], func=AF.Sigmoid))

            @blk.vector
            def _(vector):
                vector.wait_ge(se3.h, 1)
                vector.tensor_scalar(out=asb[0:1, :], in0=asb[0:1, :],
                                     scalar1=-1.0, scalar2=None, op0=ALU.mult)
                vA.inc(vector.drain())
                vector.wait_ge(mmA.h, 1)
                vector.tensor_copy(out=acol[:], in_=p_t[:])
                vacol.inc(vector.drain())
                for g in range(NG):
                    vector.wait_ge(mmT.h, g + 1)
                    vector.tensor_copy(out=btok[:, g * 8:(g + 1) * 8], in_=p_t[:])
                    bld.inc(vector.drain())


def identf_g(ctx):
    return ctx.consts["identf"]



class Auto:
    def __init__(self, ctx, name, sems):
        self.ctx = ctx
        self.name = name
        self.sems = sems
        self.ops = []
        self.last_write = {}
        self.last_reads = {}
        self.deps = []
        self.last_dma = {}

    def op(self, eng, emit, reads=(), writes=(), dma=False, ndma=1):
        idx = len(self.ops)
        waits = {}
        same = False
        for r in reads:
            lw = self.last_write.get(r)
            if lw is not None:
                if lw[0] != eng:
                    waits[lw] = True
                else:
                    same = True
        for w in writes:
            lw = self.last_write.get(w)
            if lw is not None:
                if lw[0] != eng:
                    waits[lw] = True
                else:
                    same = True
            for lr in self.last_reads.get(w, ()):
                if lr[0] != eng:
                    waits[lr] = True
                else:
                    same = True
        self.ops.append((eng, emit, tuple(reads), tuple(writes),
                         ndma if dma else 0))
        self.deps.append((tuple(waits.keys()), same))
        for w in writes:
            self.last_write[w] = (eng, idx)
            self.last_reads[w] = []
        for r in reads:
            self.last_reads.setdefault(r, []).append((eng, idx))
        return idx

    def emit(self):
        nc = self.ctx.nc
        needed = set()
        for waits, _ in self.deps:
            needed.update(waits)
        engops = {e: [] for e in ENGS}
        for idx, (eng, *_rest) in enumerate(self.ops):
            engops[eng].append(idx)
        semval = {}
        base = {e: self.sems[e].v for e in ENGS}
        cnt = dict(base)
        incset = set()
        for idx, (eng, emit, reads, writes, dma) in enumerate(self.ops):
            if dma or (eng, idx) in needed:
                cnt[eng] += 16 * dma if dma else 1
                semval[(eng, idx)] = cnt[eng]
                incset.add(idx)
        self.ctx.emit_const_preload()
        blk_name = self.name
        ctx = self.ctx
        sems = self.sems
        ops = self.ops
        deps = self.deps
        with nc.Block(blk_name) as blk:
            def section(eng_name):
                def body(eng):
                    prev_dma_val = None
                    for idx in engops[eng_name]:
                        _eng, emit, reads, writes, dma = ops[idx]
                        waits, same = deps[idx]
                        for (weng, widx) in waits:
                            eng.wait_ge(sems[weng].h, semval[(weng, widx)])
                        if dma and prev_dma_val is not None:
                            eng.wait_ge(sems[eng_name].h, prev_dma_val)
                        if same and eng_name in ("vector", "scalar", "gpsimd") and not dma:
                            eng.drain()
                        ins = emit(eng)
                        if idx in incset:
                            if dma:
                                for i2 in (ins if isinstance(ins, list)
                                           else [ins]):
                                    i2.then_inc(sems[eng_name].h, 16)
                                prev_dma_val = semval[(eng_name, idx)]
                            else:
                                if ins is None:
                                    eng.drain().then_inc(sems[eng_name].h, 1)
                                else:
                                    ins.then_inc(sems[eng_name].h, 1)
                    if prev_dma_val is not None or any(
                            ops[i][4] for i in engops[eng_name]):
                        eng.wait_ge(sems[eng_name].h, cnt[eng_name])
                return body
            for eng_name in ENGS:
                if engops[eng_name]:
                    getattr(blk, eng_name)(section(eng_name))
        for e in ENGS:
            self.sems[e].v = cnt[e]


def make_auto_sems(ctx):
    return {e: ctx.sem(f"auto_{e}") for e in ENGS}


CLIP_HI = 21.0
CLIP_LO = -86.0


def emit_l0attn(ctx, xn_dram, x_base_dram, x_mid_dram, W, name="l0a"):
    nc, T = ctx.nc, ctx.T
    NG = T // 128
    NCH = T // CHK
    attn_dram = ctx.dram_scratch(f"{name}_attn", [D, T], BF16)

    identf = ctx.const_sb("identf", f32c(np.eye(128)))
    identb = ctx.ident_bf()
    blk32 = (np.arange(128)[:, None] // 32) == (np.arange(128)[None, :] // 32)
    dist = np.arange(128)[None, :] - np.arange(128)[:, None]
    masks = {
        "identf": identf, "identb": identb,
        "m_su": ctx.const_sb("m_su", f32c(blk32 & (dist >= 1))),
        "m_ui": ctx.const_sb("m_ui", f32c(blk32 & (dist >= 0))),
        "m_far": ctx.const_sb("m_far", f32c(blk32 & (dist > BAND))),
        "m_farq": ctx.const_sb("m_farq", f32c(blk32 & (dist > BAND))),
    }
    for dd in range(0, BAND + 1):
        sh = ((np.arange(128)[None, :] - np.arange(128)[:, None]) == -dd) & blk32
        masks[f"shn{dd}"] = ctx.const_sb(f"m_shn{dd}", bfc(sh))

    sems = ctx.auto_sems()
    ctx.fconst(1e-6); ctx.fconst(1.0); ctx.fconst(1e-5)
    ctx.fconst(math.log(1.0 / 128.0)); ctx.fconst(math.log(1.0 / math.sqrt(128.0)))
    ctx.ones_col(); ctx.ones_col_bf(); ctx.ones_row()
    with ExitStack() as sc:
        B = {}
        B["xn_dram"] = xn_dram
        B["t1"] = ctx.sbuf(f"{name}_t1", [128, T], BF16, sc)
        B["t2"] = ctx.sbuf(f"{name}_t2", [128, T], BF16, sc)
        B["btok"] = ctx.sbuf(f"{name}_btok", [128, NG * 8], F32, sc)
        B["acol"] = ctx.sbuf(f"{name}_acol", [128, 8], F32, sc)
        B["dtbc"] = ctx.sbuf(f"{name}_dtbc", [128, 8], F32, sc)
        B["onw"] = ctx.sbuf(f"{name}_onw", [128, 1], F32, sc)
        B["whb"] = ctx.sbuf(f"{name}_whb", [128, 3 * NDC * 128 + 256], BF16, sc)
        B["xsl"] = ctx.sbuf(f"{name}_xsl", [128, NDC * 512], BF16, sc)
        for k, dt_ in (("fbuf", F32), ("cbuf", F32), ("gbuf", F32), ("dbuf", F32)):
            B[k] = ctx.sbuf(f"{name}_{k}", [128, T], dt_, sc)
        for k in ("qb", "kb", "vb", "gsig", "kap", "kapp", "ktl", "qhat",
                  "khat", "qkp"):
            B[k] = ctx.sbuf(f"{name}_{k}", [128, T], BF16, sc)
        B["wcols"] = ctx.sbuf(f"{name}_wcols", [128, (2 * BAND + 1) * NG], F32, sc)
        B["ge"] = ctx.sbuf(f"{name}_ge", [128, NCH * 4], F32, sc)
        B["rrow"] = ctx.sbuf(f"{name}_rrow", [1, 512], F32, sc)
        B["Sbf"] = ctx.sbuf(f"{name}_S", [128, 128], BF16, sc)
        for k in ("wfarb", "wqfarb", "Msb", "M2", "M4", "M8", "M16", "Psb",
                  "WqTm", "khatG", "rt", "rtok", "dsb", "obuf", "asb2"):
            B[k] = ctx.sbuf(f"{name}_{k}", [128, 128], BF16, sc)
        B["tbuf2"] = ctx.sbuf(f"{name}_tbuf2", [128, 128], F32, sc)
        B["diag"] = [ctx.sbuf(f"{name}_diag{i}", [128, 128], BF16, sc) for i in range(2)]
        B["obq"] = ctx.sbuf(f"{name}_obq", [128, 4], F32, sc)

        _l0_pre2(ctx, sems, W, B, name)
        for h in range(8):
            _l0_head(ctx, h, W, sems, B, masks, attn_dram, name)

    _matmul_add_phase(ctx, attn_dram, x_base_dram, x_mid_dram, W["WoT"],
                      W["zb8"], f"{name}_wo")


def _l0_pre2(ctx, sems, W, B, name):
    nc, T = ctx.nc, ctx.T
    NS = T // 512
    NG = T // 128
    au = Auto(ctx, f"{name}_pre", sems)
    with ExitStack() as psc:
        p_a = ctx.psum(f"{name}_ppa", [128, 512], F32, psc)
        p_b = ctx.psum(f"{name}_ppb", [128, 512], F32, psc)
        p_c = ctx.psum(f"{name}_ppc", [8, 512], F32, psc)
        p_t = ctx.psum(f"{name}_ppt", [128, 8], F32, psc)
        p_tb = ctx.psum(f"{name}_pptb", [128, 8], BF16, psc)
        wfa = ctx.sbuf(f"{name}_wfa", [128, NDC * 128], BF16, psc)
        wga = ctx.sbuf(f"{name}_wga", [128, NDC * 128], BF16, psc)
        wbt = ctx.sbuf(f"{name}_wbt", [128, NDC * 8], BF16, psc)
        asb = ctx.sbuf(f"{name}_asb", [1, 8], F32, psc)
        bsl = ctx.sbuf(f"{name}_bsl", [8, T], BF16, psc)
        xnr = B["xsl"]

        au.op("gpsimd", lambda e: e.dma_start(
            out=wfa[:].rearrange("p (c f) -> p c f", f=128),
            in_=W["WfaT"].rearrange("(c p) f -> p c f", p=128)),
            writes=[("wfa",)], dma=True)
        au.op("gpsimd", lambda e: e.dma_start(
            out=wga[:].rearrange("p (c f) -> p c f", f=128),
            in_=W["WgaT"].rearrange("(c p) f -> p c f", p=128)),
            writes=[("wga",)], dma=True)
        au.op("gpsimd", lambda e: e.dma_start(
            out=wbt[:].rearrange("p (c f) -> p c f", f=8),
            in_=W["WbT"].rearrange("(c p) f -> p c f", p=128)),
            writes=[("wbt",)], dma=True)
        au.op("sync", lambda e: e.dma_start(out=asb[0:1, :], in_=W["A_log"]),
              writes=[("asb",)], dma=True)
        au.op("sync", lambda e: e.dma_start(out=B["dtbc"][:], in_=W["dtb"]),
              writes=[("dtbc",)], dma=True)
        au.op("sync", lambda e: e.dma_start(out=B["onw"][:], in_=W["onwc"]),
              writes=[("onw",)], dma=True)
        au.op("scalar", lambda e: e.activation(out=asb[0:1, :], in_=asb[0:1, :],
                                               func=AF.Exp),
              reads=[("asb",)], writes=[("asb",)])
        au.op("vector", lambda e: e.tensor_scalar(out=asb[0:1, :], in0=asb[0:1, :],
                                                  scalar1=-1.0, scalar2=None,
                                                  op0=ALU.mult),
              reads=[("asb",)], writes=[("asb",)])
        au.op("tensor", lambda e: e.matmul(p_t[:], ctx.ones_row()[:], asb[0:1, :],
                                           start=True, stop=True),
              reads=[("asb",)], writes=[("p_t",)])
        au.op("vector", lambda e: e.tensor_copy(out=B["acol"][:], in_=p_t[:]),
              reads=[("p_t",)], writes=[("acol",)])
        for s in range(NS):
            bx = 0
            for dc in range(NDC):
                au.op("sync", lambda e, bx=bx, dc=dc, s=s: e.dma_start(
                    out=xnr[:, bx + dc * 512: bx + dc * 512 + 512],
                    in_=B["xn_dram"][dc * 128:(dc + 1) * 128, s * 512:(s + 1) * 512]),
                    writes=[("xsl", dc)], dma=True)

            def mmgrp(e, pb, wsb, wd, bx=0, s=s):
                ins = None
                for dc in range(NDC):
                    ins = e.matmul(pb[:], wsb[:, dc * wd:(dc + 1) * wd],
                                   xnr[:, bx + dc * 512: bx + dc * 512 + 512],
                                   start=(dc == 0), stop=(dc == NDC - 1))
                return ins
            rd = [("xsl", dc) for dc in range(NDC)]
            au.op("tensor", lambda e, f=mmgrp: f(e, p_a, wfa, 128),
                  reads=rd + [("wfa",)], writes=[("p_a",)])
            au.op("scalar", lambda e, s=s: e.activation(
                out=B["t1"][:, s * 512:(s + 1) * 512], in_=p_a[:], func=AF.Identity),
                reads=[("p_a",)], writes=[("t1",)])
            au.op("tensor", lambda e, f=mmgrp: f(e, p_b, wga, 128),
                  reads=rd + [("wga",)], writes=[("p_b",)])
            au.op("scalar", lambda e, s=s: e.activation(
                out=B["t2"][:, s * 512:(s + 1) * 512], in_=p_b[:], func=AF.Identity),
                reads=[("p_b",)], writes=[("t2",)])
            au.op("tensor", lambda e, f=mmgrp: f(e, p_c, wbt, 8),
                  reads=rd + [("wbt",)], writes=[("p_c",)])
            au.op("scalar", lambda e, s=s: e.activation(
                out=bsl[0:8, s * 512:(s + 1) * 512], in_=p_c[:], func=AF.Sigmoid),
                reads=[("p_c",)], writes=[("bsl", s)])
        for g in range(NG):
            au.op("tensor", lambda e, g=g: e.transpose(
                p_tb[:], bsl[:, g * 128:(g + 1) * 128],
                ctx.ident_bf()[0:8, 0:8]),
                reads=[("bsl", g // 4), ("acol",)], writes=[("p_tb",)])
            au.op("vector", lambda e, g=g: e.tensor_copy(
                out=B["btok"][:, g * 8:(g + 1) * 8], in_=p_tb[:]),
                reads=[("p_tb",)], writes=[("btok",)])
        au.emit()


def _l0_head(ctx, h, W, sems, B, masks, attn_dram, name):
    nc, T = ctx.nc, ctx.T
    NS = T // 512
    NG = T // 128
    NCH = T // CHK
    n = f"{name}h{h}"
    NBD = BAND
    fbuf, cbuf, gbuf, dbuf = B["fbuf"], B["cbuf"], B["gbuf"], B["dbuf"]
    qb, kb, vb = B["qb"], B["kb"], B["vb"]
    xsl = B["xsl"]
    whb = B["whb"]
    ge = B["ge"]
    rrow = B["rrow"]
    au = Auto(ctx, f"{n}_h1", sems)
    DHI = 3 * NDC * 128

    with ExitStack() as psc:
        p1 = ctx.psum(f"{n}_p1", [128, 512], F32, psc)
        p2 = ctx.psum(f"{n}_p2", [128, 512], F32, psc)
        pr = ctx.psum(f"{n}_pr", [1, 512], F32, psc)
        pbc = ctx.psum(f"{n}_pbc", [128, 512], F32, psc)
        pwc = ctx.psum(f"{n}_pwc", [128, NG], F32, psc)

        for i, wn in enumerate(("WqT", "WkT", "WvT")):
            au.op("gpsimd", lambda e, i=i, wn=wn: e.dma_start(
                out=whb[:, i * NDC * 128:(i + 1) * NDC * 128].rearrange(
                    "p (c f) -> p c f", f=128),
                in_=W[wn][:, h * 128:(h + 1) * 128].rearrange(
                    "(c p) f -> p c f", p=128)),
                writes=[("whb", i)], dma=True)
        au.op("gpsimd", lambda e: e.dma_start(
            out=whb[:, DHI:DHI + 128], in_=W["WfbT"][:, h * 128:(h + 1) * 128]),
            writes=[("whb", 3)], dma=True)
        au.op("gpsimd", lambda e: e.dma_start(
            out=whb[:, DHI + 128:DHI + 256], in_=W["WgbT"][:, h * 128:(h + 1) * 128]),
            writes=[("whb", 4)], dma=True)

        for P, (dst, wcol_name, do_rms) in enumerate(
                (("qb", "wqc", True), ("kb", "wkc", True), ("vb", "wvc", False))):
            for s in range(NS):
                j = P * NS + s
                bx = 0
                for dc in range(NDC):
                    au.op("sync", lambda e, bx=bx, dc=dc, s=s: e.dma_start(
                        out=xsl[:, bx + dc * 512: bx + dc * 512 + 512],
                        in_=B["xn_dram"][dc * 128:(dc + 1) * 128,
                                         s * 512:(s + 1) * 512]),
                        writes=[("xsl", dc)], dma=True)
                rd = [("xsl", dc) for dc in range(NDC)] + [("whb", P)]
                pb = (p1, p2)[j % 2]
                pk = ("p1",) if j % 2 == 0 else ("p2",)

                def mmg(e, pb=pb, bx=bx, P=P):
                    ins = None
                    for dc in range(NDC):
                        ins = e.matmul(
                            pb[:], whb[:, (P * NDC + dc) * 128:(P * NDC + dc + 1) * 128],
                            xsl[:, bx + dc * 512: bx + dc * 512 + 512],
                            start=(dc == 0), stop=(dc == NDC - 1))
                    return ins
                au.op("tensor", mmg, reads=rd, writes=[pk])
                au.op("scalar", lambda e, s=s, pb=pb: e.activation(
                    out=fbuf[:, s * 512:(s + 1) * 512], in_=pb[:], func=AF.Identity),
                    reads=[pk], writes=[("fbuf",)])
            FR = [("fbuf",)]
            wc = W[wcol_name + "_sb"]
            au.op("vector", lambda e, wc=wc: e.tensor_scalar(
                out=cbuf[:], in0=fbuf[:], scalar1=wc[:, h * 4 + 3:h * 4 + 4],
                scalar2=None, op0=ALU.mult),
                reads=FR, writes=[("cbuf",)])
            for i in range(3):
                sh = 3 - i
                au.op("vector", lambda e, i=i, sh=sh, wc=wc: e.scalar_tensor_tensor(
                    out=cbuf[:, sh:T], in0=fbuf[:, 0:T - sh],
                    scalar=wc[:, h * 4 + i:h * 4 + i + 1],
                    in1=cbuf[:, sh:T], op0=ALU.mult, op1=ALU.add),
                    reads=FR + [("cbuf",)], writes=[("cbuf",)])
            au.op("scalar", lambda e: e.activation(out=cbuf[:], in_=cbuf[:],
                                                   func=AF.Silu),
                  reads=[("cbuf",)], writes=[("cbuf",)])
            if not do_rms:
                au.op("vector", lambda e, dst=dst: e.tensor_copy(out=B[dst][:],
                                                                 in_=cbuf[:]),
                      reads=[("cbuf",)], writes=[(dst,)])
            else:
                au.op("vector", lambda e: e.tensor_tensor(out=fbuf[:], in0=cbuf[:],
                                                          in1=cbuf[:], op=ALU.mult),
                      reads=[("cbuf",)], writes=[("fbuf",)])
                lnsc = math.log(1.0 / 128.0) if dst == "qb" else math.log(1.0 / math.sqrt(128.0))
                for s in range(NS):
                    au.op("tensor", lambda e, s=s: e.matmul(
                        pr[:], ctx.ones_col()[:], fbuf[:, s * 512:(s + 1) * 512],
                        start=True, stop=True),
                        reads=[("fbuf",)], writes=[("pr",)])
                    au.op("scalar", lambda e, fc=ctx.fconst(1e-6): e.activation(
                        out=rrow[0:1, :], in_=pr[:], func=AF.Ln, scale=1.0 / 128.0,
                        bias=fc[0:1, 0:1]),
                        reads=[("pr",)], writes=[("rrow",)])
                    au.op("scalar", lambda e, fc=ctx.fconst(lnsc): e.activation(
                        out=rrow[0:1, :], in_=rrow[0:1, :], func=AF.Exp, scale=-0.5,
                        bias=fc[0:1, 0:1]),
                        reads=[("rrow",)], writes=[("rrow",)])
                    au.op("tensor", lambda e, s=s: e.matmul(
                        pbc[:], ctx.ones_row()[:], rrow[0:1, :],
                        start=True, stop=True),
                        reads=[("rrow",)], writes=[("pbc",)])
                    au.op("vector", lambda e, s=s, dst=dst: e.tensor_tensor(
                        out=B[dst][:, s * 512:(s + 1) * 512],
                        in0=cbuf[:, s * 512:(s + 1) * 512], in1=pbc[:], op=ALU.mult),
                        reads=[("pbc",), ("cbuf",)], writes=[(dst,)])
        for s in range(NS):
            au.op("tensor", lambda e, s=s: e.matmul(
                p1[:], whb[:, DHI:DHI + 128], B["t1"][:, s * 512:(s + 1) * 512],
                start=True, stop=True),
                reads=[("whb", 3), ("t1",)], writes=[("p1",)])
            au.op("scalar", lambda e, s=s: e.activation(
                out=fbuf[:, s * 512:(s + 1) * 512], in_=p1[:], func=AF.Identity,
                bias=B["dtbc"][:, h:h + 1], scale=1.0),
                reads=[("p1",), ("dtbc",)], writes=[("fbuf",)])
            au.op("tensor", lambda e, s=s: e.matmul(
                p2[:], whb[:, DHI + 128:DHI + 256], B["t2"][:, s * 512:(s + 1) * 512],
                start=True, stop=True),
                reads=[("whb", 4), ("t2",)], writes=[("p2",)])
            au.op("scalar", lambda e, s=s: e.activation(
                out=B["gsig"][:, s * 512:(s + 1) * 512], in_=p2[:], func=AF.Sigmoid),
                reads=[("p2",)], writes=[("gsig",)])
        GFR = [("fbuf",)]
        au.op("scalar", lambda e: e.activation(out=cbuf[:], in_=fbuf[:], func=AF.Abs),
              reads=GFR, writes=[("cbuf",)])
        au.op("scalar", lambda e: e.activation(out=cbuf[:], in_=cbuf[:], func=AF.Exp,
                                               scale=-1.0),
              reads=[("cbuf",)], writes=[("cbuf",)])
        au.op("scalar", lambda e, fc=ctx.fconst(1.0): e.activation(
            out=cbuf[:], in_=cbuf[:], func=AF.Ln, bias=fc[:, 0:1]),
              reads=[("cbuf",)], writes=[("cbuf",)])
        au.op("vector", lambda e: e.tensor_scalar(out=gbuf[:], in0=fbuf[:],
                                                  scalar1=0.0, scalar2=None,
                                                  op0=ALU.max),
              reads=GFR, writes=[("gbuf",)])
        au.op("vector", lambda e: e.tensor_tensor(out=gbuf[:], in0=gbuf[:],
                                                  in1=cbuf[:], op=ALU.add),
              reads=[("gbuf",), ("cbuf",)], writes=[("gbuf",)])
        au.op("vector", lambda e: e.tensor_scalar(out=gbuf[:], in0=gbuf[:],
                                                  scalar1=B["acol"][:, h:h + 1],
                                                  scalar2=None, op0=ALU.mult),
              reads=[("gbuf",), ("acol",)], writes=[("gbuf",)])
        au.op("vector", lambda e: e.tensor_tensor_scan(
            out=fbuf[:], data0=gbuf[:], data1=gbuf[:], initial=0.0,
            op0=ALU.add, op1=ALU.bypass),
            reads=[("gbuf",)], writes=[("fbuf",)])
        au.op("vector", lambda e: e.tensor_copy(
            out=ge[:, 0:NCH], in_=strided_cols(fbuf, CHK - 1, CHK, NCH)),
            reads=[("fbuf",)], writes=[("ge", 0)])
        au.op("vector", lambda e: e.memset(ge[:, NCH:NCH + 1], 0.0),
              writes=[("ge", 1)])
        au.op("vector", lambda e: e.tensor_copy(out=ge[:, NCH + 1:2 * NCH],
                                                in_=ge[:, 0:NCH - 1]),
              reads=[("ge", 0)], writes=[("ge", 1)])
        au.op("vector", lambda e: e.tensor_copy(
            out=cbuf[:].rearrange("p (c r) -> p c r", r=CHK),
            in_=expand_ap(ge[:, NCH:2 * NCH], CHK)),
            reads=[("ge", 1)], writes=[("cbuf",)])
        au.op("vector", lambda e: e.tensor_tensor(out=gbuf[:], in0=fbuf[:],
                                                  in1=cbuf[:], op=ALU.subtract),
              reads=[("fbuf",), ("cbuf",)], writes=[("gbuf",)])
        au.op("vector", lambda e: e.tensor_tensor(out=ge[:, 2 * NCH:3 * NCH],
                                                  in0=ge[:, 0:NCH],
                                                  in1=ge[:, NCH:2 * NCH],
                                                  op=ALU.subtract),
              reads=[("ge", 0), ("ge", 1)], writes=[("ge", 2)])
        au.op("scalar", lambda e: e.activation(out=ge[:, 2 * NCH:3 * NCH],
                                               in_=ge[:, 2 * NCH:3 * NCH],
                                               func=AF.Exp),
              reads=[("ge", 2)], writes=[("ge", 2)])
        au.op("vector", lambda e: e.tensor_copy(
            out=ge[:, 3 * NCH:4 * NCH],
            in_=strided_cols(gbuf, CHK // 2 - 1, CHK, NCH)),
              reads=[("gbuf",)], writes=[("ge", 3)])
        au.op("vector", lambda e: e.tensor_copy(
            out=cbuf[:].rearrange("p (c r) -> p c r", r=CHK),
            in_=expand_ap(ge[:, 3 * NCH:4 * NCH], CHK)),
            reads=[("ge", 3)], writes=[("cbuf",)])
        au.op("vector", lambda e: e.tensor_tensor(out=cbuf[:], in0=gbuf[:],
                                                  in1=cbuf[:], op=ALU.subtract),
              reads=[("gbuf",), ("cbuf",)], writes=[("cbuf",)])
        au.op("vector", lambda e: e.tensor_scalar(out=dbuf[:], in0=cbuf[:],
                                                  scalar1=86.0, scalar2=None,
                                                  op0=ALU.min),
              reads=[("cbuf",)], writes=[("dbuf",)])
        au.op("vector", lambda e: e.tensor_scalar(out=dbuf[:], in0=dbuf[:],
                                                  scalar1=-CLIP_HI, scalar2=None,
                                                  op0=ALU.max),
              reads=[("dbuf",)], writes=[("dbuf",)])
        au.op("scalar", lambda e: e.activation(out=dbuf[:], in_=dbuf[:],
                                               func=AF.Exp, scale=-1.0),
              reads=[("dbuf",)], writes=[("dbuf",)])
        au.op("vector", lambda e: e.tensor_tensor(out=B["ktl"][:], in0=kb[:],
                                                  in1=dbuf[:], op=ALU.mult),
              reads=[("dbuf",), ("kb",)], writes=[("ktl",)])
        au.op("vector", lambda e: e.tensor_scalar(out=cbuf[:], in0=cbuf[:],
                                                  scalar1=CLIP_HI, scalar2=None,
                                                  op0=ALU.min),
              reads=[("cbuf",)], writes=[("cbuf",)])
        au.op("vector", lambda e: e.tensor_scalar(out=cbuf[:], in0=cbuf[:],
                                                  scalar1=CLIP_LO, scalar2=None,
                                                  op0=ALU.max),
              reads=[("cbuf",)], writes=[("cbuf",)])
        au.op("scalar", lambda e: e.activation(out=cbuf[:], in_=cbuf[:], func=AF.Exp),
              reads=[("cbuf",)], writes=[("cbuf",)])
        au.op("vector", lambda e: e.tensor_tensor(out=B["kapp"][:], in0=kb[:],
                                                  in1=cbuf[:], op=ALU.mult),
              reads=[("cbuf",), ("kb",)], writes=[("kapp",)])
        au.op("vector", lambda e: e.tensor_tensor(out=B["qkp"][:], in0=qb[:],
                                                  in1=cbuf[:], op=ALU.mult),
              reads=[("cbuf",), ("qb",)], writes=[("qkp",)])
        au.op("scalar", lambda e: e.activation(out=dbuf[:], in_=gbuf[:], func=AF.Exp),
              reads=[("gbuf",)], writes=[("dbuf",)])
        au.op("vector", lambda e: e.tensor_tensor(out=B["qhat"][:], in0=qb[:],
                                                  in1=dbuf[:], op=ALU.mult),
              reads=[("dbuf",), ("qb",)], writes=[("qhat",)])
        au.op("vector", lambda e: e.tensor_tensor(out=B["kap"][:], in0=kb[:],
                                                  in1=dbuf[:], op=ALU.mult),
              reads=[("dbuf",), ("kb",)], writes=[("kap",)])
        au.op("vector", lambda e: e.tensor_copy(
            out=cbuf[:].rearrange("p (c r) -> p c r", r=CHK),
            in_=expand_ap(ge[:, 0:NCH], CHK)),
            reads=[("ge", 0)], writes=[("cbuf",)])
        au.op("vector", lambda e: e.tensor_tensor(out=cbuf[:], in0=cbuf[:],
                                                  in1=fbuf[:], op=ALU.subtract),
              reads=[("cbuf",), ("fbuf",)], writes=[("cbuf",)])
        au.op("scalar", lambda e: e.activation(out=cbuf[:], in_=cbuf[:], func=AF.Exp),
              reads=[("cbuf",)], writes=[("cbuf",)])
        au.op("vector", lambda e: e.tensor_tensor(out=B["khat"][:], in0=kb[:],
                                                  in1=cbuf[:], op=ALU.mult),
              reads=[("cbuf",), ("kb",)], writes=[("khat",)])
        for dd in range(0, NBD + 1):
            if dd == 0:
                au.op("vector", lambda e: e.tensor_tensor(out=cbuf[:], in0=qb[:],
                                                          in1=kb[:], op=ALU.mult),
                      reads=[("qb",), ("kb",)], writes=[("cbuf",)])
            else:
                au.op("vector", lambda e, dd=dd: e.tensor_tensor(
                    out=dbuf[:, dd:T], in0=gbuf[:, dd:T], in1=gbuf[:, 0:T - dd],
                    op=ALU.subtract),
                    reads=[("gbuf",)], writes=[("dbuf",)])
                au.op("vector", lambda e, dd=dd: e.tensor_scalar(
                    out=dbuf[:, dd:T], in0=dbuf[:, dd:T], scalar1=0.0, scalar2=None,
                    op0=ALU.min),
                    reads=[("dbuf",)], writes=[("dbuf",)])
                au.op("scalar", lambda e, dd=dd: e.activation(
                    out=dbuf[:, dd:T], in_=dbuf[:, dd:T], func=AF.Exp),
                    reads=[("dbuf",)], writes=[("dbuf",)])
                au.op("vector", lambda e, dd=dd: e.tensor_tensor(
                    out=fbuf[:, dd:T], in0=dbuf[:, dd:T], in1=kb[:, 0:T - dd],
                    op=ALU.mult),
                    reads=[("dbuf",), ("kb",)], writes=[("fbuf",)])
                au.op("vector", lambda e, dd=dd: e.tensor_tensor(
                    out=cbuf[:, dd:T], in0=fbuf[:, dd:T], in1=qb[:, dd:T],
                    op=ALU.mult),
                    reads=[("fbuf",), ("qb",)], writes=[("cbuf",)])
                au.op("vector", lambda e, dd=dd: e.memset(cbuf[:, 0:dd], 0.0),
                      writes=[("cbuf",)])
                au.op("vector", lambda e, dd=dd: e.tensor_tensor(
                    out=fbuf[:, dd:T], in0=fbuf[:, dd:T], in1=kb[:, dd:T],
                    op=ALU.mult),
                    reads=[("fbuf",), ("kb",)], writes=[("fbuf",)])
                au.op("vector", lambda e, dd=dd: e.memset(fbuf[:, 0:dd], 0.0),
                      writes=[("fbuf",)])
            for which, srcb, srck in (((0, fbuf, "fbuf"), (1, cbuf, "cbuf"))
                                       if dd >= 1 else ((1, cbuf, "cbuf"),)):
                col = (dd - 1) * NG if which == 0 else (NBD + dd) * NG

                def colmms(e, srcb=srcb):
                    ins = None
                    for g in range(NG):
                        ins = e.matmul(pwc[:, g:g + 1],
                                       srcb[:, g * 128:(g + 1) * 128],
                                       ctx.ones_col()[:], start=True, stop=True)
                    return ins
                au.op("tensor", colmms, reads=[(srck,)], writes=[("pwc",)])
                au.op("vector", lambda e, col=col: e.tensor_copy(
                    out=B["wcols"][:, col:col + NG], in_=pwc[:]),
                    reads=[("pwc",)], writes=[("wcols",)])
        au.emit()

    _l0_head_h2(ctx, h, sems, B, masks, attn_dram, name)


def expand_ap(src_ap, rep):
    import bass_rust
    ap = src_ap
    dims = list(ap.ap)
    dims = [list(d) for d in dims] + [[0, rep]]
    return bass.AP(tensor=ap.tensor, offset=ap.offset, ap=dims)


def strided_cols(buf, start, step, count):
    a = buf[:]
    d = [list(x) for x in a.ap]
    cstep = d[1][0]
    return bass.AP(tensor=a.tensor, offset=a.offset + start * cstep,
                   ap=[d[0], [step * cstep, count]])


def _l0_head_h2(ctx, h, sems, B, masks, attn_dram, name):
    nc, T = ctx.nc, ctx.T
    NG = T // 128
    NCH = T // CHK
    n = f"{name}h{h}"
    NBD = BAND
    identf, identb = masks["identf"], masks["identb"]
    ktl, kapp, qkp, kap, qhat, khat, vb, gsig = (
        B["ktl"], B["kapp"], B["qkp"], B["kap"], B["qhat"], B["khat"], B["vb"],
        B["gsig"])
    wcols, btok, ge, Sbf = B["wcols"], B["btok"], B["ge"], B["Sbf"]
    wfarb, wqfarb, Msb, WqTm, khatG = (B["wfarb"], B["wqfarb"], B["Msb"],
                                       B["WqTm"], B["khatG"])
    Mk = [B["M2"], B["M4"], B["M8"], B["M16"]]
    Psb, rt, dsb, obuf, asb2, tbuf2, obq = (B["Psb"], B["rt"], B["dsb"],
                                            B["obuf"], B["asb2"], B["tbuf2"],
                                            B["obq"])
    diag = B["diag"]
    au = Auto(ctx, f"{n}_h2", sems)
    with ExitStack() as psc:
        pW = ctx.psum(f"{n}_pW", [128, 128], F32, psc)
        pWq = ctx.psum(f"{n}_pWq", [128, 128], F32, psc)
        pS = ctx.psum(f"{n}_pS", [128, 128], F32, psc)
        pC1 = ctx.psum(f"{n}_pC1", [128, 128], F32, psc)
        pC2 = ctx.psum(f"{n}_pC2", [128, 128], BF16, psc)
        pU = ctx.psum(f"{n}_pU", [128, 128], F32, psc)
        pO = ctx.psum(f"{n}_pO", [128, 128], F32, psc)
        pOT = ctx.psum(f"{n}_pOT", [128, 128], BF16, psc)

        au.op("vector", lambda e: e.memset(Sbf[:], 0.0), writes=[("Sbf",)])
        for g in range(NG):
            gsl = slice(g * 128, (g + 1) * 128)
            au.op("tensor", lambda e, gsl=gsl: e.matmul(
                pW[:], ktl[:, gsl], kapp[:, gsl], start=True, stop=True),
                reads=[("ktl",), ("kapp",)], writes=[("pW",)])
            au.op("vector", lambda e: e.tensor_tensor(
                out=wfarb[:], in0=pW[:], in1=masks["m_far"][:], op=ALU.mult),
                reads=[("pW",)], writes=[("wfarb",)])
            for dd in range(1, NBD + 1):
                au.op("vector", lambda e, dd=dd, g=g: e.tensor_scalar(
                    out=diag[dd % 2][:], in0=identf[:],
                    scalar1=wcols[:, (dd - 1) * NG + g:(dd - 1) * NG + g + 1],
                    scalar2=None, op0=ALU.mult),
                    reads=[("wcols",)], writes=[("diag", dd % 2)])
                au.op("tensor", lambda e, dd=dd: e.matmul(
                    pW[:], masks[f"shn{dd}"][:], diag[dd % 2][:],
                    start=(dd == 1), stop=(dd == NBD)),
                    reads=[("diag", dd % 2), ("wfarb",)], writes=[("pW",)])
            au.op("vector", lambda e: e.tensor_tensor(
                out=tbuf2[:], in0=wfarb[:], in1=pW[:], op=ALU.add),
                reads=[("wfarb",), ("pW",)], writes=[("tbuf2",)])
            au.op("vector", lambda e, g=g: e.scalar_tensor_tensor(
                out=Msb[:], in0=tbuf2[:], scalar=btok[:, g * 8 + h:g * 8 + h + 1],
                in1=masks["m_su"][:], op0=ALU.mult, op1=ALU.mult),
                reads=[("tbuf2",), ("btok",)], writes=[("Msb",)])
            au.op("tensor", lambda e, gsl=gsl: e.matmul(
                pWq[:], ktl[:, gsl], qkp[:, gsl], start=True, stop=True),
                reads=[("ktl",), ("qkp",)], writes=[("pWq",)])
            au.op("vector", lambda e: e.tensor_tensor(
                out=wqfarb[:], in0=pWq[:], in1=masks["m_farq"][:], op=ALU.mult),
                reads=[("pWq",)], writes=[("wqfarb",)])
            for dd in range(0, NBD + 1):
                au.op("vector", lambda e, dd=dd, g=g: e.tensor_scalar(
                    out=diag[dd % 2][:], in0=identf[:],
                    scalar1=wcols[:, (NBD + dd) * NG + g:(NBD + dd) * NG + g + 1],
                    scalar2=None, op0=ALU.mult),
                    reads=[("wcols",)], writes=[("diag", dd % 2)])
                au.op("tensor", lambda e, dd=dd: e.matmul(
                    pWq[:], masks[f"shn{dd}"][:], diag[dd % 2][:],
                    start=(dd == 0), stop=(dd == NBD)),
                    reads=[("diag", dd % 2), ("wqfarb",)], writes=[("pWq",)])
            au.op("vector", lambda e: e.tensor_tensor(
                out=tbuf2[:], in0=wqfarb[:], in1=pWq[:], op=ALU.add),
                reads=[("wqfarb",), ("pWq",)], writes=[("tbuf2",)])
            au.op("vector", lambda e: e.tensor_tensor(
                out=WqTm[:], in0=tbuf2[:], in1=masks["m_ui"][:], op=ALU.mult),
                reads=[("tbuf2",)], writes=[("WqTm",)])
            au.op("tensor", lambda e, gsl=gsl: e.transpose(
                pC2[:], khat[:, gsl], identb[:]),
                reads=[("khat",)], writes=[("pC2",)])
            au.op("vector", lambda e: e.tensor_copy(out=khatG[:], in_=pC2[:]),
                  reads=[("pC2",)], writes=[("khatG",)])
            prevk = "Msb"
            prevb = Msb
            for i in range(4):
                au.op("tensor", lambda e, prevb=prevb: e.matmul(
                    pS[:], prevb[:], prevb[:], start=True, stop=True),
                    reads=[(prevk,)], writes=[("pS",)])
                au.op("vector", lambda e, i=i: e.tensor_copy(out=Mk[i][:], in_=pS[:]),
                      reads=[("pS",)], writes=[(f"Mk{i}",)])
                prevk, prevb = f"Mk{i}", Mk[i]
            au.op("vector", lambda e: e.tensor_tensor(
                out=Psb[:], in0=identf[:], in1=Msb[:], op=ALU.subtract),
                reads=[("Msb",)], writes=[("Psb",)])
            for i in range(4):
                au.op("tensor", lambda e, i=i: e.matmul(
                    pS[:], Psb[:], Mk[i][:], start=True, stop=True),
                    reads=[("Psb",), (f"Mk{i}",)], writes=[("pS",)])
                au.op("vector", lambda e: e.tensor_tensor(
                    out=Psb[:], in0=Psb[:], in1=pS[:], op=ALU.add),
                    reads=[("Psb",), ("pS",)], writes=[("Psb",)])
            for c in range(4):
                base = 32 * c
                gc = 4 * g + c
                csl = slice(gc * CHK, (gc + 1) * CHK)
                au.op("tensor", lambda e, csl=csl: e.matmul(
                    pC1[0:128, 0:32], Sbf[:], kap[:, csl], start=True, stop=True),
                    reads=[("Sbf",), ("kap",)], writes=[("pC1",)])
                au.op("vector", lambda e, csl=csl: e.tensor_tensor(
                    out=rt[:, 0:32], in0=vb[:, csl], in1=pC1[0:128, 0:32],
                    op=ALU.subtract),
                    reads=[("pC1",), ("vb",)], writes=[("rt",)])
                au.op("tensor", lambda e, base=base: e.transpose(
                    pC2[base:base + 32, :], rt[:, 0:32], identb[:],
                    tile_position=(0, base)),
                    reads=[("rt",)], writes=[("pC2",)])
                au.op("vector", lambda e, base=base: e.tensor_copy(
                    out=B["rtok"][base:base + 32, :], in_=pC2[base:base + 32, :]),
                    reads=[("pC2",)], writes=[("rtok",)])
                au.op("tensor", lambda e, base=base: e.matmul(
                    pC1[base:base + 32, :], Psb[base:base + 32, base:base + 32],
                    B["rtok"][base:base + 32, :], start=True, stop=True,
                    tile_position=(base, base)),
                    reads=[("Psb",), ("rtok",)], writes=[("pC1",)])
                au.op("vector", lambda e, base=base, g=g: e.tensor_scalar(
                    out=dsb[base:base + 32, :], in0=pC1[base:base + 32, :],
                    scalar1=btok[base:base + 32, g * 8 + h:g * 8 + h + 1],
                    scalar2=None, op0=ALU.mult),
                    reads=[("pC1",), ("btok",)], writes=[("dsb",)])
                au.op("tensor", lambda e, base=base, csl=csl: e.matmul(
                    pO[base:base + 32, :], qhat[:, csl], Sbf[:],
                    start=True, stop=False, tile_position=(0, base)),
                    reads=[("qhat",), ("Sbf",)], writes=[("pO", c)])
                au.op("tensor", lambda e, base=base: e.matmul(
                    pU[:], khatG[base:base + 32, :], dsb[base:base + 32, :],
                    start=True, stop=True, tile_position=(base, 0)),
                    reads=[("khatG",), ("dsb",)], writes=[("pU",)])
                au.op("vector", lambda e, gc=gc: e.scalar_tensor_tensor(
                    out=Sbf[:], in0=Sbf[:],
                    scalar=ge[:, 2 * NCH + gc:2 * NCH + gc + 1], in1=pU[:],
                    op0=ALU.mult, op1=ALU.add),
                    reads=[("Sbf",), ("pU",), ("ge", 2)], writes=[("Sbf",)])
                au.op("tensor", lambda e, base=base: e.matmul(
                    pO[base:base + 32, :], WqTm[base:base + 32, base:base + 32],
                    dsb[base:base + 32, :], start=False, stop=True,
                    tile_position=(base, base)),
                    reads=[("WqTm",), ("dsb",)], writes=[("pO", c)])
            PORD = [("pO", c) for c in range(4)]
            au.op("scalar", lambda e: e.activation(
                out=tbuf2[:], in_=pO[:], func=AF.Square,
                accum_out=obq[:, 0:1]),
                reads=PORD, writes=[("obq",), ("tbuf2",)])
            au.op("scalar", lambda e, fc=ctx.fconst(1e-5): e.activation(
                out=obq[:, 1:2], in_=obq[:, 0:1], func=AF.Ln, scale=1.0 / 128.0,
                bias=fc[:, 0:1]),
                reads=[("obq",)], writes=[("obq",)])
            au.op("scalar", lambda e: e.activation(
                out=obq[:, 1:2], in_=obq[:, 1:2], func=AF.Exp, scale=-0.5),
                reads=[("obq",)], writes=[("obq",)])
            au.op("vector", lambda e: e.tensor_scalar(
                out=obuf[:], in0=pO[:], scalar1=obq[:, 1:2], scalar2=None,
                op0=ALU.mult),
                reads=PORD + [("obq",)], writes=[("obuf",)])
            au.op("tensor", lambda e: e.transpose(pOT[:], obuf[:], identb[:]),
                  reads=[("obuf",)], writes=[("pOT",)])
            au.op("vector", lambda e, gsl=gsl: e.scalar_tensor_tensor(
                out=asb2[:], in0=pOT[:], scalar=B["onw"][:, 0:1],
                in1=gsig[:, gsl], op0=ALU.mult, op1=ALU.mult),
                reads=[("pOT",), ("onw",), ("gsig",)], writes=[("asb2",)])
            au.op("sync", lambda e, gsl=gsl: e.dma_start(
                out=attn_dram[h * 128:(h + 1) * 128, gsl], in_=asb2[:]),
                reads=[("asb2",)], dma=True)
        au.emit()



def emit_l1attn(ctx, xn_dram, x_base_dram, x_mid_dram, Wd, name="l1a"):
    nc, T = ctx.nc, ctx.T
    NT = T // 128
    NS = T // 512
    if os.environ.get("L1A_DBG", "0") == "1":
        attn_dram = ctx.dram_out(f"{name}_attn", [D, T], BF16)
    else:
        attn_dram = ctx.dram_scratch(f"{name}_attn", [D, T], BF16)
    ctx.dbg[f"{name}_attn"] = attn_dram
    sems = ctx.auto_sems()
    identb = ctx.ident_bf()
    cosb = Wd["cos_sb"]
    sinb = Wd["sin_sb"]
    rotm = Wd["rot_sb"]

    with ExitStack() as sc:
        xnr = ctx.sbuf(f"{name}_xnr", [128, NDC * T], BF16, sc)
        lnp = ctx.sbuf(f"{name}_lnp", [128, 4], F32, sc)
        wpair = ctx.sbuf(f"{name}_wp", [128, 3 * NDC * 128], BF16, sc)
        q2 = ctx.sbuf(f"{name}_q2", [128, T], BF16, sc)
        k2 = ctx.sbuf(f"{name}_k2", [128, T], BF16, sc)
        v2f = ctx.sbuf(f"{name}_v2f", [128, T], BF16, sc)
        vt = ctx.sbuf(f"{name}_vt", [128, NT * 130], BF16, sc)
        srow = ctx.sbuf(f"{name}_srow", [1, 512 * 4], F32, sc)
        fsc = ctx.sbuf(f"{name}_fsc", [128, 512 * 2], F32, sc)
        pt = ctx.sbuf(f"{name}_pt", [128, 512 * 2], BF16, sc)
        ot = ctx.sbuf(f"{name}_ot", [128, 512], F32, sc)
        at = ctx.sbuf(f"{name}_at", [128, 512 * 2], BF16, sc)
        otr = ctx.sbuf(f"{name}_otr", [1, 512 * 2], F32, sc)

        au0 = Auto(ctx, f"{name}_ld", sems)
        au0.op("sync", lambda e: e.dma_start(out=lnp[:], in_=Wd["lnpar"]),
               writes=[("lnp",)], dma=True)
        for dc in range(NDC):
            au0.op("sync", lambda e, dc=dc: e.dma_start(
                out=xnr[:, dc * T:(dc + 1) * T],
                in_=xn_dram[dc * 128:(dc + 1) * 128, :]),
                writes=[("xnr",)], dma=True)
        au0.emit()

        for hp in range(8):
            _l1a_pair(ctx, hp, sems, Wd, locals(), attn_dram, name)

    _matmul_add_phase(ctx, attn_dram, x_base_dram, x_mid_dram, Wd["w_projT"],
                      Wd["bpc"], f"{name}_wo")


def _l1a_pair(ctx, hp, sems, Wd, L, attn_dram, name):
    nc, T = ctx.nc, ctx.T
    NT = T // 128
    NS = T // 512
    n = f"{name}p{hp}"
    identb = ctx.ident_bf()
    xnr, wpair, q2, k2, v2f, vt = (L["xnr"], L["wpair"], L["q2"], L["k2"],
                                   L["v2f"], L["vt"])
    lnp, srow, fsc, pt, ot, at, otr = (L["lnp"], L["srow"], L["fsc"], L["pt"],
                                       L["ot"], L["at"], L["otr"])
    cosb, sinb, rotm = L["cosb"], L["sinb"], L["rotm"]

    au = Auto(ctx, f"{n}_prep", sems)
    with ExitStack() as psc:
        p1 = ctx.psum(f"{n}_p1", [128, 512], F32, psc)
        p2 = ctx.psum(f"{n}_p2", [128, 512], F32, psc)
        ps0 = ctx.psum(f"{n}_ps0", [1, 512], F32, psc)
        ps1 = ctx.psum(f"{n}_ps1", [1, 512], F32, psc)
        pbc = ctx.psum(f"{n}_pbc", [128, 512], F32, psc)
        ptp = ctx.psum(f"{n}_ptp", [128, 128], BF16, psc)

        for P in range(3):
            au.op("gpsimd", lambda e, P=P: e.dma_start(
                out=wpair[:, P * NDC * 128:(P + 1) * NDC * 128].rearrange(
                    "p (c f) -> p c f", f=128),
                in_=Wd["w_qkvT"][:, P * 1024 + hp * 128: P * 1024 + hp * 128 + 128]
                .rearrange("(c p) f -> p c f", p=128)),
                writes=[("wpair", P)], dma=True)
        for P, dst in enumerate(("q2", "k2", "v2f")):
            dstb = L[dst]
            for s in range(NS):
                ssl = slice(s * 512, (s + 1) * 512)
                pb = (p1, p2)[s % 2]
                pk = ("p1",) if s % 2 == 0 else ("p2",)

                def mmg(e, pb=pb, s=s, P=P):
                    ins = None
                    for dc in range(NDC):
                        ins = e.matmul(
                            pb[:], wpair[:, (P * NDC + dc) * 128:(P * NDC + dc + 1) * 128],
                            xnr[:, dc * T + s * 512: dc * T + s * 512 + 512],
                            start=(dc == 0), stop=(dc == NDC - 1))
                    return ins
                au.op("tensor", mmg, reads=[("xnr",), ("wpair", P)], writes=[pk])
                if P == 2:
                    au.op("scalar", lambda e, ssl=ssl, pb=pb, dstb=dstb: e.activation(
                        out=dstb[:, ssl], in_=pb[:], func=AF.Identity),
                        reads=[pk], writes=[(dst,)])
                    continue
                gcol = lnp[:, 2 * P:2 * P + 1]
                bcol = lnp[:, 2 * P + 1:2 * P + 2]
                for hh in range(2):
                    hsl = slice(hh * 64, (hh + 1) * 64)

                    def statmm(e, pb=pb, hsl=hsl, ssl=ssl, hh=hh):
                        e.matmul(ps0[:], ctx.ones_col()[hsl.start:hsl.start + 64, 0:1],
                                 pb[hsl, :] if False else None, start=True, stop=True) if False else None
                        return None
                    au.op("vector", lambda e, pb=pb, ssl=ssl: e.tensor_copy(
                        out=fsc[:, 0:512], in_=pb[:]),
                        reads=[pk], writes=[("fsc", 0)]) if hh == 0 else None
                    au.op("vector", lambda e, ssl=ssl: e.tensor_tensor(
                        out=fsc[:, 512:1024], in0=fsc[:, 0:512], in1=fsc[:, 0:512],
                        op=ALU.mult),
                        reads=[("fsc", 0)], writes=[("fsc", 1)]) if hh == 0 else None
                    au.op("tensor", lambda e, hsl=hsl: e.matmul(
                        ps0[:], ctx.ones_col()[hsl, 0:1], fsc[hsl, 0:512],
                        start=True, stop=True, tile_position=(hsl.start, 0)),
                        reads=[("fsc", 0)], writes=[("ps0",)])
                    au.op("tensor", lambda e, hsl=hsl: e.matmul(
                        ps1[:], ctx.ones_col()[hsl, 0:1], fsc[hsl, 512:1024],
                        start=True, stop=True, tile_position=(hsl.start, 0)),
                        reads=[("fsc", 1)], writes=[("ps1",)])
                    au.op("vector", lambda e: e.tensor_scalar(
                        out=srow[0:1, 0:512], in0=ps0[0:1, :], scalar1=1.0 / 64,
                        scalar2=None, op0=ALU.mult),
                        reads=[("ps0",)], writes=[("srow", 0)])
                    au.op("vector", lambda e: e.tensor_tensor(
                        out=srow[0:1, 512:1024], in0=srow[0:1, 0:512],
                        in1=srow[0:1, 0:512], op=ALU.mult),
                        reads=[("srow", 0)], writes=[("srow", 1)])
                    au.op("vector", lambda e: e.scalar_tensor_tensor(
                        out=srow[0:1, 1024:1536], in0=ps1[0:1, :], scalar=1.0 / 64,
                        in1=srow[0:1, 512:1024], op0=ALU.mult, op1=ALU.subtract),
                        reads=[("ps1",), ("srow", 1)], writes=[("srow", 2)])
                    au.op("scalar", lambda e, fc=ctx.fconst(1e-6): e.activation(
                        out=srow[0:1, 1536:2048], in_=srow[0:1, 1024:1536],
                        func=AF.Ln, bias=fc[0:1, 0:1]),
                        reads=[("srow", 2)], writes=[("srow", 3)])
                    au.op("scalar", lambda e: e.activation(
                        out=srow[0:1, 1536:2048], in_=srow[0:1, 1536:2048],
                        func=AF.Exp, scale=-0.5),
                        reads=[("srow", 3)], writes=[("srow", 3)])
                    au.op("tensor", lambda e: e.matmul(
                        pbc[:], ctx.ones_row()[:], srow[0:1, 0:512],
                        start=True, stop=True),
                        reads=[("srow", 0)], writes=[("pbc",)])
                    au.op("vector", lambda e, hsl=hsl: e.tensor_tensor(
                        out=fsc[hsl, 512:1024], in0=fsc[hsl, 0:512], in1=pbc[hsl, :],
                        op=ALU.subtract),
                        reads=[("pbc",), ("fsc", 0)], writes=[("fsc", 1)])
                    au.op("tensor", lambda e: e.matmul(
                        pbc[:], ctx.ones_row()[:], srow[0:1, 1536:2048],
                        start=True, stop=True),
                        reads=[("srow", 3), ("fsc", 1)], writes=[("pbc",)])
                    au.op("vector", lambda e, hsl=hsl: e.tensor_tensor(
                        out=fsc[hsl, 512:1024], in0=fsc[hsl, 512:1024], in1=pbc[hsl, :],
                        op=ALU.mult),
                        reads=[("pbc",), ("fsc", 1)], writes=[("fsc", 1)])
                au.op("vector", lambda e, gcol=gcol, bcol=bcol: e.tensor_scalar(
                    out=fsc[:, 512:1024], in0=fsc[:, 512:1024], scalar1=gcol,
                    scalar2=bcol, op0=ALU.mult, op1=ALU.add),
                    reads=[("fsc", 1), ("lnp",)], writes=[("fsc", 1)])
                au.op("vector", lambda e: e.tensor_copy(out=pt[:, 0:512],
                                                        in_=fsc[:, 512:1024]),
                      reads=[("fsc", 1)], writes=[("ptb",)])
                au.op("tensor", lambda e: e.matmul(
                    pbc[:], rotm[:], pt[:, 0:512], start=True, stop=True),
                    reads=[("ptb",)], writes=[("pbc",)])
                au.op("vector", lambda e, ssl=ssl: e.tensor_tensor(
                    out=fsc[:, 0:512], in0=fsc[:, 512:1024], in1=cosb[:, ssl],
                    op=ALU.mult),
                    reads=[("fsc", 1)], writes=[("fsc", 0)])
                au.op("vector", lambda e, ssl=ssl: e.tensor_tensor(
                    out=fsc[:, 512:1024], in0=pbc[:], in1=sinb[:, ssl],
                    op=ALU.mult),
                    reads=[("pbc",)], writes=[("fsc", 1)])
                au.op("vector", lambda e, ssl=ssl, dstb=dstb: e.tensor_tensor(
                    out=dstb[:, ssl], in0=fsc[:, 0:512], in1=fsc[:, 512:1024],
                    op=ALU.add),
                    reads=[("fsc", 0), ("fsc", 1)], writes=[(dst,)])
        au.op("vector", lambda e: e.memset(vt[:], 1.0), writes=[("vt",)])
        for k in range(NT):
            au.op("tensor", lambda e, k=k: e.transpose(
                ptp[:], v2f[:, k * 128:(k + 1) * 128], identb[:]),
                reads=[("v2f",)], writes=[("ptp",)])
            au.op("vector", lambda e, k=k: e.tensor_copy(
                out=vt[:, k * 130: k * 130 + 64], in_=ptp[:, 0:64]),
                reads=[("ptp",)], writes=[("vt",)])
            au.op("vector", lambda e, k=k: e.tensor_copy(
                out=vt[:, k * 130 + 65: k * 130 + 129], in_=ptp[:, 64:128]),
                reads=[("ptp",)], writes=[("vt",)])
        au.emit()

    au = Auto(ctx, f"{n}_flash", sems)
    with ExitStack() as psc:
        pS = [ctx.psum(f"{n}_pS{i}", [128, 512], F32, psc) for i in range(4)]
        pO = [ctx.psum(f"{n}_pO{i}", [128, 512], F32, psc) for i in range(2)]
        pB = ctx.psum(f"{n}_pB", [128, 512], F32, psc)

        for qs in range(NS):
            qsl = slice(qs * 512, (qs + 1) * 512)
            for kt in range(NT):
                for h in range(2):
                    bidx = 2 * (kt % 2) + h
                    au.op("tensor", lambda e, kt=kt, h=h, bidx=bidx, qsl=qsl: e.matmul(
                        pS[bidx][:], k2[h * 64:(h + 1) * 64, kt * 128:(kt + 1) * 128],
                        q2[h * 64:(h + 1) * 64, qsl], start=True, stop=True,
                        tile_position=(h * 64, 0)),
                        reads=[("k2",), ("q2",)], writes=[("pS", bidx)])
                    au.op("scalar", lambda e, bidx=bidx, h=h: e.activation(
                        out=pt[:, h * 512:(h + 1) * 512], in_=pS[bidx][:],
                        func=AF.Exp, scale=0.125),
                        reads=[("pS", bidx)], writes=[("pth", h)])
                    au.op("tensor", lambda e, kt=kt, h=h: e.matmul(
                        pO[h][0:65, :], vt[:, kt * 130 + h * 65: kt * 130 + h * 65 + 65],
                        pt[:, h * 512:(h + 1) * 512], start=(kt == 0),
                        stop=(kt == NT - 1)),
                        reads=[("pth", h), ("vt",)], writes=[("pO", h)])
            for h in range(2):
                au.op("vector", lambda e, h=h: e.reciprocal(
                    out=otr[0:1, h * 512:(h + 1) * 512], in_=pO[h][64:65, :]),
                    reads=[("pO", h)], writes=[("otr", h)])
                au.op("tensor", lambda e, h=h: e.matmul(
                    pB[:], ctx.ones_row()[:], otr[0:1, h * 512:(h + 1) * 512],
                    start=True, stop=True),
                    reads=[("otr", h)], writes=[("pB",)])
                au.op("vector", lambda e, h=h: e.tensor_copy(
                    out=ot[0:64, :], in_=pO[h][0:64, :]),
                    reads=[("pO", h)], writes=[("ot",)])
                au.op("vector", lambda e, h=h: e.tensor_tensor(
                    out=at[0:64, h * 512:(h + 1) * 512], in0=ot[0:64, :],
                    in1=pB[0:64, :], op=ALU.mult),
                    reads=[("ot",), ("pB",)], writes=[("at", h)])
                au.op("sync", lambda e, h=h, qsl=qsl: e.dma_start(
                    out=attn_dram[hp * 128 + h * 64: hp * 128 + h * 64 + 64, qsl],
                    in_=at[0:64, h * 512:(h + 1) * 512]),
                    reads=[("at", h)], dma=True)
        au.emit()




CAP = 512
NEXP = 64
TOPK = 8


def emit_l1ffn(ctx, xn_dram, x_base_dram, x_out_dram, Wd, name="l1f",
               multicore=True):
    T = ctx.T
    ctx.ident_bf()
    ctx.const_sb("identf", f32c(np.eye(128)))
    ctx.const_sb("iota64", f32c(np.tile(np.arange(NEXP), (128, 1))))
    ctx.const_sb("iotacol", f32c(np.arange(128).reshape(128, 1)))
    ctx.fconst(0.0)
    ctx.ones_row()
    sems = ctx.auto_sems()

    xn_tok = ctx.dram_scratch(f"{name}_xntok", [T, D], BF16)
    nrowp = -(-(NEXP * (CAP + 1)) // 128) * 128
    comb1 = ctx.dram_scratch(f"{name}_comb1", [nrowp, 1], F32)
    ybuf = ctx.dram_scratch(f"{name}_ybuf", [NEXP * (CAP + 1), D], BF16)
    flat_d = ctx.dram_scratch(f"{name}_flat", [T, TOPK], I32)
    wk_d = ctx.dram_scratch(f"{name}_wk", [T, TOPK], F32)
    hist_d = ctx.dram_scratch(f"{name}_hist", [1, NEXP], F32)
    allh_d = ctx.dram_scratch(f"{name}_allh", [8, NEXP], F32)
    ctx.dbg[f"{name}_comb1"] = comb1
    ctx.dbg[f"{name}_ybuf"] = ybuf
    ctx.dbg[f"{name}_flat"] = flat_d
    ctx.dbg[f"{name}_wk"] = wk_d

    _l1f_route(ctx, sems, Wd, xn_dram, xn_tok, comb1, flat_d, wk_d,
               hist_d, allh_d, name, multicore)
    _l1f_experts(ctx, sems, Wd, comb1, xn_tok, ybuf, name)
    _l1f_shared(ctx, sems, Wd, xn_dram, x_base_dram, x_out_dram, ybuf,
                flat_d, wk_d, name)


def _l1f_route(ctx, sems, Wd, xn_dram, xn_tok, comb1, flat_d, wk_d,
               hist_d, allh_d, name, multicore):
    nc, T = ctx.nc, ctx.T
    NTT = T // 128
    identb = ctx.ident_bf()
    identf = ctx.consts["identf"]
    iot = ctx.consts["iota64"]
    ioc = ctx.consts["iotacol"]
    au = Auto(ctx, f"{name}_rt", sems)
    with ExitStack() as psc:
        p_lg = ctx.psum(f"{name}_plg", [128, NEXP], F32, psc)
        p_cs = ctx.psum(f"{name}_pcs", [NEXP, 128], F32, psc)
        p_ct = ctx.psum(f"{name}_pct", [128, NEXP], F32, psc)
        p_h = ctx.psum(f"{name}_ph", [1, NEXP], F32, psc)
        p_px = ctx.psum(f"{name}_ppx", [NEXP, 1], F32, psc)
        gwb = ctx.sbuf(f"{name}_gwb", [128, NDC * NEXP], BF16, psc)
        xsl = ctx.sbuf(f"{name}_xsl", [128, NDC * 512], BF16, psc)
        lgw = ctx.sbuf(f"{name}_lgw", [128, NEXP], F32, psc)
        msk = ctx.sbuf(f"{name}_msk", [128, NEXP], F32, psc)
        tmp = ctx.sbuf(f"{name}_tmp", [128, NEXP], F32, psc)
        sel = ctx.sbuf(f"{name}_sel", [128, NEXP], F32, psc)
        mrow = ctx.sbuf(f"{name}_mrow", [128, 8], F32, psc)
        ekb = ctx.sbuf(f"{name}_ekb", [128, NTT * 2 * TOPK], F32, psc)
        selT = ctx.sbuf(f"{name}_selT", [NEXP, T], F32, psc)
        cex = ctx.sbuf(f"{name}_cex", [NEXP, T], F32, psc)
        hr = ctx.sbuf(f"{name}_hr", [NEXP, 1], F32, psc)
        h1 = ctx.sbuf(f"{name}_h1", [1, NEXP], F32, psc)
        alls = ctx.sbuf(f"{name}_alls", [8, NEXP], F32, psc)
        pfms = ctx.sbuf(f"{name}_pfms", [8, 1], F32, psc)
        pxc = ctx.sbuf(f"{name}_pxc", [NEXP, 1], F32, psc)
        cpt = ctx.sbuf(f"{name}_cpt", [128, NEXP], F32, psc)
        tidc = ctx.sbuf(f"{name}_tidc", [128, 1], F32, psc)
        flf = ctx.sbuf(f"{name}_flf", [128, TOPK], F32, psc)
        fli = ctx.sbuf(f"{name}_fli", [128, TOPK], I32, psc)
        ttk = ctx.sbuf(f"{name}_ttk", [128, D], BF16, psc)
        p_tb = ctx.psum(f"{name}_ptb", [128, 128], BF16, psc)
        zf = ctx.sbuf(f"{name}_zf", [128, comb1.shape[0] // 128], F32, psc)

        au.op("gpsimd", lambda e: e.dma_start(
            out=gwb[:].rearrange("p (c f) -> p c f", f=NEXP),
            in_=Wd["gwT"].rearrange("(c p) f -> p c f", p=128)),
            writes=[("gwb",)], dma=True)
        au.op("sync", lambda e: e.dma_start(out=pfms[:], in_=Wd["pfm"]),
              writes=[("pfms",)], dma=True)
        au.op("vector", lambda e: e.memset(zf[:], 0.0), writes=[("zf",)])
        au.op("sync", lambda e: e.dma_start(
            out=comb1.rearrange("(b a) one -> b (a one)", b=128), in_=zf[:]),
            reads=[("zf",)], dma=True)

        for tt in range(NTT):
            s, to = tt // 4, (tt % 4) * 128
            if tt % 4 == 0:
                for dc in range(NDC):
                    au.op("sync", lambda e, s=s, dc=dc: e.dma_start(
                        out=xsl[:, dc * 512:(dc + 1) * 512],
                        in_=xn_dram[dc * 128:(dc + 1) * 128,
                                    s * 512:(s + 1) * 512]),
                        writes=[("xsl", dc)], dma=True)
            for dc in range(NDC):
                au.op("tensor", lambda e, to=to, dc=dc: e.transpose(
                    p_tb[:], xsl[:, dc * 512 + to: dc * 512 + to + 128],
                    identb[:]),
                    reads=[("xsl", dc)], writes=[("p_tb",)])
                au.op("vector", lambda e, dc=dc: e.tensor_copy(
                    out=ttk[:, dc * 128:(dc + 1) * 128], in_=p_tb[:]),
                    reads=[("p_tb",)], writes=[("ttk", dc)])
            au.op("sync", lambda e, tt=tt: e.dma_start(
                out=xn_tok[tt * 128:(tt + 1) * 128, :], in_=ttk[:]),
                reads=[("ttk", dc) for dc in range(NDC)], dma=True)

            def lgmm(e, to=to):
                ins = None
                for dc in range(NDC):
                    ins = e.matmul(p_lg[:],
                                   xsl[:, dc * 512 + to: dc * 512 + to + 128],
                                   gwb[:, dc * NEXP:(dc + 1) * NEXP],
                                   start=(dc == 0), stop=(dc == NDC - 1))
                return ins
            au.op("tensor", lgmm,
                  reads=[("xsl", dc) for dc in range(NDC)] + [("gwb",)],
                  writes=[("p_lg",)])
            au.op("vector", lambda e: e.tensor_reduce(
                out=mrow[:, 0:1], in_=p_lg[:], axis=AX.X, op=ALU.max),
                reads=[("p_lg",)], writes=[("mr", 0)])
            au.op("vector", lambda e: e.tensor_scalar(
                out=mrow[:, 1:2], in0=mrow[:, 0:1], scalar1=-1.0, scalar2=None,
                op0=ALU.mult),
                reads=[("mr", 0)], writes=[("mr", 1)])
            au.op("vector", lambda e: e.tensor_copy(out=lgw[:], in_=p_lg[:]),
                  reads=[("p_lg",)], writes=[("lgw",)])
            au.op("vector", lambda e: e.memset(sel[:], 0.0), writes=[("sel",)])
            for k in range(TOPK):
                au.op("vector", lambda e: e.tensor_reduce(
                    out=mrow[:, 2:3], in_=lgw[:], axis=AX.X, op=ALU.max),
                    reads=[("lgw",)], writes=[("mr", 2)])
                au.op("vector", lambda e: e.tensor_scalar(
                    out=msk[:], in0=lgw[:], scalar1=mrow[:, 2:3], scalar2=None,
                    op0=ALU.is_ge),
                    reads=[("lgw",), ("mr", 2)], writes=[("msk",)])
                au.op("vector", lambda e: e.scalar_tensor_tensor(
                    out=lgw[:], in0=msk[:], scalar=-3e38, in1=lgw[:],
                    op0=ALU.mult, op1=ALU.add),
                    reads=[("msk",), ("lgw",)], writes=[("lgw",)])
                au.op("vector", lambda e: e.tensor_tensor(
                    out=sel[:], in0=sel[:], in1=msk[:], op=ALU.add),
                    reads=[("sel",), ("msk",)], writes=[("sel",)])
                au.op("vector", lambda e: e.tensor_tensor(
                    out=tmp[:], in0=msk[:], in1=iot[:], op=ALU.mult),
                    reads=[("msk",)], writes=[("tmp",)])
                au.op("vector", lambda e, tt=tt, k=k: e.tensor_reduce(
                    out=ekb[:, tt * 16 + k: tt * 16 + k + 1], in_=tmp[:],
                    axis=AX.X, op=ALU.add),
                    reads=[("tmp",)], writes=[("ekb", tt)])
                au.op("vector", lambda e, tt=tt, k=k: e.tensor_copy(
                    out=ekb[:, tt * 16 + 8 + k: tt * 16 + 9 + k],
                    in_=mrow[:, 2:3]),
                    reads=[("mr", 2)], writes=[("ekb", tt)])
            au.op("scalar", lambda e, tt=tt: e.activation(
                out=ekb[:, tt * 16 + 8: tt * 16 + 16],
                in_=ekb[:, tt * 16 + 8: tt * 16 + 16], func=AF.Exp,
                bias=mrow[:, 1:2]),
                reads=[("ekb", tt), ("mr", 1)], writes=[("ekb", tt)])
            au.op("vector", lambda e, tt=tt: e.tensor_reduce(
                out=mrow[:, 3:4], in_=ekb[:, tt * 16 + 8: tt * 16 + 16],
                axis=AX.X, op=ALU.add),
                reads=[("ekb", tt)], writes=[("mr", 3)])
            au.op("vector", lambda e: e.reciprocal(
                out=mrow[:, 4:5], in_=mrow[:, 3:4]),
                reads=[("mr", 3)], writes=[("mr", 4)])
            au.op("vector", lambda e, tt=tt: e.tensor_scalar(
                out=ekb[:, tt * 16 + 8: tt * 16 + 16],
                in0=ekb[:, tt * 16 + 8: tt * 16 + 16],
                scalar1=mrow[:, 4:5], scalar2=None, op0=ALU.mult),
                reads=[("ekb", tt), ("mr", 4)], writes=[("ekb", tt)])
            au.op("sync", lambda e, tt=tt: e.dma_start(
                out=wk_d[tt * 128:(tt + 1) * 128, :],
                in_=ekb[:, tt * 16 + 8: tt * 16 + 16]),
                reads=[("ekb", tt)], dma=True)
            au.op("tensor", lambda e: e.transpose(
                p_cs[:], sel[:, 0:NEXP], identf[:]),
                reads=[("sel",)], writes=[("p_cs",)])
            au.op("vector", lambda e, tt=tt: e.tensor_copy(
                out=selT[:, tt * 128:(tt + 1) * 128], in_=p_cs[:]),
                reads=[("p_cs",)], writes=[("selT",)])

        au.op("vector", lambda e: e.tensor_tensor_scan(
            out=cex[:], data0=selT[:], data1=selT[:], initial=0.0,
            op0=ALU.add, op1=ALU.bypass),
            reads=[("selT",)], writes=[("cex",)])
        au.op("vector", lambda e: e.tensor_copy(
            out=hr[:], in_=cex[:, T - 1:T]),
            reads=[("cex",)], writes=[("hr",)])
        au.op("vector", lambda e: e.scalar_tensor_tensor(
            out=cex[:], in0=selT[:], scalar=-1.0, in1=cex[:], op0=ALU.mult,
            op1=ALU.add),
            reads=[("selT",), ("cex",)], writes=[("cex",)])
        if multicore:
            au.op("tensor", lambda e: e.transpose(
                p_h[:], hr[:], identf[0:NEXP, 0:NEXP]),
                reads=[("hr",)], writes=[("p_h",)])
            au.op("vector", lambda e: e.tensor_copy(
                out=h1[:], in_=p_h[0:1, :]),
                reads=[("p_h",)], writes=[("h1",)])
            au.op("sync", lambda e: e.dma_start(out=hist_d[:], in_=h1[:]),
                  reads=[("h1",)], dma=True, writes=[("histd",)])
            au.op("gpsimd", lambda e: e.collective_compute(
                "AllGather", ALU.bypass, [list(range(8))],
                [hist_d[:]], [allh_d[:]]),
                reads=[("histd",)], writes=[("allhd",)])
            au.op("sync", lambda e: e.dma_start(out=alls[:], in_=allh_d[:]),
                  reads=[("allhd",)], writes=[("alls",)], dma=True)
            au.op("tensor", lambda e: e.matmul(
                p_h[:], pfms[:], alls[:], start=True, stop=True),
                reads=[("alls",), ("pfms",)], writes=[("p_h",)])
            au.op("vector", lambda e: e.tensor_copy(
                out=h1[:], in_=p_h[0:1, :]),
                reads=[("p_h",)], writes=[("h1",)])
            au.op("tensor", lambda e: e.transpose(
                p_px[:], h1[:], identf[0:1, 0:1]),
                reads=[("h1",)], writes=[("p_px",)])
            au.op("vector", lambda e: e.tensor_copy(
                out=pxc[:], in_=p_px[:, 0:1]),
                reads=[("p_px",)], writes=[("pxc",)])
            au.op("vector", lambda e: e.tensor_scalar(
                out=cex[:], in0=cex[:], scalar1=pxc[:, 0:1], scalar2=None,
                op0=ALU.add),
                reads=[("cex",), ("pxc",)], writes=[("cex",)])

        for tt in range(NTT):
            au.op("tensor", lambda e, tt=tt: e.transpose(
                p_ct[:], cex[:, tt * 128:(tt + 1) * 128],
                identf[0:NEXP, 0:NEXP]),
                reads=[("cex",)], writes=[("p_ct",)])
            au.op("vector", lambda e: e.tensor_copy(out=cpt[:], in_=p_ct[:]),
                  reads=[("p_ct",)], writes=[("cpt",)])
            au.op("vector", lambda e, tt=tt: e.tensor_scalar(
                out=tidc[:], in0=ioc[:], scalar1=float(tt * 128), scalar2=None,
                op0=ALU.add),
                writes=[("tidc",)])
            for k in range(TOPK):
                au.op("vector", lambda e, tt=tt, k=k: e.tensor_scalar(
                    out=msk[:], in0=iot[:],
                    scalar1=ekb[:, tt * 16 + k: tt * 16 + k + 1], scalar2=None,
                    op0=ALU.is_equal),
                    reads=[("ekb", tt)], writes=[("msk",)])
                au.op("vector", lambda e: e.tensor_tensor(
                    out=tmp[:], in0=msk[:], in1=cpt[:], op=ALU.mult),
                    reads=[("msk",), ("cpt",)], writes=[("tmp",)])
                au.op("vector", lambda e: e.tensor_reduce(
                    out=mrow[:, 5:6], in_=tmp[:], axis=AX.X, op=ALU.add),
                    reads=[("tmp",)], writes=[("mr", 5)])
                au.op("vector", lambda e: e.tensor_scalar(
                    out=mrow[:, 6:7], in0=mrow[:, 5:6], scalar1=float(CAP),
                    scalar2=None, op0=ALU.min),
                    reads=[("mr", 5)], writes=[("mr", 6)])
                au.op("vector", lambda e, tt=tt, k=k: e.scalar_tensor_tensor(
                    out=flf[:, k:k + 1], in0=ekb[:, tt * 16 + k: tt * 16 + k + 1],
                    scalar=float(CAP + 1), in1=mrow[:, 6:7],
                    op0=ALU.mult, op1=ALU.add),
                    reads=[("ekb", tt), ("mr", 6)], writes=[("flf", k)])
                au.op("vector", lambda e, k=k: e.tensor_copy(
                    out=fli[:, k:k + 1], in_=flf[:, k:k + 1]),
                    reads=[("flf", k)], writes=[("fli", k)])
                au.op("gpsimd", lambda e, k=k: e.indirect_dma_start(
                    out=comb1[:],
                    out_offset=IndirectOffsetOnAxis(ap=fli[:, k:k + 1], axis=0),
                    in_=tidc[:], in_offset=None),
                    reads=[("fli", k), ("tidc",)], dma=True)
            au.op("sync", lambda e, tt=tt: e.dma_start(
                out=flat_d[tt * 128:(tt + 1) * 128, :], in_=fli[:]),
                reads=[("fli", k) for k in range(TOPK)], dma=True)
        au.emit()


def _l1f_experts(ctx, sems, Wd, comb1, xn_tok, ybuf, name):
    nc, T = ctx.nc, ctx.T
    identb = ctx.ident_bf()
    au = Auto(ctx, f"{name}_ex", sems)
    with ExitStack() as psc:
        p_mm = [ctx.psum(f"{name}_pmm{i}", [128, CAP], F32, psc)
                for i in range(2)]
        p_tb = [ctx.psum(f"{name}_ptb2{i}", [128, 128], BF16, psc)
                for i in range(2)]
        gwe = ctx.sbuf(f"{name}_gwe", [128, 2 * NDC * 1024], BF16, psc)
        dwe = ctx.sbuf(f"{name}_dwe", [128, 2 * 4 * 1024], BF16, psc)
        xfm = ctx.sbuf(f"{name}_xfm", [128, 2 * NDC * CAP], BF16, psc)
        gtk = ctx.sbuf(f"{name}_gtk", [128, 2 * D], BF16, psc)
        cm1 = ctx.sbuf(f"{name}_cm1", [128, 2], F32, psc)
        cmi = ctx.sbuf(f"{name}_cmi", [128, 2], I32, psc)
        u1t = ctx.sbuf(f"{name}_u1t", [128, 4 * CAP], BF16, psc)
        u2t = ctx.sbuf(f"{name}_u2t", [128, 4 * CAP], BF16, psc)
        hfm = ctx.sbuf(f"{name}_hfm", [128, 4 * CAP], BF16, psc)
        yfm = ctx.sbuf(f"{name}_yfm", [128, NDC * CAP], BF16, psc)
        ytk = ctx.sbuf(f"{name}_ytk", [128, 2 * D], BF16, psc)
        ztb = ctx.sbuf(f"{name}_ztb", [128, D], BF16, psc)

        au.op("vector", lambda e: e.memset(ztb[:], 0.0), writes=[("ztb",)])
        au.op("sync", lambda e: e.dma_start(
            out=ybuf.rearrange("(ne s) d -> ne (s d)", s=CAP + 1)
                [:, CAP * D:(CAP + 1) * D],
            in_=ztb[0:NEXP, :]),
            reads=[("ztb",)], dma=True)

        for ee in range(NEXP):
            b = ee % 2
            au.op("gpsimd", lambda e, ee=ee, b=b: e.dma_start(
                out=gwe[:, b * NDC * 1024:(b + 1) * NDC * 1024]
                    .rearrange("p (c f) -> p c f", f=1024),
                in_=Wd["eguT"][ee * 1024:(ee + 1) * 1024, :]
                    .rearrange("(c p) f -> p c f", p=128)),
                writes=[("gwe", b)], dma=True)
            au.op("gpsimd", lambda e, ee=ee, b=b: e.dma_start(
                out=dwe[:, b * 4 * 1024:(b + 1) * 4 * 1024]
                    .rearrange("p (c f) -> p c f", f=1024),
                in_=Wd["edwT"][ee * 512:(ee + 1) * 512, :]
                    .rearrange("(c p) f -> p c f", p=128)),
                writes=[("dwe", b)], dma=True)
            for ti in range(CAP // 128):
                tb = ti % 2
                au.op("sync", lambda e, ee=ee, ti=ti, tb=tb: e.dma_start(
                    out=cm1[:, tb:tb + 1],
                    in_=comb1[ee * (CAP + 1) + ti * 128:
                              ee * (CAP + 1) + ti * 128 + 128, :]),
                    writes=[("cm1", tb)], dma=True)
                au.op("vector", lambda e, tb=tb: e.tensor_copy(
                    out=cmi[:, tb:tb + 1], in_=cm1[:, tb:tb + 1]),
                    reads=[("cm1", tb)], writes=[("cmi", tb)])
                au.op("gpsimd", lambda e, tb=tb: e.indirect_dma_start(
                    out=gtk[:, tb * D:(tb + 1) * D], out_offset=None,
                    in_=xn_tok[:],
                    in_offset=IndirectOffsetOnAxis(ap=cmi[:, tb:tb + 1],
                                                   axis=0)),
                    reads=[("cmi", tb)], writes=[("gtk", tb)], dma=True)
                for dc in range(NDC):
                    pt = p_tb[dc % 2]
                    au.op("tensor", lambda e, dc=dc, tb=tb, pt=pt: e.transpose(
                        pt[:], gtk[:, tb * D + dc * 128: tb * D + (dc + 1) * 128],
                        identb[:]),
                        reads=[("gtk", tb)], writes=[("p_tb", dc % 2)])
                    au.op("vector", lambda e, b=b, dc=dc, ti=ti, pt=pt:
                          e.tensor_copy(
                        out=xfm[:, b * NDC * CAP + dc * CAP + ti * 128:
                                b * NDC * CAP + dc * CAP + ti * 128 + 128],
                        in_=pt[:]),
                        reads=[("p_tb", dc % 2)], writes=[("xfm", b, dc)])
            for fc in range(NDC):
                pm = p_mm[fc % 2]

                def gumm(e, b=b, fc=fc, pm=pm):
                    ins = None
                    for dc in range(NDC):
                        ins = e.matmul(
                            pm[:],
                            gwe[:, b * NDC * 1024 + dc * 1024 + fc * 128:
                                b * NDC * 1024 + dc * 1024 + fc * 128 + 128],
                            xfm[:, b * NDC * CAP + dc * CAP:
                                b * NDC * CAP + (dc + 1) * CAP],
                            start=(dc == 0), stop=(dc == NDC - 1))
                    return ins
                au.op("tensor", gumm,
                      reads=[("xfm", b, dc) for dc in range(NDC)]
                      + [("gwe", b)],
                      writes=[("p_mm", fc % 2)])
                if fc < 4:
                    au.op("scalar", lambda e, fc=fc, pm=pm: e.activation(
                        out=u1t[:, fc * CAP:(fc + 1) * CAP], in_=pm[:],
                        func=AF.Silu),
                        reads=[("p_mm", fc % 2)], writes=[("u1t", fc)])
                else:
                    au.op("scalar", lambda e, fc=fc, pm=pm: e.activation(
                        out=u2t[:, (fc - 4) * CAP:(fc - 3) * CAP], in_=pm[:],
                        func=AF.Copy),
                        reads=[("p_mm", fc % 2)], writes=[("u2t", fc - 4)])
            for hc in range(4):
                au.op("vector", lambda e, hc=hc: e.tensor_tensor(
                    out=hfm[:, hc * CAP:(hc + 1) * CAP],
                    in0=u1t[:, hc * CAP:(hc + 1) * CAP],
                    in1=u2t[:, hc * CAP:(hc + 1) * CAP], op=ALU.mult),
                    reads=[("u1t", hc), ("u2t", hc)], writes=[("hfm", hc)])
            for oc in range(NDC):
                pm = p_mm[oc % 2]

                def dnmm(e, b=b, oc=oc, pm=pm):
                    ins = None
                    for hc in range(4):
                        ins = e.matmul(
                            pm[:],
                            dwe[:, b * 4 * 1024 + hc * 1024 + oc * 128:
                                b * 4 * 1024 + hc * 1024 + oc * 128 + 128],
                            hfm[:, hc * CAP:(hc + 1) * CAP],
                            start=(hc == 0), stop=(hc == 3))
                    return ins
                au.op("tensor", dnmm,
                      reads=[("hfm", hc) for hc in range(4)] + [("dwe", b)],
                      writes=[("p_mm", oc % 2)])
                au.op("scalar", lambda e, oc=oc, pm=pm: e.activation(
                    out=yfm[:, oc * CAP:(oc + 1) * CAP], in_=pm[:],
                    func=AF.Copy),
                    reads=[("p_mm", oc % 2)], writes=[("yfm", oc)])
            for ti in range(CAP // 128):
                tb = ti % 2
                for oc in range(NDC):
                    pt = p_tb[oc % 2]
                    au.op("tensor", lambda e, oc=oc, ti=ti, pt=pt: e.transpose(
                        pt[:], yfm[:, oc * CAP + ti * 128:
                                   oc * CAP + ti * 128 + 128], identb[:]),
                        reads=[("yfm", oc)], writes=[("p_tb", oc % 2)])
                    au.op("vector", lambda e, oc=oc, tb=tb, pt=pt:
                          e.tensor_copy(
                        out=ytk[:, tb * D + oc * 128: tb * D + (oc + 1) * 128],
                        in_=pt[:]),
                        reads=[("p_tb", oc % 2)], writes=[("ytk", tb, oc)])
                au.op("sync", lambda e, ee=ee, ti=ti, tb=tb: e.dma_start(
                    out=ybuf[ee * (CAP + 1) + ti * 128:
                             ee * (CAP + 1) + ti * 128 + 128, :],
                    in_=ytk[:, tb * D:(tb + 1) * D]),
                    reads=[("ytk", tb, oc) for oc in range(NDC)], dma=True)
        au.emit()


def _l1f_shared(ctx, sems, Wd, xn_dram, x_base_dram, x_out_dram, ybuf,
                flat_d, wk_d, name):
    nc, T = ctx.nc, ctx.T
    NS = T // 512
    identf = ctx.consts["identf"]
    onesr = ctx.ones_row()
    au = Auto(ctx, f"{name}_sh", sems)
    with ExitStack() as psc:
        p_mm = [ctx.psum(f"{name}_psm{i}", [128, 512], F32, psc)
                for i in range(2)]
        p_g = ctx.psum(f"{name}_pg", [1, 512], F32, psc)
        p_b = ctx.psum(f"{name}_pb", [128, 512], F32, psc)
        p_cf = ctx.psum(f"{name}_pcf", [128, 128], F32, psc)
        sgub = ctx.sbuf(f"{name}_sgub", [128, NDC * 4096], BF16, psc)
        sdwb = ctx.sbuf(f"{name}_sdwb", [128, 16 * 1024], BF16, psc)
        sgwb = ctx.sbuf(f"{name}_sgwb", [128, NDC], BF16, psc)
        xslb = ctx.sbuf(f"{name}_xslb", [128, NDC * 512], BF16, psc)
        su1 = ctx.sbuf(f"{name}_su1", [128, 16 * 512], BF16, psc)
        u2tr = ctx.sbuf(f"{name}_u2tr", [128, 512], BF16, psc)
        sh = ctx.sbuf(f"{name}_sh", [128, 16 * 512], BF16, psc)
        moefm = ctx.sbuf(f"{name}_moefm", [128, NDC * 512], F32, psc)
        acc = ctx.sbuf(f"{name}_acc", [128, D], F32, psc)
        tm2 = ctx.sbuf(f"{name}_tm2", [128, D], F32, psc)
        ytk2 = ctx.sbuf(f"{name}_ytk2", [128, D], BF16, psc)
        wkt = ctx.sbuf(f"{name}_wkt", [128, TOPK], F32, psc)
        flt = ctx.sbuf(f"{name}_flt", [128, TOPK], I32, psc)
        gro = ctx.sbuf(f"{name}_gro", [1, 512], F32, psc)
        gbb = ctx.sbuf(f"{name}_gbb", [128, 512], F32, psc)
        xbt = ctx.sbuf(f"{name}_xbt", [128, 2 * 512], F32, psc)
        ot = ctx.sbuf(f"{name}_ot", [128, 2 * 512], F32, psc)

        au.op("gpsimd", lambda e: e.dma_start(
            out=sgub[:].rearrange("p (c f) -> p c f", f=4096),
            in_=Wd["sguT"].rearrange("(c p) f -> p c f", p=128)),
            writes=[("sgub",)], dma=True)
        au.op("gpsimd", lambda e: e.dma_start(
            out=sdwb[:].rearrange("p (c f) -> p c f", f=1024),
            in_=Wd["sdwT"].rearrange("(c p) f -> p c f", p=128)),
            writes=[("sdwb",)], dma=True)
        au.op("gpsimd", lambda e: e.dma_start(out=sgwb[:], in_=Wd["sgwc"]),
              writes=[("sgwb",)], dma=True)

        for s in range(NS):
            for dc in range(NDC):
                au.op("sync", lambda e, s=s, dc=dc: e.dma_start(
                    out=xslb[:, dc * 512:(dc + 1) * 512],
                    in_=xn_dram[dc * 128:(dc + 1) * 128,
                                s * 512:(s + 1) * 512]),
                    writes=[("xslb", dc)], dma=True)

            def gmm(e):
                ins = None
                for dc in range(NDC):
                    ins = e.matmul(p_g[:], sgwb[:, dc:dc + 1],
                                   xslb[:, dc * 512:(dc + 1) * 512],
                                   start=(dc == 0), stop=(dc == NDC - 1))
                return ins
            au.op("tensor", gmm,
                  reads=[("xslb", dc) for dc in range(NDC)] + [("sgwb",)],
                  writes=[("p_g",)])
            au.op("scalar", lambda e: e.activation(
                out=gro[:], in_=p_g[:], func=AF.Sigmoid),
                reads=[("p_g",)], writes=[("gro",)])
            au.op("tensor", lambda e: e.matmul(
                p_b[:], onesr[0:1, 0:128], gro[:], start=True, stop=True),
                reads=[("gro",)], writes=[("p_b",)])
            au.op("vector", lambda e: e.tensor_copy(out=gbb[:], in_=p_b[:]),
                  reads=[("p_b",)], writes=[("gbb",)])

            for ic in range(16):
                pm = p_mm[ic % 2]

                def umm(e, ic=ic, pm=pm):
                    ins = None
                    for dc in range(NDC):
                        ins = e.matmul(
                            pm[:],
                            sgub[:, dc * 4096 + ic * 128:
                                 dc * 4096 + ic * 128 + 128],
                            xslb[:, dc * 512:(dc + 1) * 512],
                            start=(dc == 0), stop=(dc == NDC - 1))
                    return ins
                au.op("tensor", umm,
                      reads=[("xslb", dc) for dc in range(NDC)] + [("sgub",)],
                      writes=[("p_mm", ic % 2)])
                au.op("scalar", lambda e, ic=ic, pm=pm: e.activation(
                    out=su1[:, ic * 512:(ic + 1) * 512], in_=pm[:],
                    func=AF.Silu),
                    reads=[("p_mm", ic % 2)], writes=[("su1", ic)])
            for ic in range(16):
                pm = p_mm[ic % 2]

                def umm2(e, ic=ic, pm=pm):
                    ins = None
                    for dc in range(NDC):
                        ins = e.matmul(
                            pm[:],
                            sgub[:, dc * 4096 + (16 + ic) * 128:
                                 dc * 4096 + (16 + ic) * 128 + 128],
                            xslb[:, dc * 512:(dc + 1) * 512],
                            start=(dc == 0), stop=(dc == NDC - 1))
                    return ins
                au.op("tensor", umm2,
                      reads=[("xslb", dc) for dc in range(NDC)] + [("sgub",)],
                      writes=[("p_mm", ic % 2)])
                au.op("scalar", lambda e, ic=ic, pm=pm: e.activation(
                    out=u2tr[:], in_=pm[:], func=AF.Copy),
                    reads=[("p_mm", ic % 2)], writes=[("u2tr",)])
                au.op("vector", lambda e, ic=ic: e.tensor_tensor(
                    out=sh[:, ic * 512:(ic + 1) * 512],
                    in0=su1[:, ic * 512:(ic + 1) * 512], in1=u2tr[:],
                    op=ALU.mult),
                    reads=[("su1", ic), ("u2tr",)], writes=[("sh", ic)])

            for j in range(4):
                tt = s * 4 + j
                au.op("sync", lambda e, tt=tt: e.dma_start(
                    out=wkt[:], in_=wk_d[tt * 128:(tt + 1) * 128, :]),
                    writes=[("wkt",)], dma=True)
                au.op("sync", lambda e, tt=tt: e.dma_start(
                    out=flt[:], in_=flat_d[tt * 128:(tt + 1) * 128, :]),
                    writes=[("flt",)], dma=True)
                for k in range(TOPK):
                    au.op("gpsimd", lambda e, k=k: e.indirect_dma_start(
                        out=ytk2[:], out_offset=None, in_=ybuf[:],
                        in_offset=IndirectOffsetOnAxis(ap=flt[:, k:k + 1],
                                                       axis=0)),
                        reads=[("flt",)], writes=[("ytk2",)], dma=True)
                    if k == 0:
                        au.op("vector", lambda e, k=k: e.tensor_scalar(
                            out=acc[:], in0=ytk2[:], scalar1=wkt[:, k:k + 1],
                            scalar2=None, op0=ALU.mult),
                            reads=[("ytk2",), ("wkt",)], writes=[("acc",)])
                    else:
                        au.op("vector", lambda e, k=k: e.tensor_scalar(
                            out=tm2[:], in0=ytk2[:], scalar1=wkt[:, k:k + 1],
                            scalar2=None, op0=ALU.mult),
                            reads=[("ytk2",), ("wkt",)], writes=[("tm2",)])
                        au.op("vector", lambda e: e.tensor_tensor(
                            out=acc[:], in0=acc[:], in1=tm2[:], op=ALU.add),
                            reads=[("acc",), ("tm2",)], writes=[("acc",)])
                for dc in range(NDC):
                    au.op("tensor", lambda e, dc=dc: e.transpose(
                        p_cf[:], acc[:, dc * 128:(dc + 1) * 128], identf[:]),
                        reads=[("acc",)], writes=[("p_cf",)])
                    au.op("vector", lambda e, dc=dc, j=j: e.tensor_copy(
                        out=moefm[:, dc * 512 + j * 128:
                                  dc * 512 + j * 128 + 128],
                        in_=p_cf[:]),
                        reads=[("p_cf",)], writes=[("moefm", dc, j)])

            for dc in range(NDC):
                pm = p_mm[dc % 2]
                pb = dc % 2

                def dmm(e, dc=dc, pm=pm):
                    ins = None
                    for hc in range(16):
                        ins = e.matmul(
                            pm[:],
                            sdwb[:, hc * 1024 + dc * 128:
                                 hc * 1024 + dc * 128 + 128],
                            sh[:, hc * 512:(hc + 1) * 512],
                            start=(hc == 0), stop=(hc == 15))
                    return ins
                au.op("tensor", dmm,
                      reads=[("sh", hc) for hc in range(16)] + [("sdwb",)],
                      writes=[("p_mm", dc % 2)])
                au.op("sync", lambda e, s=s, dc=dc, pb=pb: e.dma_start(
                    out=xbt[:, pb * 512:(pb + 1) * 512],
                    in_=x_base_dram[dc * 128:(dc + 1) * 128,
                                    s * 512:(s + 1) * 512]),
                    writes=[("xbt", pb)], dma=True)
                au.op("vector", lambda e, dc=dc, pm=pm, pb=pb: e.tensor_tensor(
                    out=ot[:, pb * 512:(pb + 1) * 512], in0=pm[:], in1=gbb[:],
                    op=ALU.mult),
                    reads=[("p_mm", dc % 2), ("gbb",)], writes=[("ot", pb)])
                au.op("vector", lambda e, dc=dc, pb=pb: e.tensor_tensor(
                    out=ot[:, pb * 512:(pb + 1) * 512],
                    in0=ot[:, pb * 512:(pb + 1) * 512],
                    in1=moefm[:, dc * 512:(dc + 1) * 512], op=ALU.add),
                    reads=[("ot", pb)] + [("moefm", dc, j) for j in range(4)],
                    writes=[("ot", pb)])
                au.op("vector", lambda e, pb=pb: e.tensor_tensor(
                    out=ot[:, pb * 512:(pb + 1) * 512],
                    in0=ot[:, pb * 512:(pb + 1) * 512],
                    in1=xbt[:, pb * 512:(pb + 1) * 512], op=ALU.add),
                    reads=[("ot", pb), ("xbt", pb)], writes=[("ot", pb)])
                au.op("sync", lambda e, s=s, dc=dc, pb=pb: e.dma_start(
                    out=x_out_dram[dc * 128:(dc + 1) * 128,
                                   s * 512:(s + 1) * 512],
                    in_=ot[:, pb * 512:(pb + 1) * 512]),
                    reads=[("ot", pb)], dma=True)
        au.emit()



def emit_l3attn(ctx, xn_dram, x_mid_dram, x_out_dram, Wd, name="l3a"):
    nc, T = ctx.nc, ctx.T
    NH, DH = 16, 64
    sems = ctx.auto_sems()
    ctx.ident_bf()
    ctx.const_sb("identf", f32c(np.eye(128)))
    ctx.ones_col(); ctx.ones_col_bf(); ctx.ones_row()
    ctx.fconst(0.0); ctx.fconst(1e-6)
    ctx.fconst(float(np.log(1.0 / np.sqrt(DH))))
    ctx.fconst(float(T))
    invf = 1.0 / 10000.0 ** (np.arange(0, DH, 2, dtype=np.float64) / DH)
    pfree = invf[((np.arange(128) % DH) // 2)]
    ang = np.arange(T, dtype=np.float64)[None, :] * pfree[:, None]
    ctx.const_sb("l3a_cos", bfc(np.cos(ang)))
    ctx.const_sb("l3a_sin", bfc(np.sin(ang)))
    rr = np.zeros((128, 128))
    for i in range(64):
        rr[2 * i + 1, 2 * i] = -1.0
        rr[2 * i, 2 * i + 1] = 1.0
    ctx.const_sb("l3a_rot", bfc(rr))
    tid = np.arange(T, dtype=np.float64).reshape(T // 128, 128).T
    ctx.const_sb("l3a_tid", f32c(tid))

    qkr_d = ctx.dram_scratch(f"{name}_qkr", [D, T], BF16)
    qv_d = ctx.dram_scratch(f"{name}_qv", [NH * T, 128], BF16)
    sidx_d = ctx.dram_scratch(f"{name}_sidx", [NH * 2 * T, 1], F32)
    olin_d = (ctx.dram_scratch(f"{name}_olin0", [T, 65], F32),
              ctx.dram_scratch(f"{name}_olin1", [T, 65], F32))
    afm_d = ctx.dram_scratch(f"{name}_afm", [D, T], BF16)
    ctx.dbg[f"{name}_qkr"] = qkr_d
    ctx.dbg[f"{name}_olin0"] = olin_d[0]
    ctx.dbg[f"{name}_olin1"] = olin_d[1]
    ctx.dbg[f"{name}_qv"] = qv_d
    ctx.dbg[f"{name}_sidx"] = sidx_d
    ctx.dbg[f"{name}_afm"] = afm_d

    _l3a_proj(ctx, sems, Wd, xn_dram, qkr_d, qv_d, name)
    nh_lim = int(os.environ.get("L3A_NH", str(NH)))
    with ExitStack() as psc:
        hb = _l3a_head_bufs(ctx, psc, name)
        for h in range(nh_lim):
            _l3a_head(ctx, sems, Wd, hb, qkr_d, qv_d, sidx_d, olin_d,
                      afm_d, h, name)
    if os.environ.get("L3A_SIDX_OUT"):
        sidxo = ctx.dram_out(f"{name}_sidxo", [NH * 2 * T, 1], F32)
        au2 = Auto(ctx, f"{name}_sxc", sems)
        scp = ctx.sbuf(f"{name}_scp", [128, 1024], F32)
        nrow = NH * 2 * T
        for i in range(nrow // (128 * 1024)):
            au2.op("sync", lambda e, i=i: e.dma_start(
                out=scp[:],
                in_=sidx_d.rearrange("(a b) one -> a (b one)", b=1024)
                    [i * 128:(i + 1) * 128, :]),
                writes=[("scp",)], dma=True)
            au2.op("sync", lambda e, i=i: e.dma_start(
                out=sidxo.rearrange("(a b) one -> a (b one)", b=1024)
                    [i * 128:(i + 1) * 128, :],
                in_=scp[:]),
                reads=[("scp",)], dma=True)
        au2.emit()
    if nh_lim == NH:
        _l3a_final(ctx, sems, x_mid_dram, afm_d, x_out_dram, name)


def _l3a_proj(ctx, sems, Wd, xn_dram, qkr_d, qv_d, name):
    nc, T = ctx.nc, ctx.T
    NS = T // 512
    identb = ctx.ident_bf()
    cos = ctx.consts["l3a_cos"]
    sin = ctx.consts["l3a_sin"]
    rotc = ctx.consts["l3a_rot"]
    au = Auto(ctx, f"{name}_pj", sems)
    with ExitStack() as psc:
        p_q = ctx.psum(f"{name}_pq", [128, 512], F32, psc)
        p_v = ctx.psum(f"{name}_pv", [128, 512], F32, psc)
        p_r = ctx.psum(f"{name}_pr", [128, 512], F32, psc)
        p_tb = ctx.psum(f"{name}_ptb", [128, 128], BF16, psc)
        wqk = ctx.sbuf(f"{name}_wqk", [128, NDC * 1024], BF16, psc)
        wv = ctx.sbuf(f"{name}_wv", [128, NDC * 1024], BF16, psc)
        xsl = ctx.sbuf(f"{name}_xsl", [128, NDC * 512], BF16, psc)
        uq = ctx.sbuf(f"{name}_uq", [128, 512], BF16, psc)
        uv = ctx.sbuf(f"{name}_uv", [128, 512], BF16, psc)
        ur = ctx.sbuf(f"{name}_ur", [128, 512], BF16, psc)
        tcs = ctx.sbuf(f"{name}_tcs", [128, 512], BF16, psc)
        tsn = ctx.sbuf(f"{name}_tsn", [128, 512], BF16, psc)
        qks = ctx.sbuf(f"{name}_qks", [128, 512], BF16, psc)
        stg = ctx.sbuf(f"{name}_stg", [128, 4 * 2048], BF16, psc)

        au.op("gpsimd", lambda e: e.dma_start(
            out=wqk[:].rearrange("p (c f) -> p c f", f=1024),
            in_=Wd["wqkT"].rearrange("(c p) f -> p c f", p=128)),
            writes=[("wqk",)], dma=True)
        au.op("gpsimd", lambda e: e.dma_start(
            out=wv[:].rearrange("p (c f) -> p c f", f=1024),
            in_=Wd["wvT"].rearrange("(c p) f -> p c f", p=128)),
            writes=[("wv",)], dma=True)

        for s in range(NS):
            for dc in range(NDC):
                au.op("sync", lambda e, s=s, dc=dc: e.dma_start(
                    out=xsl[:, dc * 512:(dc + 1) * 512],
                    in_=xn_dram[dc * 128:(dc + 1) * 128,
                                s * 512:(s + 1) * 512]),
                    writes=[("xsl", dc)], dma=True)
            for fc in range(NDC):
                def qmm(e, fc=fc):
                    ins = None
                    for dc in range(NDC):
                        ins = e.matmul(
                            p_q[:],
                            wqk[:, dc * 1024 + fc * 128:
                                dc * 1024 + fc * 128 + 128],
                            xsl[:, dc * 512:(dc + 1) * 512],
                            start=(dc == 0), stop=(dc == NDC - 1))
                    return ins
                au.op("tensor", qmm,
                      reads=[("xsl", dc) for dc in range(NDC)] + [("wqk",)],
                      writes=[("p_q",)])
                au.op("scalar", lambda e: e.activation(
                    out=uq[:], in_=p_q[:], func=AF.Copy),
                    reads=[("p_q",)], writes=[("uq",)])
                au.op("tensor", lambda e: e.matmul(
                    p_r[:], rotc[:], uq[:], start=True, stop=True),
                    reads=[("uq",)], writes=[("p_r",)])
                au.op("vector", lambda e, s=s: e.tensor_tensor(
                    out=tcs[:], in0=uq[:],
                    in1=cos[:, s * 512:(s + 1) * 512], op=ALU.mult),
                    reads=[("uq",)], writes=[("tcs",)])
                au.op("scalar", lambda e: e.activation(
                    out=ur[:], in_=p_r[:], func=AF.Copy),
                    reads=[("p_r",)], writes=[("ur",)])
                au.op("vector", lambda e, s=s: e.tensor_tensor(
                    out=tsn[:], in0=ur[:],
                    in1=sin[:, s * 512:(s + 1) * 512], op=ALU.mult),
                    reads=[("ur",)], writes=[("tsn",)])
                au.op("vector", lambda e: e.tensor_tensor(
                    out=qks[:], in0=tcs[:], in1=tsn[:], op=ALU.add),
                    reads=[("tcs",), ("tsn",)], writes=[("qks",)])
                au.op("sync", lambda e, s=s, fc=fc: e.dma_start(
                    out=qkr_d[fc * 128:(fc + 1) * 128,
                              s * 512:(s + 1) * 512],
                    in_=qks[:]),
                    reads=[("qks",)], dma=True)

                def vmm(e, fc=fc):
                    ins = None
                    for dc in range(NDC):
                        ins = e.matmul(
                            p_v[:],
                            wv[:, dc * 1024 + fc * 128:
                               dc * 1024 + fc * 128 + 128],
                            xsl[:, dc * 512:(dc + 1) * 512],
                            start=(dc == 0), stop=(dc == NDC - 1))
                    return ins
                au.op("tensor", vmm,
                      reads=[("xsl", dc) for dc in range(NDC)] + [("wv",)],
                      writes=[("p_v",)])
                au.op("scalar", lambda e: e.activation(
                    out=uv[:], in_=p_v[:], func=AF.Copy),
                    reads=[("p_v",)], writes=[("uv",)])
                hA, hB = 2 * fc, 2 * fc + 1
                for j in range(4):
                    au.op("tensor", lambda e, j=j: e.transpose(
                        p_tb[:], qks[:, j * 128:(j + 1) * 128], identb[:]),
                        reads=[("qks",)], writes=[("p_tb",)])
                    au.op("vector", lambda e, j=j, hA=hA: e.tensor_copy(
                        out=stg[:, j * 2048 + hA * 128:
                                j * 2048 + hA * 128 + 64],
                        in_=p_tb[:, 0:64]),
                        reads=[("p_tb",)], writes=[("stg", j, fc, 0)])
                    au.op("vector", lambda e, j=j, hB=hB: e.tensor_copy(
                        out=stg[:, j * 2048 + hB * 128:
                                j * 2048 + hB * 128 + 64],
                        in_=p_tb[:, 64:128]),
                        reads=[("p_tb",)], writes=[("stg", j, fc, 1)])
                    au.op("tensor", lambda e, j=j: e.transpose(
                        p_tb[:], uv[:, j * 128:(j + 1) * 128], identb[:]),
                        reads=[("uv",)], writes=[("p_tb",)])
                    au.op("vector", lambda e, j=j, hA=hA: e.tensor_copy(
                        out=stg[:, j * 2048 + hA * 128 + 64:
                                j * 2048 + hA * 128 + 128],
                        in_=p_tb[:, 0:64]),
                        reads=[("p_tb",)], writes=[("stg", j, fc, 2)])
                    au.op("vector", lambda e, j=j, hB=hB: e.tensor_copy(
                        out=stg[:, j * 2048 + hB * 128 + 64:
                                j * 2048 + hB * 128 + 128],
                        in_=p_tb[:, 64:128]),
                        reads=[("p_tb",)], writes=[("stg", j, fc, 3)])
            for j in range(4):
                def stgst(e, s=s, j=j):
                    return [e.dma_start(
                        out=qv_d[hh * T + s * 512 + j * 128:
                                 hh * T + s * 512 + (j + 1) * 128, :],
                        in_=stg[:, j * 2048 + hh * 128:
                                j * 2048 + (hh + 1) * 128])
                        for hh in range(16)]
                au.op("sync", stgst,
                      reads=[("stg", j, fc, q) for fc in range(NDC)
                             for q in range(4)], dma=True, ndma=16)
        au.emit()

def _l3a_head_bufs(ctx, psc, name):
    T = ctx.T
    hb = {}
    hb["qh"] = ctx.sbuf(f"{name}_qh", [64, T], BF16, psc)
    hb["rot2"] = ctx.sbuf(f"{name}_rot2", [64, 256], BF16, psc)
    hb["msk"] = ctx.sbuf(f"{name}_mskb", [128, 256], BF16, psc)
    hb["ohT"] = ctx.sbuf(f"{name}_ohT", [128, 2 * T], F32, psc)
    hb["incl"] = ctx.sbuf(f"{name}_incl", [128, T], F32, psc)
    hb["posr"] = ctx.sbuf(f"{name}_posr", [1, T], F32, psc)
    hb["posc"] = ctx.sbuf(f"{name}_posc", [128, T // 128], F32, psc)
    hb["posi"] = ctx.sbuf(f"{name}_posi", [128, T // 128], I32, psc)
    hb["scolh"] = ctx.sbuf(f"{name}_scolh", [128, 2], F32, psc)
    hb["hrow"] = ctx.sbuf(f"{name}_hrow", [1, 256], F32, psc)
    hb["hcum"] = ctx.sbuf(f"{name}_hcum", [1, 256], F32, psc)
    hb["hst"] = ctx.sbuf(f"{name}_hst", [1, 256], F32, psc)
    hb["bcl"] = ctx.sbuf(f"{name}_bcl", [128, 2], F32, psc)
    hb["sid"] = ctx.sbuf(f"{name}_sid", [128, 2 * T // 128], F32, psc)
    hb["sidi"] = ctx.sbuf(f"{name}_sidi", [128, 2 * T // 128], I32, psc)
    hb["sidh"] = ctx.sbuf(f"{name}_sidh", [128, 2 * T // 128], F32, psc)
    hb["sidt"] = ctx.sbuf(f"{name}_sidt", [128, 2 * T // 128], I32, psc)
    hb["qrow"] = ctx.sbuf(f"{name}_qrow", [1, 2 * T], F32, psc)
    hb["qvr"] = ctx.sbuf(f"{name}_qvr", [128, 8 * 128], BF16, psc)
    hb["kvf"] = ctx.sbuf(f"{name}_kvf", [64, 8 * 128], BF16, psc)
    hb["qfm"] = ctx.sbuf(f"{name}_qfm", [64, 2 * 128], BF16, psc)
    hb["kht"] = ctx.sbuf(f"{name}_kht", [128, 64], BF16, psc)
    hb["sqt"] = ctx.sbuf(f"{name}_sqt", [128, 64], BF16, psc)
    hb["rcl"] = ctx.sbuf(f"{name}_rcl", [128, 8], F32, psc)
    hb["e1b"] = ctx.sbuf(f"{name}_e1b", [128, 128], F32, psc)
    hb["egb"] = ctx.sbuf(f"{name}_egb", [128, 128], F32, psc)
    hb["t3a"] = ctx.sbuf(f"{name}_t3a", [128, 128], F32, psc)
    hb["t3b"] = ctx.sbuf(f"{name}_t3b", [128, 128], F32, psc)
    hb["Pp"] = ctx.sbuf(f"{name}_Pp", [128, 128], BF16, psc)
    hb["Pt"] = ctx.sbuf(f"{name}_Pt", [128, 128], BF16, psc)
    hb["osc"] = ctx.sbuf(f"{name}_osc", [128, 4 * 65], F32, psc)
    hb["ol"] = ctx.sbuf(f"{name}_ol", [128, 4 * 130], F32, psc)
    hb["cmb"] = ctx.sbuf(f"{name}_cmb", [128, 8], F32, psc)
    hb["otf"] = ctx.sbuf(f"{name}_otf", [128, 64], F32, psc)
    hb["otf2"] = ctx.sbuf(f"{name}_otf2", [128, 64], F32, psc)
    hb["otb"] = ctx.sbuf(f"{name}_otb", [128, 64], BF16, psc)
    hb["vcmb"] = ctx.sbuf(f"{name}_vcmb", [128, 64], BF16, psc)
    hb["af"] = ctx.sbuf(f"{name}_af", [64, 4 * 128], BF16, psc)
    return hb


def _l3a_head(ctx, sems, Wd, hb, qkr_d, qv_d, sidx_d, olin_d, afm_d, h, name):
    _l3a_sort(ctx, sems, Wd, hb, qkr_d, sidx_d, h, name)
    _l3a_att(ctx, sems, hb, qv_d, sidx_d, olin_d, afm_d, h, name)


def _l3a_sort(ctx, sems, Wd, hb, qkr_d, sidx_d, h, name):
    nc, T = ctx.nc, ctx.T
    NTT = T // 128
    identb = ctx.ident_bf()
    identf = ctx.consts["identf"]
    onec = ctx.ones_col()
    onecb = ctx.ones_col_bf()
    tidt = ctx.consts["l3a_tid"]
    dbg = os.environ.get("L3A_DBG") and h == 0
    if dbg:
        hrowd = ctx.dram_scratch(f"{name}_hrowd", [2, 256], F32)
        posrd = ctx.dram_scratch(f"{name}_posrd", [2, T], F32)
        mskd = ctx.dram_scratch(f"{name}_mskd", [128, 256], F32)
        ctx.dbg[f"{name}_hrowd"] = hrowd
        ctx.dbg[f"{name}_posrd"] = posrd
        ctx.dbg[f"{name}_mskd"] = mskd
    qh, rot2, msk = hb["qh"], hb["rot2"], hb["msk"]
    ohT, incl, posr = hb["ohT"], hb["incl"], hb["posr"]
    posc, posi, scolh = hb["posc"], hb["posi"], hb["scolh"]
    hrow, hcum, hst, bcl = hb["hrow"], hb["hcum"], hb["hst"], hb["bcl"]
    au = Auto(ctx, f"{name}_srt{h}", sems)
    with ExitStack() as psc:
        p_bk = ctx.psum(f"{name}_pbk{h}", [128, 256], F32, psc)
        p_hist = ctx.psum(f"{name}_phist{h}", [1, 256], F32, psc)
        p_oh = ctx.psum(f"{name}_poh{h}", [128, 128], BF16, psc)
        p_pos = ctx.psum(f"{name}_ppos{h}", [1, 512], F32, psc)
        p_px = ctx.psum(f"{name}_ppx{h}", [128, 1], F32, psc)

        au.op("sync", lambda e: e.dma_start(
            out=qh[:], in_=qkr_d[h * 64:(h + 1) * 64, :]),
            writes=[("qh",)], dma=True)
        for r2 in range(2):
            au.op("gpsimd", lambda e, r2=r2: e.dma_start(
                out=rot2[:, 0:128],
                in_=Wd["rotT"][(h * 2 + r2) * 64:(h * 2 + r2) * 64 + 64, :]),
                writes=[("rot2", 0)], dma=True)
            au.op("vector", lambda e: e.tensor_scalar(
                out=rot2[:, 128:256], in0=rot2[:, 0:128], scalar1=-1.0,
                scalar2=None, op0=ALU.mult),
                reads=[("rot2", 0)], writes=[("rot2", 1)])
            for tb in range(NTT):
                au.op("tensor", lambda e, tb=tb: e.matmul(
                    p_bk[:], qh[:, tb * 128:(tb + 1) * 128], rot2[:],
                    start=True, stop=True),
                    reads=[("qh",), ("rot2", 0), ("rot2", 1)],
                    writes=[("p_bk",)])
                au.op("vector", lambda e: e.tensor_reduce(
                    out=bcl[:, 0:1], in_=p_bk[:], axis=AX.X, op=ALU.max),
                    reads=[("p_bk",)], writes=[("bcl",)])
                au.op("vector", lambda e: e.tensor_scalar(
                    out=msk[:], in0=p_bk[:], scalar1=bcl[:, 0:1],
                    scalar2=None, op0=ALU.is_ge),
                    reads=[("p_bk",), ("bcl",)], writes=[("msk",)])
                if dbg and r2 == 0 and tb == 0:
                    au.op("vector", lambda e: e.tensor_copy(
                        out=hb["ohT"][:, 0:256], in_=msk[:]),
                        reads=[("msk",)], writes=[("mskf",)])
                    au.op("sync", lambda e: e.dma_start(
                        out=mskd[:], in_=hb["ohT"][:, 0:256]),
                        reads=[("mskf",), ("ohT", 0, 0), ("ohT", 0, 1),
                               ("ohT", 1, 0), ("ohT", 1, 1)], dma=True)
                au.op("tensor", lambda e, tb=tb: e.matmul(
                    p_hist[:], onecb[:], msk[:],
                    start=(tb == 0), stop=(tb == NTT - 1)),
                    reads=[("msk",)], writes=[("p_hist",)])
                for hf in range(2):
                    au.op("tensor", lambda e, hf=hf: e.transpose(
                        p_oh[:], msk[:, hf * 128:(hf + 1) * 128], identb[:]),
                        reads=[("msk",)], writes=[("p_oh",)])
                    au.op("scalar", lambda e, hf=hf, tb=tb: e.activation(
                        out=ohT[:, hf * T + tb * 128: hf * T + (tb + 1) * 128],
                        in_=p_oh[:], func=AF.Copy),
                        reads=[("p_oh",)], writes=[("ohT", hf, tb)])
            au.op("scalar", lambda e: e.activation(
                out=hrow[:], in_=p_hist[:], func=AF.Copy),
                reads=[("p_hist",)], writes=[("hrow",)])
            au.op("vector", lambda e: e.tensor_tensor_scan(
                out=hcum[:], data0=hrow[:], data1=hrow[:], initial=0.0,
                op0=ALU.add, op1=ALU.bypass),
                reads=[("hrow",)], writes=[("hcum",)])
            au.op("vector", lambda e: e.scalar_tensor_tensor(
                out=hst[:], in0=hrow[:], scalar=-1.0, in1=hcum[:],
                op0=ALU.mult, op1=ALU.add),
                reads=[("hrow",), ("hcum",)], writes=[("hst",)])
            au.op("vector", lambda e: e.tensor_scalar(
                out=hst[:], in0=hst[:], scalar1=-1.0, scalar2=None,
                op0=ALU.add),
                reads=[("hst",)], writes=[("hst",)])
            for hf in range(2):
                au.op("tensor", lambda e, hf=hf: e.transpose(
                    p_px[:], hst[0:1, hf * 128:(hf + 1) * 128],
                    identf[0:1, 0:1]),
                    reads=[("hst",)], writes=[("p_px",)])
                au.op("scalar", lambda e, hf=hf: e.activation(
                    out=scolh[:, hf:hf + 1], in_=p_px[:], func=AF.Copy),
                    reads=[("p_px",)], writes=[("scolh", hf)])
            for hf in range(2):
                au.op("vector", lambda e, hf=hf: e.tensor_tensor_scan(
                    out=incl[:], data0=ohT[:, hf * T:(hf + 1) * T],
                    data1=ohT[:, hf * T:(hf + 1) * T], initial=0.0,
                    op0=ALU.add, op1=ALU.bypass),
                    reads=[("ohT", hf, tb) for tb in range(NTT)],
                    writes=[("incl",)])
                au.op("vector", lambda e, hf=hf: e.tensor_scalar(
                    out=incl[:], in0=incl[:], scalar1=scolh[:, hf:hf + 1],
                    scalar2=None, op0=ALU.add),
                    reads=[("incl",), ("scolh", hf)], writes=[("incl",)])
                au.op("vector", lambda e, hf=hf: e.tensor_tensor(
                    out=ohT[:, hf * T:(hf + 1) * T], in0=incl[:],
                    in1=ohT[:, hf * T:(hf + 1) * T], op=ALU.mult),
                    reads=[("incl",)] + [("ohT", hf, tb) for tb in range(NTT)],
                    writes=[("t2", hf)])
            for sl in range(T // 512):
                def pmm(e, sl=sl):
                    e.matmul(p_pos[:], onec[:],
                             ohT[:, sl * 512:(sl + 1) * 512],
                             start=True, stop=False)
                    return e.matmul(p_pos[:], onec[:],
                                    ohT[:, T + sl * 512: T + (sl + 1) * 512],
                                    start=False, stop=True)
                au.op("tensor", pmm, reads=[("t2", 0), ("t2", 1)],
                      writes=[("p_pos",)])
                au.op("scalar", lambda e, sl=sl: e.activation(
                    out=posr[0:1, sl * 512:(sl + 1) * 512], in_=p_pos[:],
                    func=AF.Copy),
                    reads=[("p_pos",)], writes=[("posr", sl)])
            for tb in range(NTT):
                au.op("tensor", lambda e, tb=tb: e.transpose(
                    p_px[:], posr[0:1, tb * 128:(tb + 1) * 128],
                    identf[0:1, 0:1]),
                    reads=[("posr", tb // 4)], writes=[("p_px",)])
                au.op("scalar", lambda e, tb=tb, r2=r2: e.activation(
                    out=posc[:, tb:tb + 1], in_=p_px[:], func=AF.Copy,
                    bias=float(h * 2 * T + r2 * T)),
                    reads=[("p_px",)], writes=[("posc", tb)])
            au.op("vector", lambda e: e.tensor_copy(out=posi[:], in_=posc[:]),
                  reads=[("posc", tb) for tb in range(NTT)],
                  writes=[("posi",)])
            if dbg:
                au.op("sync", lambda e, r2=r2: e.dma_start(
                    out=hrowd[r2:r2 + 1, :], in_=hrow[:]),
                    reads=[("hrow",)], dma=True)
                au.op("sync", lambda e, r2=r2: e.dma_start(
                    out=posrd[r2:r2 + 1, :], in_=posr[:]),
                    reads=[("posr", sl) for sl in range(T // 512)], dma=True)

            def bscat(e):
                return [e.indirect_dma_start(
                    out=sidx_d[:],
                    out_offset=IndirectOffsetOnAxis(
                        ap=posi[:, tb:tb + 1], axis=0),
                    in_=tidt[:, tb:tb + 1], in_offset=None)
                    for tb in range(NTT)]
            au.op("gpsimd", bscat, reads=[("posi",)],
                  writes=[("sidxd", h, r2)], dma=True, ndma=NTT)
        au.emit()

def _l3a_att(ctx, sems, hb, qv_d, sidx_d, olin_d, afm_d, h, name):
    nc, T = ctx.nc, ctx.T
    NST = 2 * T // 128
    identb = ctx.ident_bf()
    identf = ctx.consts["identf"]
    onesr = ctx.ones_row()
    sid, sidi, qrow = hb["sid"], hb["sidi"], hb["qrow"]
    sidh, sidt = hb["sidh"], hb["sidt"]
    qvr, kvf, qfm = hb["qvr"], hb["kvf"], hb["qfm"]
    kht, sqt, rcl = hb["kht"], hb["sqt"], hb["rcl"]
    e1b, egb, t3a, t3b = hb["e1b"], hb["egb"], hb["t3a"], hb["t3b"]
    Pp, Pt, osc, ol = hb["Pp"], hb["Pt"], hb["osc"], hb["ol"]
    cmb, otf, otf2, otb, af = (hb["cmb"], hb["otf"], hb["otf2"], hb["otb"],
                               hb["af"])
    vcmb = hb["vcmb"]
    lnscale = ctx.fconst(float(np.log(1.0 / np.sqrt(64))))
    CUT = set(os.environ.get("L3A_CUT", "").split(","))
    au = Auto(ctx, f"{name}_att{h}", sems)
    with ExitStack() as psc:
        p_qr = ctx.psum(f"{name}_pqr{h}", [1, 128], F32, psc)
        p_kid = ctx.psum(f"{name}_pkid{h}", [128, 128], F32, psc)
        p_s = ctx.psum(f"{name}_ps{h}", [128, 128], F32, psc)
        p_pt = ctx.psum(f"{name}_ppt{h}", [128, 128], BF16, psc)
        p_ov = ctx.psum(f"{name}_pov{h}", [128, 64], F32, psc)
        p_kf = ctx.psum(f"{name}_pkf{h}", [64, 128], BF16, psc)
        p_qf = ctx.psum(f"{name}_pqf{h}", [64, 128], BF16, psc)

        def sidload(e):
            return [e.dma_start(
                out=sid[:, st:st + 1],
                in_=sidx_d[h * 2 * T + st * 128: h * 2 * T + (st + 1) * 128, :])
                for st in range(NST)]
        au.op("sync", sidload,
              reads=[("sidxd", h, 0), ("sidxd", h, 1)], writes=[("sid",)],
              dma=True, ndma=NST)
        au.op("vector", lambda e: e.tensor_scalar(
            out=sidh[:], in0=sid[:], scalar1=float(h * T), scalar2=None,
            op0=ALU.add),
            reads=[("sid",)], writes=[("sidh",)])
        au.op("vector", lambda e: e.tensor_copy(out=sidi[:], in_=sidh[:]),
              reads=[("sidh",)], writes=[("sidi",)])
        au.op("vector", lambda e: e.tensor_copy(out=sidt[:], in_=sid[:]),
              reads=[("sid",)], writes=[("sidt",)])
        for st in range(NST):
            au.op("tensor", lambda e, st=st: e.transpose(
                p_qr[:], sid[:, st:st + 1], identf[:]),
                reads=[("sid",)], writes=[("p_qr",)])
            au.op("scalar", lambda e, st=st: e.activation(
                out=qrow[0:1, st * 128:(st + 1) * 128], in_=p_qr[:],
                func=AF.Copy),
                reads=[("p_qr",)], writes=[("qrow", st)])

        def emit_gather(e, sts):
            return [e.indirect_dma_start(
                out=qvr[:, (st % 8) * 128:(st % 8 + 1) * 128],
                out_offset=None,
                in_=qv_d[:],
                in_offset=IndirectOffsetOnAxis(
                    ap=sidi[:, st:st + 1], axis=0))
                for st in sts]

        def emit_kv(au, st):
            slot = st % 8
            au.op("scalar", lambda e, slot=slot: e.activation(
                out=sqt[:], in_=qvr[:, slot * 128: slot * 128 + 64],
                func=AF.Square, accum_out=rcl[:, 0:1]),
                reads=[("qvr", slot)], writes=[("sqt",), ("rcl", 0)])
            au.op("scalar", lambda e: e.activation(
                out=rcl[:, 1:2], in_=rcl[:, 0:1], func=AF.Ln,
                scale=1.0 / 64.0, bias=ctx.fconst(1e-6)[:, 0:1]),
                reads=[("rcl", 0)], writes=[("rcl", 1)])
            au.op("scalar", lambda e: e.activation(
                out=rcl[:, 2:3], in_=rcl[:, 1:2], func=AF.Exp,
                scale=-0.5, bias=lnscale[:, 0:1]),
                reads=[("rcl", 1)], writes=[("rcl", 2)])
            au.op("vector", lambda e, slot=slot: e.tensor_scalar(
                out=kht[:], in0=qvr[:, slot * 128: slot * 128 + 64],
                scalar1=rcl[:, 2:3], scalar2=None, op0=ALU.mult),
                reads=[("qvr", slot), ("rcl", 2)], writes=[("kht",)])
            au.op("tensor", lambda e: e.transpose(
                p_kf[:], kht[:], identb[:]),
                reads=[("kht",)], writes=[("p_kf",)])
            au.op("scalar", lambda e, slot=slot: e.activation(
                out=kvf[:, slot * 128:(slot + 1) * 128], in_=p_kf[:],
                func=AF.Copy),
                reads=[("p_kf",)], writes=[("kvf", slot)])
            au.op("tensor", lambda e, slot=slot: e.transpose(
                p_qf[:], qvr[:, slot * 128: slot * 128 + 64], identb[:]),
                reads=[("qvr", slot)], writes=[("p_qf",)])
            au.op("scalar", lambda e, st=st: e.activation(
                out=qfm[:, (st % 2) * 128:(st % 2 + 1) * 128], in_=p_qf[:],
                func=AF.Copy),
                reads=[("p_qf",)], writes=[("qfm", st % 2)])

        au.op("gpsimd", lambda e: emit_gather(e, [NST - 1]),
              reads=[("sidi",)], writes=[("qvr", (NST - 1) % 8)], dma=True,
              ndma=1)
        emit_kv(au, NST - 1)

        for g in range(NST // 4):
            sts = list(range(g * 4, g * 4 + 4))
            au.op("gpsimd", lambda e, sts=sts: emit_gather(e, sts),
                  reads=[("sidi",)],
                  writes=[("qvr", st % 8) for st in sts], dma=True, ndma=4)
            for st in sts:
                slot = st % 8
                ps = (st + NST - 1) % NST
                pslot = ps % 8
                par = st % 2
                oslot = st % 4
                emit_kv(au, st)
                lo = (st * 128 - 64) % (2 * T)

                def kmm(e, st=st, lo=lo):
                    if "kid" in CUT:
                        return e.matmul(p_kid[:], onesr[0:1, 0:128],
                                        qrow[0:1, st * 128:(st + 1) * 128],
                                        start=True, stop=True)
                    e.matmul(p_kid[0:64, 0:64], onesr[0:1, 0:64],
                             qrow[0:1, st * 128: st * 128 + 64],
                             start=True, stop=True, tile_position=(0, 0))
                    e.matmul(p_kid[0:64, 64:128], onesr[0:1, 0:64],
                             qrow[0:1, lo: lo + 64],
                             start=True, stop=True, tile_position=(0, 0))
                    return e.matmul(p_kid[64:128, 0:128], onesr[0:1, 0:64],
                                    qrow[0:1, st * 128:(st + 1) * 128],
                                    start=True, stop=True,
                                    tile_position=(0, 64))
                au.op("tensor", kmm,
                      reads=[("qrow", st), ("qrow", ps)], writes=[("p_kid",)])

                def smm(e, slot=slot, pslot=pslot, par=par):
                    if "s" in CUT:
                        return e.matmul(p_s[:],
                                        qfm[:, par * 128:(par + 1) * 128],
                                        kvf[:, slot * 128:(slot + 1) * 128],
                                        start=True, stop=True)
                    e.matmul(p_s[0:64, 0:64],
                             qfm[:, par * 128: par * 128 + 64],
                             kvf[:, slot * 128: slot * 128 + 64],
                             start=True, stop=True, tile_position=(0, 0))
                    e.matmul(p_s[0:64, 64:128],
                             qfm[:, par * 128: par * 128 + 64],
                             kvf[:, pslot * 128 + 64: pslot * 128 + 128],
                             start=True, stop=True, tile_position=(0, 0))
                    e.matmul(p_s[64:128, 0:64],
                             qfm[:, par * 128 + 64: par * 128 + 128],
                             kvf[:, slot * 128: slot * 128 + 64],
                             start=True, stop=True, tile_position=(0, 64))
                    return e.matmul(p_s[64:128, 64:128],
                                    qfm[:, par * 128 + 64: par * 128 + 128],
                                    kvf[:, slot * 128 + 64: slot * 128 + 128],
                                    start=True, stop=True,
                                    tile_position=(0, 64))
                au.op("tensor", smm,
                      reads=[("qfm", par), ("kvf", slot), ("kvf", pslot)],
                      writes=[("p_s",)])
                au.op("vector", lambda e, st=st: e.tensor_scalar(
                    out=e1b[:], in0=p_kid[:], scalar1=sid[:, st:st + 1],
                    scalar2=None, op0=ALU.is_gt),
                    reads=[("p_kid",), ("sid",)], writes=[("e1b",)])
                au.op("vector", lambda e, st=st: e.tensor_scalar(
                    out=egb[:], in0=p_kid[:], scalar1=sid[:, st:st + 1],
                    scalar2=None, op0=ALU.is_ge),
                    reads=[("p_kid",), ("sid",)], writes=[("egb",)])
                au.op("vector", lambda e: e.scalar_tensor_tensor(
                    out=t3a[:], in0=e1b[:], scalar=-999900000.0, in1=p_s[:],
                    op0=ALU.mult, op1=ALU.add),
                    reads=[("e1b",), ("p_s",)], writes=[("t3a",)])
                au.op("vector", lambda e: e.scalar_tensor_tensor(
                    out=t3b[:], in0=egb[:], scalar=-100000.0, in1=t3a[:],
                    op0=ALU.mult, op1=ALU.add),
                    reads=[("egb",), ("t3a",)], writes=[("t3b",)])
                au.op("vector", lambda e: e.tensor_reduce(
                    out=rcl[:, 5:6], in_=t3b[:], axis=AX.X, op=ALU.max),
                    reads=[("t3b",)], writes=[("rcl", 5)])
                au.op("vector", lambda e: e.tensor_scalar(
                    out=rcl[:, 6:7], in0=rcl[:, 5:6], scalar1=-1.0,
                    scalar2=None, op0=ALU.mult),
                    reads=[("rcl", 5)], writes=[("rcl", 6)])
                au.op("scalar", lambda e: e.activation(
                    out=Pp[:], in_=t3b[:], func=AF.Exp, bias=rcl[:, 6:7],
                    accum_out=rcl[:, 3:4]),
                    reads=[("t3b",), ("rcl", 6)],
                    writes=[("Pp",), ("rcl", 3)])
                au.op("scalar", lambda e: e.activation(
                    out=rcl[:, 7:8], in_=rcl[:, 3:4], func=AF.Ln),
                    reads=[("rcl", 3)], writes=[("rcl", 7)])
                au.op("vector", lambda e, oslot=oslot: e.tensor_tensor(
                    out=osc[:, oslot * 65 + 64: oslot * 65 + 65],
                    in0=rcl[:, 7:8], in1=rcl[:, 5:6], op=ALU.add),
                    reads=[("rcl", 7), ("rcl", 5)], writes=[("osc", oslot, 1)])
                au.op("vector", lambda e: e.reciprocal(
                    out=rcl[:, 4:5], in_=rcl[:, 3:4]),
                    reads=[("rcl", 3)], writes=[("rcl", 4)])
                au.op("tensor", lambda e: e.transpose(
                    p_pt[:], Pp[:], identb[:]),
                    reads=[("Pp",)], writes=[("p_pt",)])
                au.op("vector", lambda e: e.tensor_copy(out=Pt[:], in_=p_pt[:]),
                      reads=[("p_pt",)], writes=[("Pt",)])

                au.op("vector", lambda e, slot=slot: e.tensor_copy(
                    out=vcmb[0:64, :],
                    in_=qvr[0:64, slot * 128 + 64: slot * 128 + 128]),
                    reads=[("qvr", slot)], writes=[("vcmb", 0)])
                au.op("vector", lambda e, pslot=pslot: e.tensor_copy(
                    out=vcmb[64:128, :],
                    in_=qvr[64:128, pslot * 128 + 64: pslot * 128 + 128]),
                    reads=[("qvr", pslot)], writes=[("vcmb", 1)])

                def vmm(e, slot=slot):
                    e.matmul(p_ov[0:64, :], Pt[:, 0:64], vcmb[:],
                             start=True, stop=True, tile_position=(0, 0))
                    return e.matmul(p_ov[64:128, :], Pt[:, 64:128],
                                    qvr[:, slot * 128 + 64: slot * 128 + 128],
                                    start=True, stop=True,
                                    tile_position=(0, 64))
                au.op("tensor", vmm,
                      reads=[("Pt",), ("qvr", slot), ("vcmb", 0), ("vcmb", 1)],
                      writes=[("p_ov",)])
                au.op("vector", lambda e, oslot=oslot: e.tensor_scalar(
                    out=osc[:, oslot * 65: oslot * 65 + 64], in0=p_ov[:],
                    scalar1=rcl[:, 4:5], scalar2=None, op0=ALU.mult),
                    reads=[("p_ov",), ("rcl", 4)], writes=[("osc", oslot, 0)])

            def oscat(e, sts=sts):
                return [e.indirect_dma_start(
                    out=olin_d[st // (NST // 2)][:],
                    out_offset=IndirectOffsetOnAxis(
                        ap=sidt[:, st:st + 1], axis=0),
                    in_=osc[:, (st % 4) * 65:(st % 4 + 1) * 65],
                    in_offset=None)
                    for st in sts]
            au.op("gpsimd", oscat,
                  reads=[("sidt",)] + [("osc", st % 4, q) for st in sts
                                       for q in range(2)],
                  writes=[("olind", g)], dma=True, ndma=4)

        NDT = T // 128
        for g2 in range(NDT // 4):
            tiles = list(range(g2 * 4, g2 * 4 + 4))

            def olload(e, tiles=tiles):
                out = []
                for q, tl in enumerate(tiles):
                    out.append(e.dma_start(
                        out=ol[:, q * 130: q * 130 + 65],
                        in_=olin_d[0][tl * 128:(tl + 1) * 128, :]))
                    out.append(e.dma_start(
                        out=ol[:, q * 130 + 65:(q + 1) * 130],
                        in_=olin_d[1][tl * 128:(tl + 1) * 128, :]))
                return out
            au.op("sync", olload,
                  reads=[("olind", g) for g in range(NST // 4)],
                  writes=[("ol", q) for q in range(4)], dma=True, ndma=8)
            for q, tl in enumerate(tiles):
                L0 = ol[:, q * 130 + 64: q * 130 + 65]
                L1 = ol[:, q * 130 + 129: q * 130 + 130]
                au.op("vector", lambda e, L0=L0, L1=L1: e.tensor_tensor(
                    out=cmb[:, 0:1], in0=L0, in1=L1, op=ALU.max),
                    reads=[("ol", q)], writes=[("cmb", 0)])
                au.op("vector", lambda e: e.tensor_scalar(
                    out=cmb[:, 1:2], in0=cmb[:, 0:1], scalar1=-1.0,
                    scalar2=None, op0=ALU.mult),
                    reads=[("cmb", 0)], writes=[("cmb", 1)])
                au.op("scalar", lambda e, L0=L0: e.activation(
                    out=cmb[:, 2:3], in_=L0, func=AF.Exp, bias=cmb[:, 1:2]),
                    reads=[("ol", q), ("cmb", 1)], writes=[("cmb", 2)])
                au.op("scalar", lambda e, L1=L1: e.activation(
                    out=cmb[:, 3:4], in_=L1, func=AF.Exp, bias=cmb[:, 1:2]),
                    reads=[("ol", q), ("cmb", 1)], writes=[("cmb", 3)])
                au.op("vector", lambda e: e.tensor_tensor(
                    out=cmb[:, 4:5], in0=cmb[:, 2:3], in1=cmb[:, 3:4],
                    op=ALU.add),
                    reads=[("cmb", 2), ("cmb", 3)], writes=[("cmb", 4)])
                au.op("vector", lambda e: e.reciprocal(
                    out=cmb[:, 5:6], in_=cmb[:, 4:5]),
                    reads=[("cmb", 4)], writes=[("cmb", 5)])
                au.op("vector", lambda e: e.tensor_tensor(
                    out=cmb[:, 6:7], in0=cmb[:, 2:3], in1=cmb[:, 5:6],
                    op=ALU.mult),
                    reads=[("cmb", 2), ("cmb", 5)], writes=[("cmb", 6)])
                au.op("vector", lambda e: e.tensor_tensor(
                    out=cmb[:, 7:8], in0=cmb[:, 3:4], in1=cmb[:, 5:6],
                    op=ALU.mult),
                    reads=[("cmb", 3), ("cmb", 5)], writes=[("cmb", 7)])
                au.op("vector", lambda e, q=q: e.tensor_scalar(
                    out=otf[:], in0=ol[:, q * 130: q * 130 + 64],
                    scalar1=cmb[:, 6:7], scalar2=None, op0=ALU.mult),
                    reads=[("ol", q), ("cmb", 6)], writes=[("otf",)])
                au.op("vector", lambda e, q=q: e.tensor_scalar(
                    out=otf2[:], in0=ol[:, q * 130 + 65: q * 130 + 129],
                    scalar1=cmb[:, 7:8], scalar2=None, op0=ALU.mult),
                    reads=[("ol", q), ("cmb", 7)], writes=[("otf2",)])
                au.op("vector", lambda e: e.tensor_tensor(
                    out=otb[:], in0=otf[:], in1=otf2[:], op=ALU.add),
                    reads=[("otf",), ("otf2",)], writes=[("otb",)])
                au.op("tensor", lambda e: e.transpose(
                    p_qf[:], otb[:], identb[:]),
                    reads=[("otb",)], writes=[("p_qf",)])
                au.op("scalar", lambda e, q=q: e.activation(
                    out=af[0:64, q * 128:(q + 1) * 128], in_=p_qf[:],
                    func=AF.Copy),
                    reads=[("p_qf",)], writes=[("af", q)])
            def afstore(e, tiles=tiles):
                return [e.dma_start(
                    out=afm_d[h * 64:(h + 1) * 64,
                              tl * 128:(tl + 1) * 128],
                    in_=af[0:64, q * 128:(q + 1) * 128])
                    for q, tl in enumerate(tiles)]
            au.op("sync", afstore, reads=[("af", q) for q in range(4)],
                  dma=True, ndma=4)
        au.emit()


def _l3a_final(ctx, sems, x_mid_dram, afm_d, x_out_dram, name):
    nc, T = ctx.nc, ctx.T
    NS = T // 512
    au = Auto(ctx, f"{name}_fin", sems)
    with ExitStack() as psc:
        a32 = ctx.sbuf(f"{name}_a32", [128, 2 * 512], F32, psc)
        xb = ctx.sbuf(f"{name}_xb", [128, 2 * 512], F32, psc)
        ot = ctx.sbuf(f"{name}_ot", [128, 2 * 512], F32, psc)
        for s in range(NS):
            for dc in range(NDC):
                pb = dc % 2
                au.op("gpsimd", lambda e, s=s, dc=dc, pb=pb: e.dma_start(
                    out=a32[:, pb * 512:(pb + 1) * 512],
                    in_=afm_d[dc * 128:(dc + 1) * 128,
                              s * 512:(s + 1) * 512]),
                    writes=[("a32", pb)], dma=True)
                au.op("sync", lambda e, s=s, dc=dc, pb=pb: e.dma_start(
                    out=xb[:, pb * 512:(pb + 1) * 512],
                    in_=x_mid_dram[dc * 128:(dc + 1) * 128,
                                   s * 512:(s + 1) * 512]),
                    writes=[("xb", pb)], dma=True)
                au.op("vector", lambda e, pb=pb: e.tensor_tensor(
                    out=ot[:, pb * 512:(pb + 1) * 512],
                    in0=a32[:, pb * 512:(pb + 1) * 512],
                    in1=xb[:, pb * 512:(pb + 1) * 512], op=ALU.add),
                    reads=[("a32", pb), ("xb", pb)], writes=[("ot", pb)])
                au.op("sync", lambda e, s=s, dc=dc, pb=pb: e.dma_start(
                    out=x_out_dram[dc * 128:(dc + 1) * 128,
                                   s * 512:(s + 1) * 512],
                    in_=ot[:, pb * 512:(pb + 1) * 512]),
                    reads=[("ot", pb)], dma=True)
        au.emit()


TFULL = 4096
NB_CORES = 8


def _mk_ctx():
    nc = bass.Bass("TRN2", target_bir_lowering=False, debug=False)
    ctx = Ctx(nc, TFULL)
    return nc, ctx


def _base_consts(ctx):
    ctx.ones_col(); ctx.ones_row(); ctx.fconst(1e-5)


def _build_l0(w):
    nc, ctx = _mk_ctx()
    T = TFULL
    x_d = ctx.dram_in("x", np.zeros((D, T), np.float32))
    W = {}
    W["WqT"] = ctx.dram_in("wqt", f32c(w["L0_attn_Wq"].T))
    W["WkT"] = ctx.dram_in("wkt", f32c(w["L0_attn_Wk"].T))
    W["WvT"] = ctx.dram_in("wvt", f32c(w["L0_attn_Wv"].T))
    for nmc, key in (("wqc", "L0_attn_wq_conv"), ("wkc", "L0_attn_wk_conv"),
                     ("wvc", "L0_attn_wv_conv")):
        arr = w[key]
        a = np.zeros((128, 32), np.float32)
        for h in range(8):
            for i in range(4):
                a[:, h * 4 + i] = arr[h * 128:(h + 1) * 128, 0, i]
        W[nmc + "_sb"] = ctx.const_sb(nmc, f32c(a))
    W["WfaT"] = ctx.dram_in("wfat", f32c(w["L0_attn_Wfa"].T))
    W["WfbT"] = ctx.dram_in("wfbt", f32c(w["L0_attn_Wfb"].T))
    W["WbT"] = ctx.dram_in("wbt", f32c(w["L0_attn_Wb"].T))
    W["WgaT"] = ctx.dram_in("wgat", f32c(w["L0_attn_Wga"].T))
    W["WgbT"] = ctx.dram_in("wgbt", f32c(w["L0_attn_Wgb"].T))
    W["A_log"] = ctx.dram_in("alog", f32c(w["L0_attn_A_log"][None, :]))
    W["dtb"] = ctx.dram_in("dtb", f32c(w["L0_attn_dt_bias"].T))
    W["onwc"] = ctx.dram_in("onwc", f32c(w["L0_attn_o_norm_w"][:, None]))
    W["WoT"] = ctx.dram_in("wot", f32c(w["L0_attn_Wo"].T))
    W["zb8"] = ctx.dram_in("zb8", f32c(np.zeros((128, 8))))
    W1T_d = ctx.dram_in("w1t", f32c(w["L0_ffn_W1"].T))
    W2T_d = ctx.dram_in("w2t", f32c(w["L0_ffn_W2"].T))
    b1c_d = ctx.dram_in("b1c", f32c(w["L0_ffn_b1"].reshape(32, 128).T))
    b2c_d = ctx.dram_in("b2c", f32c(w["L0_ffn_b2"].reshape(8, 128).T))
    xo_d = ctx.dram_out("xo", [D, T], F32)
    xn_d = ctx.dram_scratch("xn", [D, T], BF16)
    xm_d = ctx.dram_scratch("xm", [D, T], F32)
    _base_consts(ctx)
    ctx.emit_const_preload()
    emit_layernorm(ctx, x_d, xn_d, "ln0")
    emit_l0attn(ctx, xn_d, x_d, xm_d, W)
    emit_l0ffn(ctx, xn_d, xm_d, xo_d, W1T_d, W2T_d, b1c_d, b2c_d)
    ctx.es.close()
    return nc, ctx


def _build_l1a(w):
    nc, ctx = _mk_ctx()
    T = TFULL
    x_d = ctx.dram_in("x", np.zeros((D, T), np.float32))
    Wd = {}
    Wd["w_qkvT"] = ctx.dram_in("wqkvt", f32c(w["L1_attn_w_qkv"].T))
    Wd["lnpar"] = ctx.dram_in("lnpar", f32c(np.stack(
        [np.tile(w["L1_attn_q_gamma"], 2), np.tile(w["L1_attn_q_beta"], 2),
         np.tile(w["L1_attn_k_gamma"], 2), np.tile(w["L1_attn_k_beta"], 2)],
        axis=1)))
    Wd["w_projT"] = ctx.dram_in("wprojt", f32c(w["L1_attn_w_proj"].T))
    Wd["bpc"] = ctx.dram_in("bpc", f32c(w["L1_attn_b_proj"].reshape(8, 128).T))
    half = 32
    inv = 1.0 / 10000.0 ** (np.arange(half) / half)
    fr = np.outer(inv, np.arange(T))
    emb = np.concatenate([fr, fr], 0)
    Wd["cos_sb"] = ctx.const_sb("ropecos", f32c(np.tile(np.cos(emb), (2, 1))))
    Wd["sin_sb"] = ctx.const_sb("ropesin", f32c(np.tile(np.sin(emb), (2, 1))))
    R = np.zeros((128, 128), np.float32)
    for b0 in (0, 64):
        for f in range(32):
            R[b0 + f + 32, b0 + f] = -1.0
        for f in range(32, 64):
            R[b0 + f - 32, b0 + f] = 1.0
    Wd["rot_sb"] = ctx.const_sb("rotmat", bfc(R))
    xo_d = ctx.dram_out("xo", [D, T], F32)
    xn_d = ctx.dram_out("xnout", [D, T], BF16)
    _base_consts(ctx)
    ctx.fconst(1e-6)
    ctx.emit_const_preload()
    emit_layernorm(ctx, x_d, xn_d, "ln1")
    emit_l1attn(ctx, xn_d, x_d, xo_d, Wd)
    ctx.es.close()
    return nc, ctx


def _build_l1f(w):
    nc, ctx = _mk_ctx()
    T = TFULL
    x_d = ctx.dram_in("x", np.zeros((D, T), np.float32))
    xn_d = ctx.dram_in("xnin", np.zeros((D, T), ml_dtypes.bfloat16))
    W2 = {
        "gwT": ctx.dram_in("gwT", f32c(w["L1_ffn_gate_w"])),
        "sgwc": ctx.dram_in("sgwc",
                            f32c(w["L1_ffn_shared_gate_w"].reshape(8, 128).T)),
        "sguT": ctx.dram_in("sguT", f32c(w["L1_ffn_shared_gate_up_w"])),
        "sdwT": ctx.dram_in("sdwT", f32c(w["L1_ffn_shared_down_w"])),
        "eguT": ctx.dram_in("eguT",
                            f32c(w["L1_ffn_expert_gate_up_w"].reshape(
                                NEXP * 1024, 1024))),
        "edwT": ctx.dram_in("edwT",
                            f32c(w["L1_ffn_expert_down_w"].reshape(
                                NEXP * 512, 1024))),
        "pfm": ctx.dram_in("pfm", np.zeros((8, 1), np.float32)),
    }
    xo_d = ctx.dram_out("xo", [D, T], F32)
    _base_consts(ctx)
    ctx.emit_const_preload()
    emit_l1ffn(ctx, xn_d, x_d, xo_d, W2, multicore=True)
    ctx.es.close()
    return nc, ctx


def _build_l2(w):
    nc, ctx = _mk_ctx()
    T = TFULL
    x_d = ctx.dram_in("x", np.zeros((D, T), np.float32))
    w_inT_d = ctx.dram_in("wint", f32c(w["L2_attn_w_in"].T))
    binc = np.zeros((128, 24), np.float32)
    b_in = w["L2_attn_b_in"]
    for P in range(3):
        for hp in range(8):
            binc[:, P * 8 + hp] = b_in[P * 1024 + hp * 128:
                                       P * 1024 + (hp + 1) * 128]
    binc_d = ctx.dram_in("binc", f32c(binc))
    w_outT_d = ctx.dram_in("woutt", f32c(w["L2_attn_w_out"].T))
    boutc_d = ctx.dram_in("boutc", f32c(w["L2_attn_b_out"].reshape(8, 128).T))
    W1_d = ctx.dram_in("w1", f32c(w["L2_ffn_W1"]))
    W2_d = ctx.dram_in("w2", f32c(w["L2_ffn_W2"]))
    cc = lambda n, v: ctx.dram_in(n, f32c(v.reshape(32, 128).T))
    b1c = cc("b1c", w["L2_ffn_b1"])
    thc = cc("thc", w["L2_ffn_quantum_phases"])
    wrc = cc("wrc", w["L2_ffn_quantum_weights_real"])
    wic = cc("wic", w["L2_ffn_quantum_weights_imag"])
    b2c_d = ctx.dram_in("b2c", f32c(w["L2_ffn_b2"].reshape(8, 128).T))
    xo_d = ctx.dram_out("xo", [D, T], F32)
    xn_d = ctx.dram_scratch("xn", [D, T], BF16)
    xm_d = ctx.dram_scratch("xm", [D, T], F32)
    _base_consts(ctx)
    ctx.ident_bf()
    ctx.const_sb("trid", bfc(np.tril(np.ones((128, 128))).T))
    ctx.const_sb("tris", bfc(np.tril(np.ones((128, 128)))))
    ctx.emit_const_preload()
    emit_layernorm(ctx, x_d, xn_d, "ln2")
    emit_l2attn(ctx, xn_d, x_d, xm_d, w_inT_d, binc_d, w_outT_d, boutc_d)
    emit_l2ffn(ctx, xn_d, xm_d, xo_d, W1_d, W2_d, b1c, thc, wrc, wic, b2c_d)
    ctx.es.close()
    return nc, ctx


def _build_l3(w):
    nc, ctx = _mk_ctx()
    T = TFULL
    x_d = ctx.dram_in("x", np.zeros((D, T), np.float32))
    rw = w["L3_ffn_router_w"]
    rwc = []
    for l in range(2):
        a = np.zeros((128, 8 * 8), np.float32)
        for dc in range(8):
            a[:, dc * 8:(dc + 1) * 8] = rw[l, :, dc * 128:(dc + 1) * 128].T
        rwc.append(ctx.dram_in(f"rwc{l}", f32c(a)))
    rb_d = ctx.dram_in("rb", f32c(w["L3_ffn_router_b"]))
    ewT = np.ascontiguousarray(np.transpose(
        w["L3_ffn_expert_w"], (0, 1, 3, 2))).reshape(2 * 8 * 1024, 1024)
    ewTc_d = ctx.dram_in("ewt", f32c(ewT))
    ebc_d = ctx.dram_in("ebc", f32c(w["L3_ffn_expert_b"].reshape(16, 1024)))
    Wd3 = {
        "wqkT": ctx.dram_in("wqkT", f32c(w["L3_attn_w_qk"])),
        "wvT": ctx.dram_in("wvT", f32c(w["L3_attn_w_v"])),
        "rotT": ctx.dram_in("rotT", f32c(
            w["L3_attn_rotations"].transpose(0, 2, 1, 3).reshape(2048, 128))),
    }
    xo_d = ctx.dram_out("xo", [D, T], F32)
    xn_d = ctx.dram_scratch("xn", [D, T], BF16)
    xm_d = ctx.dram_scratch("xm", [D, T], F32)
    _base_consts(ctx)
    ctx.ident_bf()
    ctx.const_sb("negbig8", f32c(np.full((128, 8), -3e38)))
    ctx.emit_const_preload()
    emit_layernorm(ctx, x_d, xn_d, "ln3")
    emit_l3ffn(ctx, xn_d, x_d, xm_d, rwc, rb_d, ewTc_d, ebc_d)
    emit_l3attn(ctx, xn_d, xm_d, xo_d, Wd3)
    ctx.es.close()
    return nc, ctx


_PROG_CACHE = {}


def kernel_multi(**inputs):
    from concourse.bass_utils import run_bass_kernel_spmd
    w = inputs
    if "progs" not in _PROG_CACHE:
        _PROG_CACHE["progs"] = [_build_l0(w), _build_l1a(w), _build_l1f(w),
                                _build_l2(w), _build_l3(w)]
    progs = _PROG_CACHE["progs"]

    x = inputs["L0_attn_x"]
    xs = [np.ascontiguousarray(x[c].T).astype(np.float32)
          for c in range(NB_CORES)]
    xns = None
    for li, (nc, ctx) in enumerate(progs):
        dicts = []
        for c in range(NB_CORES):
            d = dict(ctx.inputs)
            d["x"] = xs[c]
            if li == 2:
                pfm = np.zeros((8, 1), np.float32)
                pfm[:c] = 1.0
                d["pfm"] = pfm
                d["xnin"] = xns[c]
            dicts.append(d)
        res = run_bass_kernel_spmd(nc, dicts, core_ids=list(range(NB_CORES)))
        xs = [np.asarray(res.results[c]["xo"], np.float32)
              for c in range(NB_CORES)]
        if li == 1:
            xns = [np.asarray(res.results[c]["xnout"]) for c in range(NB_CORES)]
    out = np.stack([xs[c].T for c in range(NB_CORES)], axis=0)
    return np.ascontiguousarray(out)


def _build_fused(w):
    nc, ctx = _mk_ctx()
    T = TFULL
    x_d = ctx.dram_in("x", np.zeros((D, T), np.float32))
    xo_d = ctx.dram_out("xo", [D, T], F32)
    x1 = ctx.dram_scratch("xl1", [D, T], F32)
    x2 = ctx.dram_scratch("xl2", [D, T], F32)
    x3 = ctx.dram_scratch("xl3", [D, T], F32)
    _base_consts(ctx)
    ctx.fconst(1e-6)
    ctx.ident_bf()
    sems = ctx.auto_sems()

    W = {}
    W["WqT"] = ctx.dram_in("wqt", f32c(w["L0_attn_Wq"].T))
    W["WkT"] = ctx.dram_in("wkt", f32c(w["L0_attn_Wk"].T))
    W["WvT"] = ctx.dram_in("wvt", f32c(w["L0_attn_Wv"].T))
    for nmc, key in (("wqc", "L0_attn_wq_conv"), ("wkc", "L0_attn_wk_conv"),
                     ("wvc", "L0_attn_wv_conv")):
        arr = w[key]
        a = np.zeros((128, 32), np.float32)
        for h in range(8):
            for i in range(4):
                a[:, h * 4 + i] = arr[h * 128:(h + 1) * 128, 0, i]
        W[nmc + "_sb"] = ctx.const_sb(nmc, f32c(a))
    W["WfaT"] = ctx.dram_in("wfat", f32c(w["L0_attn_Wfa"].T))
    W["WfbT"] = ctx.dram_in("wfbt", f32c(w["L0_attn_Wfb"].T))
    W["WbT"] = ctx.dram_in("wbt", f32c(w["L0_attn_Wb"].T))
    W["WgaT"] = ctx.dram_in("wgat", f32c(w["L0_attn_Wga"].T))
    W["WgbT"] = ctx.dram_in("wgbt", f32c(w["L0_attn_Wgb"].T))
    W["A_log"] = ctx.dram_in("alog", f32c(w["L0_attn_A_log"][None, :]))
    W["dtb"] = ctx.dram_in("dtb", f32c(w["L0_attn_dt_bias"].T))
    W["onwc"] = ctx.dram_in("onwc", f32c(w["L0_attn_o_norm_w"][:, None]))
    W["WoT"] = ctx.dram_in("wot", f32c(w["L0_attn_Wo"].T))
    W["zb8"] = ctx.dram_in("zb8", f32c(np.zeros((128, 8))))
    W1T_d = ctx.dram_in("w1t", f32c(w["L0_ffn_W1"].T))
    W2T_d = ctx.dram_in("w2t", f32c(w["L0_ffn_W2"].T))
    b1c_d = ctx.dram_in("b1c", f32c(w["L0_ffn_b1"].reshape(32, 128).T))
    b2c_d = ctx.dram_in("b2c", f32c(w["L0_ffn_b2"].reshape(8, 128).T))
    xn_d = ctx.dram_scratch("xn0d", [D, T], BF16)
    xm_d = ctx.dram_scratch("xm0d", [D, T], F32)
    ctx.emit_const_preload()
    emit_layernorm(ctx, x_d, xn_d, "ln0")
    emit_l0attn(ctx, xn_d, x_d, xm_d, W)
    emit_l0ffn(ctx, xn_d, xm_d, x1, W1T_d, W2T_d, b1c_d, b2c_d)

    Wd = {}
    Wd["w_qkvT"] = ctx.dram_in("wqkvt", f32c(w["L1_attn_w_qkv"].T))
    Wd["lnpar"] = ctx.dram_in("lnpar", f32c(np.stack(
        [np.tile(w["L1_attn_q_gamma"], 2), np.tile(w["L1_attn_q_beta"], 2),
         np.tile(w["L1_attn_k_gamma"], 2), np.tile(w["L1_attn_k_beta"], 2)],
        axis=1)))
    Wd["w_projT"] = ctx.dram_in("wprojt", f32c(w["L1_attn_w_proj"].T))
    Wd["bpc"] = ctx.dram_in("bpc", f32c(w["L1_attn_b_proj"].reshape(8, 128).T))
    half = 32
    inv = 1.0 / 10000.0 ** (np.arange(half) / half)
    fr = np.outer(inv, np.arange(T))
    emb = np.concatenate([fr, fr], 0)
    cos_dram = ctx.dram_in("ropecosd", f32c(np.tile(np.cos(emb), (2, 1))))
    sin_dram = ctx.dram_in("ropesind", f32c(np.tile(np.sin(emb), (2, 1))))
    R = np.zeros((128, 128), np.float32)
    for b0 in (0, 64):
        for f in range(32):
            R[b0 + f + 32, b0 + f] = -1.0
        for f in range(32, 64):
            R[b0 + f - 32, b0 + f] = 1.0
    Wd["rot_sb"] = ctx.const_sb("rotmat", bfc(R))
    W2 = {
        "gwT": ctx.dram_in("gwT", f32c(w["L1_ffn_gate_w"])),
        "sgwc": ctx.dram_in("sgwc",
                            f32c(w["L1_ffn_shared_gate_w"].reshape(8, 128).T)),
        "sguT": ctx.dram_in("sguT", f32c(w["L1_ffn_shared_gate_up_w"])),
        "sdwT": ctx.dram_in("sdwT", f32c(w["L1_ffn_shared_down_w"])),
        "eguT": ctx.dram_in("eguT",
                            f32c(w["L1_ffn_expert_gate_up_w"].reshape(
                                NEXP * 1024, 1024))),
        "edwT": ctx.dram_in("edwT",
                            f32c(w["L1_ffn_expert_down_w"].reshape(
                                NEXP * 512, 1024))),
        "pfm": ctx.dram_in("pfm", np.zeros((8, 1), np.float32)),
    }
    xn1_d = ctx.dram_scratch("xn1d", [D, T], BF16)
    xm1_d = ctx.dram_scratch("xm1d", [D, T], F32)
    emit_layernorm(ctx, x1, xn1_d, "ln1")
    with ExitStack() as lsc:
        cos_t = ctx.sbuf("ropecos_t", [128, T], F32, lsc)
        sin_t = ctx.sbuf("ropesin_t", [128, T], F32, lsc)
        Wd["cos_sb"] = cos_t
        Wd["sin_sb"] = sin_t
        au = Auto(ctx, "l1_ropeld", sems)
        au.op("sync", lambda e: e.dma_start(out=cos_t[:], in_=cos_dram),
              writes=[("c",)], dma=True)
        au.op("sync", lambda e: e.dma_start(out=sin_t[:], in_=sin_dram),
              writes=[("s",)], dma=True)
        au.emit()
        emit_l1attn(ctx, xn1_d, x1, xm1_d, Wd)
    emit_l1ffn(ctx, xn1_d, xm1_d, x2, W2, multicore=True)

    w_inT_d = ctx.dram_in("wint", f32c(w["L2_attn_w_in"].T))
    binc = np.zeros((128, 24), np.float32)
    b_in = w["L2_attn_b_in"]
    for P in range(3):
        for hp in range(8):
            binc[:, P * 8 + hp] = b_in[P * 1024 + hp * 128:
                                       P * 1024 + (hp + 1) * 128]
    binc_d = ctx.dram_in("binc", f32c(binc))
    w_outT_d = ctx.dram_in("woutt", f32c(w["L2_attn_w_out"].T))
    boutc_d = ctx.dram_in("boutc", f32c(w["L2_attn_b_out"].reshape(8, 128).T))
    W1_2d = ctx.dram_in("w1", f32c(w["L2_ffn_W1"]))
    W2_2d = ctx.dram_in("w2", f32c(w["L2_ffn_W2"]))
    cc = lambda n, v: ctx.dram_in(n, f32c(v.reshape(32, 128).T))
    b1c2 = cc("b1c2", w["L2_ffn_b1"])
    thc = cc("thc", w["L2_ffn_quantum_phases"])
    wrc = cc("wrc", w["L2_ffn_quantum_weights_real"])
    wic = cc("wic", w["L2_ffn_quantum_weights_imag"])
    b2c2 = ctx.dram_in("b2c2", f32c(w["L2_ffn_b2"].reshape(8, 128).T))
    ctx.const_sb("trid", bfc(np.tril(np.ones((128, 128))).T))
    ctx.const_sb("tris", bfc(np.tril(np.ones((128, 128)))))
    xn2_d = ctx.dram_scratch("xn2d", [D, T], BF16)
    xm2_d = ctx.dram_scratch("xm2d", [D, T], F32)
    ctx.emit_const_preload()
    emit_layernorm(ctx, x2, xn2_d, "ln2")
    emit_l2attn(ctx, xn2_d, x2, xm2_d, w_inT_d, binc_d, w_outT_d, boutc_d)
    emit_l2ffn(ctx, xn2_d, xm2_d, x3, W1_2d, W2_2d, b1c2, thc, wrc, wic, b2c2)

    rw = w["L3_ffn_router_w"]
    rwc = []
    for l in range(2):
        a = np.zeros((128, 8 * 8), np.float32)
        for dc in range(8):
            a[:, dc * 8:(dc + 1) * 8] = rw[l, :, dc * 128:(dc + 1) * 128].T
        rwc.append(ctx.dram_in(f"rwc{l}", f32c(a)))
    rb_d = ctx.dram_in("rb", f32c(w["L3_ffn_router_b"]))
    ewT = np.ascontiguousarray(np.transpose(
        w["L3_ffn_expert_w"], (0, 1, 3, 2))).reshape(2 * 8 * 1024, 1024)
    ewTc_d = ctx.dram_in("ewt", f32c(ewT))
    ebc_d = ctx.dram_in("ebc", f32c(w["L3_ffn_expert_b"].reshape(16, 1024)))
    Wd3 = {
        "wqkT": ctx.dram_in("wqkT", f32c(w["L3_attn_w_qk"])),
        "wvT": ctx.dram_in("wvT", f32c(w["L3_attn_w_v"])),
        "rotT": ctx.dram_in("rotT", f32c(
            w["L3_attn_rotations"].transpose(0, 2, 1, 3).reshape(2048, 128))),
    }
    ctx.const_sb("negbig8", f32c(np.full((128, 8), -3e38)))
    xn3_d = ctx.dram_scratch("xn3d", [D, T], BF16)
    xm3_d = ctx.dram_scratch("xm3d", [D, T], F32)
    ctx.emit_const_preload()
    emit_layernorm(ctx, x3, xn3_d, "ln3")
    emit_l3ffn(ctx, xn3_d, x3, xm3_d, rwc, rb_d, ewTc_d, ebc_d)
    emit_l3attn(ctx, xn3_d, xm3_d, xo_d, Wd3)
    ctx.es.close()
    return nc, ctx


def kernel(**inputs):
    from concourse.bass_utils import run_bass_kernel_spmd
    if "fused" not in _PROG_CACHE:
        _PROG_CACHE["fused"] = _build_fused(inputs)
    nc, ctx = _PROG_CACHE["fused"]
    x = inputs["L0_attn_x"]
    dicts = []
    for c in range(NB_CORES):
        d = dict(ctx.inputs)
        d["x"] = np.ascontiguousarray(x[c].T).astype(np.float32)
        pfm = np.zeros((8, 1), np.float32)
        pfm[:c] = 1.0
        d["pfm"] = pfm
        dicts.append(d)
    res = run_bass_kernel_spmd(nc, dicts, core_ids=list(range(NB_CORES)))
    out = np.stack([np.asarray(res.results[c]["xo"], np.float32).T
                    for c in range(NB_CORES)], axis=0)
    return np.ascontiguousarray(out)
